# version 01
# baseline (speedup 1.0000x reference)
import numpy as np
import ml_dtypes
import concourse.bass as bass
import concourse.mybir as mybir
from concourse.bass_utils import run_bass_kernel_spmd

F32 = mybir.dt.float32
BF16 = mybir.dt.bfloat16
AF = mybir.ActivationFunctionType
ALU = mybir.AluOpType
AX = mybir.AxisListType

D = 4096
HC = 32
TOWN = 1024
THALO = 512
TTOT = 1536
NT = 12
NTO = 8
NQH = 32
NKV = 8
HD = 128
DVA = NKV * HD
QSCALE = float(128.0 ** -0.25)
KB = 1024


class _StopBuild(Exception):
    pass


class Ctr:
    def __init__(self, sem):
        self.sem = sem
        self.v = 0

    def inc(self, ins, n=1):
        ins.then_inc(self.sem, n)
        self.v += n
        return self.v


def build(stop=None, dbg=False):
    nc = bass.Bass("TRN2", target_bir_lowering=False)

    x_in = nc.dram_tensor("x_in", [TTOT, D], F32, kind="ExternalInput").ap()
    wqT = nc.dram_tensor("wqT", [D, D], F32, kind="ExternalInput").ap()
    wkT = nc.dram_tensor("wkT", [D, NKV * HD], F32, kind="ExternalInput").ap()
    wvT = nc.dram_tensor("wvT", [D, NKV * HD], F32, kind="ExternalInput").ap()
    woT = nc.dram_tensor("woT", [D, D], F32, kind="ExternalInput").ap()
    wupT = nc.dram_tensor("wupT", [D, D], F32, kind="ExternalInput").ap()
    swr = nc.dram_tensor("swr", [128, HC * 36], F32, kind="ExternalInput").ap()
    bflat = nc.dram_tensor("bflat", [32, D], F32, kind="ExternalInput").ap()
    ebias = nc.dram_tensor("ebias", [128, 48], F32, kind="ExternalInput").ap()
    m0rep = nc.dram_tensor("m0rep", [128, 512], BF16, kind="ExternalInput").ap()
    m4rep = nc.dram_tensor("m4rep", [128, 512], BF16, kind="ExternalInput").ap()
    emrep = nc.dram_tensor("emrep", [128, 10 * 512], BF16, kind="ExternalInput").ap()
    identin = nc.dram_tensor("identin", [128, 128], BF16, kind="ExternalInput").ap()

    xkind = dict(kind="ExternalOutput") if dbg else {}
    x1 = nc.dram_tensor("x1", [TTOT, D], F32, **xkind).ap()
    x2 = nc.dram_tensor("x2", [TOWN, D], F32, **xkind).ap()
    qscr = nc.dram_tensor("qscr", [D, TOWN], BF16).ap()
    qmd = nc.dram_tensor("qmd", [1, NQH * TOWN], BF16).ap()
    out_t = nc.dram_tensor("out_t", [TOWN, D], F32, kind="ExternalOutput").ap()

    arena = nc.alloc_sbuf_tensor("arena", [128, 206 * KB + 512], mybir.dt.uint8)
    B0 = nc.lookup_mloc(arena).addr

    def sbt(name, shape, dtype, off):
        return nc.alloc_sbuf_tensor_at(name, shape, dtype, offset=B0 + off)

    ident = sbt("ident", [128, 128], BF16, 0)
    ones_col = sbt("ones_col", [128, 1], BF16, 256)
    halfneg1 = sbt("halfneg1", [1, 128], BF16, 288)
    onesf = sbt("onesf", [1, 128], F32, 544)
    sw_sb = sbt("sw_sb", [128, HC * 36], BF16, 1056)
    ebias_sb = sbt("ebias_sb", [128, 48], F32, 3392)
    m0_sb = sbt("m0_sb", [128, 512], BF16, 3584)
    m4_sb = sbt("m4_sb", [128, 512], BF16, 4608)
    lg_sb = sbt("lg_sb", [128, 48], F32, 5632)
    cmp_sb = sbt("cmp_sb", [128, 144], F32, 5824)
    oh_sb = sbt("oh_sb", [128, 48], F32, 6400)
    kmax2_sb = sbt("kmax2_sb", [1, 16], F32, 6592)
    tmax_sb = sbt("tmax_sb", [1, 16], F32, 6656)
    rs_sb = sbt("rs_sb", [1, 512], F32, 6720)
    st_s = [sbt(f"st_s{i}", [128, 1], F32, 8768 + i * 32) for i in range(2)]
    st_ss = [sbt(f"st_ss{i}", [128, 1], F32, 8832 + i * 32) for i in range(2)]
    st_mean = [sbt(f"st_mean{i}", [128, 1], F32, 8896 + i * 32) for i in range(2)]
    st_v2 = [sbt(f"st_v2{i}", [128, 1], F32, 8960 + i * 32) for i in range(2)]
    st_sd = [sbt(f"st_sd{i}", [128, 1], F32, 9024 + i * 32) for i in range(2)]
    st_r = [sbt(f"st_r{i}", [128, 1], F32, 9088 + i * 32) for i in range(2)]
    st_mr = [sbt(f"st_mr{i}", [128, 1], F32, 9152 + i * 32) for i in range(2)]
    xold = [sbt(f"xold{i}", [128, 512], F32, 9216 + i * 2048) for i in range(2)]
    xnew = [sbt(f"xnew{i}", [128, 512], F32, 13312 + i * 2048) for i in range(2)]
    GOFF = 18 * KB
    bflat_sb = sbt("bflat_sb", [32, D], BF16, GOFF)
    zT_sb = sbt("zT_sb", [32, NT * 128], BF16, GOFF + 8 * KB)
    zm_tm = sbt("zm_tm", [128, NT * 32], BF16, GOFF + 11 * KB)
    zmm_tm = sbt("zmm_tm", [128, NT * 32], BF16, GOFF + 12 * KB)
    zmT_sb = sbt("zmT_sb", [32, NT * 128], BF16, GOFF + 13 * KB)
    gfm_sb = sbt("gfm_sb", [4, TTOT], BF16, GOFF + 16 * KB)
    WOFF = 40 * KB
    wbuf = [sbt(f"wbuf{i}", [128, HC * 512], BF16, WOFF + i * 32 * KB) for i in range(2)]
    xt = [sbt(f"xt{i}", [128, D], F32, WOFF + i * 16 * KB) for i in range(2)]
    At = [sbt(f"At{i}", [128, D], BF16, WOFF + 32 * KB + i * 8 * KB) for i in range(2)]
    kwbuf = [sbt(f"kwbuf{i}", [128, HC * 128], BF16, WOFF + i * 8 * KB) for i in range(2)]
    vwbuf = [sbt(f"vwbuf{i}", [128, HC * 128], BF16, WOFF + i * 8 * KB) for i in range(2)]
    qwbuf = [sbt(f"qwbuf{i}", [128, HC * 128], BF16, WOFF + i * 8 * KB) for i in range(2)]
    K_sb = sbt("K_sb", [128, NKV * TTOT], BF16, WOFF + 16 * KB)
    V_sb = sbt("V_sb", [128, NT * DVA], BF16, WOFF + 40 * KB)
    E_sb = sbt("E_sb", [128, 5 * 512], BF16, WOFF)
    em_sb = sbt("em_sb", [128, 10 * 512], BF16, WOFF + 5 * KB)
    R1 = 104 * KB
    A_own = sbt("A_own", [128, HC * TOWN], BF16, R1)
    OT_sb = sbt("OT_sb", [128, NQH * TOWN], BF16, R1)
    R2 = 168 * KB
    A_halo = sbt("A_halo", [128, HC * THALO], BF16, R2)
    qtile = [sbt(f"qtile{i}", [128, NQH * 128], BF16, R2 + i * 8 * KB) for i in range(2)]
    qmrow2 = [sbt(f"qmrow2_{i}", [1, NQH * 128], BF16, R2 + 16 * KB + i * 8 * KB) for i in range(2)]
    FOFF = 200 * KB
    qstage = [sbt(f"qstage{i}", [128, 1024], BF16, FOFF + i * 2 * KB) for i in range(2)]
    qsq_sb = sbt("qsq_sb", [128, 512], BF16, FOFF + 4 * KB)
    ksq_sb = [sbt(f"ksq{i}", [128, 512], BF16, FOFF + 4 * KB + i * KB) for i in range(2)]
    qmrow = [sbt(f"qmrow{i}", [1, 1024], BF16, R2 + i * 2 * KB) for i in range(2)]
    rsb_sb = sbt("rsb_sb", [128, 512], F32, FOFF + 4 * KB)

    P0 = nc.alloc_psum_tensor("P0", [128, 512], F32)
    P1 = nc.alloc_psum_tensor("P1", [128, 512], F32)
    PQ = nc.alloc_psum_tensor("PQ", [128, 1024], F32)
    PSs = [nc.alloc_psum_tensor(f"PSs{i}", [1, 512], F32) for i in range(2)]
    PR = nc.alloc_psum_tensor("PR", [128, 512], F32)
    PT = nc.alloc_psum_tensor("PT", [128, 1024], BF16)
    PF = [P0, P1]
    PRb = PR.ap().bitcast(BF16)
    POT = [PQ.ap()[:, 0:512], PQ.ap()[:, 512:1024]]

    from contextlib import ExitStack
    ctx = ExitStack()
    sLD = Ctr(ctx.enter_context(nc.semaphore("sLD")))
    sX = Ctr(ctx.enter_context(nc.semaphore("sX")))
    sMM = Ctr(ctx.enter_context(nc.semaphore("sMM")))
    sDV = Ctr(ctx.enter_context(nc.semaphore("sDV")))
    sSE = Ctr(ctx.enter_context(nc.semaphore("sSE")))
    sST = Ctr(ctx.enter_context(nc.semaphore("sST")))

    M = {}
    futures = []

    def fut(key, val):
        futures.append((key, val))
        return val

    def A_block(hc, tile):
        if tile < 4:
            return A_halo[:, hc * THALO + tile * 128: hc * THALO + (tile + 1) * 128]
        return A_own[:, hc * TOWN + (tile - 4) * 128: hc * TOWN + (tile - 3) * 128]

    def A_span(hc, tg):
        if tg == 0:
            return A_halo[:, hc * THALO: (hc + 1) * THALO]
        return A_own[:, hc * TOWN + (tg - 1) * 512: hc * TOWN + tg * 512]

    def A_evac_dst(sub, gidx, tile):
        if sub != 3 and tile < 4:
            return A_halo[:].rearrange("p (hc t) -> p hc t", hc=HC)[:, gidx * 4:gidx * 4 + 4,
                                                                   tile * 128:(tile + 1) * 128]
        t2 = tile - 4 if sub != 3 else tile
        return A_own[:].rearrange("p (hc t) -> p hc t", hc=HC)[:, gidx * 4:gidx * 4 + 4,
                                                               t2 * 128:(t2 + 1) * 128]

    def A3_block(hc, tile):
        return A_own[:, hc * TOWN + tile * 128: hc * TOWN + (tile + 1) * 128]

    def A3_span(hc, tg):
        return A_own[:, hc * TOWN + tg * 512: hc * TOWN + (tg + 1) * 512]

    def K_slice(kh, tg):
        return K_sb[:, kh * TTOT + tg * 512: kh * TTOT + (tg + 1) * 512]

    x1_store = {}
    x2_store = {}

    try:
      with nc.Block() as blk:
        @blk.gpsimd
        def _(g):
            sLD.inc(g.dma_start(out=sw_sb[:], in_=swr), 16)
            sLD.inc(g.dma_start(out=bflat_sb[:], in_=bflat), 16)
            sLD.inc(g.dma_start(out=ebias_sb[:], in_=ebias), 16)
            sLD.inc(g.dma_start(out=m0_sb[:], in_=m0rep), 16)
            sLD.inc(g.dma_start(out=m4_sb[:], in_=m4rep), 16)
            sLD.inc(g.dma_start(out=ident[:], in_=identin), 16)
        M["setup_ld"] = sLD.v

        @blk.vector
        def _(v):
            v.memset(ones_col[:], 1.0)
            v.memset(halfneg1[:], -0.5)
            sDV.inc(v.memset(onesf[:], 1.0))
        M["setup_dv"] = sDV.v

        def ln_stage(sub, n_tiles, in_stream, store_wait, region_wait_mm):
            for i in range(n_tiles):
                if stop == f"ln{sub}t{i}":
                    @blk.sync
                    def _(s):
                        s.wait_ge(sST.sem, sST.v)
                    raise _StopBuild()
                b = i % 2

                @blk.sync
                def _(s, i=i, b=b):
                    if i >= 2:
                        s.wait_ge(sSE.sem, M[f"ln{sub}_ap_{i-2}"])
                    elif region_wait_mm is not None:
                        s.wait_ge(sMM.sem, region_wait_mm)
                    w = store_wait(i)
                    if w is not None:
                        s.wait_ge(sST.sem, w)
                    sX.inc(s.dma_start(out=xt[b][:], in_=in_stream[i * 128:(i + 1) * 128, :]), 16)
                    M[f"ln{sub}_x_{i}"] = sX.v

                @blk.vector
                def _(v, i=i, b=b):
                    v.wait_ge(sX.sem, M[f"ln{sub}_x_{i}"])
                    if i >= 2:
                        v.wait_ge(sSE.sem, M[f"ln{sub}_ap_{i-2}"])
                        v.wait_ge(sMM.sem, M[f"ln{sub}_tp_{i-2}"])
                    elif region_wait_mm is not None:
                        v.wait_ge(sMM.sem, region_wait_mm)
                    sDV.inc(v.reduce_sum(out=st_s[b][:], in_=xt[b][:], axis=AX.X))
                    sDV.inc(v.scalar_tensor_tensor(out=At[b][:], in0=xt[b][:], scalar=1.0,
                                                   in1=xt[b][:], op0=ALU.mult, op1=ALU.mult,
                                                   accum_out=st_ss[b][:]))
                    v.wait_ge(sDV.sem, sDV.v)
                    sDV.inc(v.tensor_scalar_mul(out=st_mean[b][:], in0=st_s[b][:], scalar1=1.0 / D))
                    v.wait_ge(sDV.sem, sDV.v)
                    sDV.inc(v.scalar_tensor_tensor(out=st_v2[b][:], in0=st_s[b][:], scalar=st_mean[b][:],
                                                   in1=st_ss[b][:], op0=ALU.mult, op1=ALU.subtract))
                    v.wait_ge(sDV.sem, sDV.v)
                    sDV.inc(v.tensor_scalar(out=st_v2[b][:], in0=st_v2[b][:], scalar1=-1.0 / D,
                                            scalar2=1e-5, op0=ALU.mult, op1=ALU.add))
                    M[f"ln{sub}_v2_{i}"] = sDV.v

                if stop == f"ln{sub}s{i}":
                    raise _StopBuild()
                @blk.scalar
                def _(sc, i=i, b=b):
                    sc.wait_ge(sDV.sem, M[f"ln{sub}_v2_{i}"])
                    sSE.inc(sc.activation(st_sd[b][:], st_v2[b][:], AF.Sqrt))
                    M[f"ln{sub}_sd_{i}"] = sSE.v

                @blk.vector
                def _(v, i=i, b=b):
                    v.wait_ge(sSE.sem, M[f"ln{sub}_sd_{i}"])
                    sDV.inc(v.reciprocal(st_r[b][:], st_sd[b][:]))
                    v.wait_ge(sDV.sem, sDV.v)
                    sDV.inc(v.tensor_scalar(out=st_mr[b][:], in0=st_mean[b][:], scalar1=st_r[b][:],
                                            scalar2=-1.0, op0=ALU.mult, op1=ALU.mult))
                    M[f"ln{sub}_mr_{i}"] = sDV.v

                @blk.scalar
                def _(sc, i=i, b=b):
                    sc.wait_ge(sDV.sem, M[f"ln{sub}_mr_{i}"])
                    sSE.inc(sc.activation(At[b][:], xt[b][:], AF.Identity,
                                          bias=st_mr[b][:], scale=st_r[b][:]))
                    M[f"ln{sub}_ap_{i}"] = sSE.v

                if stop == f"ln{sub}a{i}":
                    raise _StopBuild()
                dv_base_tp = sDV.v

                @blk.tensor
                def _(t, i=i, b=b, dv_base_tp=dv_base_tp):
                    t.wait_ge(sSE.sem, M[f"ln{sub}_ap_{i}"])
                    if sub == 1 and i == 0:
                        t.wait_ge(sLD.sem, M["setup_ld"])
                    for gidx in range(8):
                        dst = PT[:] if gidx % 2 == 0 else PRb
                        if gidx >= 2:
                            t.wait_ge(sDV.sem, fut(f"ln{sub}_tpe_{i}_{gidx-2}",
                                                   dv_base_tp + gidx - 1))
                        elif i >= 1:
                            t.wait_ge(sDV.sem, M[f"ln{sub}_tpe_{i-1}_{6+gidx}"])
                        for bb in range(4):
                            hcx = gidx * 4 + bb
                            last = t.transpose(dst[:, bb * 128:(bb + 1) * 128],
                                               At[b][:, hcx * 128:(hcx + 1) * 128], ident[:])
                        sMM.inc(last)
                        M[f"ln{sub}_tpg_{i}_{gidx}"] = sMM.v
                    M[f"ln{sub}_tp_{i}"] = sMM.v

                @blk.vector
                def _(v, i=i, b=b):
                    for gidx in range(8):
                        v.wait_ge(sMM.sem, M[f"ln{sub}_tpg_{i}_{gidx}"])
                        src = PT[:, 0:512] if gidx % 2 == 0 else PRb[:, 0:512]
                        sDV.inc(v.tensor_copy(out=A_evac_dst(sub, gidx, i),
                                              in_=src.rearrange("p (c t) -> p c t", c=4)))
                        M[f"ln{sub}_tpe_{i}_{gidx}"] = sDV.v
            M[f"ln{sub}_done"] = sDV.v
            M[f"ln{sub}_ap_last"] = M[f"ln{sub}_ap_{n_tiles-1}"]
            M[f"ln{sub}_tp_last"] = M[f"ln{sub}_tp_{n_tiles-1}"]

        def gating(sub, n_tiles, span):
            ngr = (n_tiles * 128) // 512
            for tg in range(ngr):
                @blk.tensor
                def _(t, tg=tg):
                    if tg == 0:
                        t.wait_ge(sDV.sem, M[f"ln{sub}_done"])
                    else:
                        t.wait_ge(sSE.sem, M[f"g{sub}_gev_{tg-1}"])
                    for hcx in range(HC):
                        last = t.matmul(P0[0:4, :], sw_sb[:, hcx * 36:hcx * 36 + 4],
                                        span(hcx, tg), start=(hcx == 0), stop=(hcx == HC - 1))
                    sMM.inc(last)
                    M[f"g{sub}_g_{tg}"] = sMM.v
                    for hcx in range(HC):
                        last = t.matmul(P1[0:32, :], sw_sb[:, hcx * 36 + 4:hcx * 36 + 36],
                                        span(hcx, tg), start=(hcx == 0), stop=(hcx == HC - 1))
                    sMM.inc(last)
                    M[f"g{sub}_z_{tg}"] = sMM.v

                @blk.scalar
                def _(sc, tg=tg):
                    sc.wait_ge(sMM.sem, M[f"g{sub}_g_{tg}"])
                    sSE.inc(sc.activation(gfm_sb[0:4, tg * 512:(tg + 1) * 512], P0[0:4, :], AF.Identity))
                    sc.wait_ge(sMM.sem, M[f"g{sub}_z_{tg}"])
                    sSE.inc(sc.activation(zT_sb[0:32, tg * 512:(tg + 1) * 512], P1[0:32, :], AF.Identity))
                    M[f"g{sub}_gev_{tg}"] = sSE.v
            M[f"g{sub}_ev"] = sSE.v
            if stop == f"g{sub}mm":
                raise _StopBuild()

            @blk.tensor
            def _(t):
                t.wait_ge(sSE.sem, M[f"g{sub}_ev"])
                for i in range(n_tiles):
                    last = t.transpose(PT[:, 384 + i * 4:384 + (i + 1) * 4],
                                       gfm_sb[0:4, i * 128:(i + 1) * 128], ident[0:4, 0:4])
                sMM.inc(last)
                M[f"g{sub}_lgT"] = sMM.v
                for i in range(n_tiles):
                    last = t.transpose(PT[:, i * 32:(i + 1) * 32],
                                       zT_sb[0:32, i * 128:(i + 1) * 128], ident[0:32, 0:32])
                sMM.inc(last)
                M[f"g{sub}_zT"] = sMM.v

            if stop == f"g{sub}tp":
                raise _StopBuild()

            @blk.vector
            def _(v):
                v.wait_ge(sMM.sem, M[f"g{sub}_zT"])
                sDV.inc(v.tensor_copy(out=lg_sb[:, 0:n_tiles * 4], in_=PT[:, 384:384 + n_tiles * 4]))
                sDV.inc(v.tensor_copy(out=zm_tm[:, 0:n_tiles * 32], in_=PT[:, 0:n_tiles * 32]))
                M[f"g{sub}_ptev"] = sDV.v
                v.wait_ge(sDV.sem, sDV.v)
                sDV.inc(v.tensor_add(out=lg_sb[:, 0:n_tiles * 4], in0=lg_sb[:, 0:n_tiles * 4],
                                     in1=ebias_sb[:, 0:n_tiles * 4]))
                v.wait_ge(sDV.sem, sDV.v)
                if stop == f"g{sub}evac":
                    return

                def lcol(e):
                    return lg_sb[:].rearrange("p (t e) -> p t e", e=4)[:, 0:n_tiles, e:e + 1]

                def ccol(k):
                    return cmp_sb[:].rearrange("p (k t) -> p k t", t=12)[:, k:k + 1, 0:n_tiles]

                pairs = [(0, 1), (0, 2), (0, 3), (1, 2), (1, 3), (2, 3)]
                cidx = {}
                k = 0
                last = None
                for (a, bq) in pairs:
                    last = v.tensor_tensor(out=ccol(k), in0=lcol(a), in1=lcol(bq), op=ALU.is_ge)
                    cidx[(a, bq)] = k; k += 1
                    last = v.tensor_tensor(out=ccol(k), in0=lcol(bq), in1=lcol(a), op=ALU.is_gt)
                    cidx[(bq, a)] = k; k += 1
                sDV.inc(last)
                v.wait_ge(sDV.sem, sDV.v)
                if stop == f"g{sub}cmp":
                    return

                def ocol(e):
                    return oh_sb[:].rearrange("p (t e) -> p t e", e=4)[:, 0:n_tiles, e:e + 1]

                need = {0: [cidx[(0, 1)], cidx[(0, 2)], cidx[(0, 3)]],
                        1: [cidx[(1, 0)], cidx[(1, 2)], cidx[(1, 3)]],
                        2: [cidx[(2, 0)], cidx[(2, 1)], cidx[(2, 3)]],
                        3: [cidx[(3, 0)], cidx[(3, 1)], cidx[(3, 2)]]}
                for e in range(4):
                    a, bq, cq = need[e]
                    last = v.tensor_tensor(out=ocol(e), in0=ccol(a), in1=ccol(bq), op=ALU.mult)
                sDV.inc(last)
                v.wait_ge(sDV.sem, sDV.v)
                for e in range(4):
                    a, bq, cq = need[e]
                    last = v.tensor_tensor(out=ocol(e), in0=ocol(e), in1=ccol(cq), op=ALU.mult)
                sDV.inc(last)
                M[f"g{sub}_oh"] = sDV.v
                v.wait_ge(sDV.sem, sDV.v)
                if stop == f"g{sub}oh":
                    return
                for i in range(n_tiles):
                    for e in range(4):
                        last = v.tensor_scalar(out=zmm_tm[:, i * 32 + e * 8:i * 32 + e * 8 + 8],
                                               in0=zm_tm[:, i * 32 + e * 8:i * 32 + e * 8 + 8],
                                               scalar1=oh_sb[:, i * 4 + e:i * 4 + e + 1],
                                               scalar2=None, op0=ALU.mult)
                sDV.inc(last)
                M[f"g{sub}_zmm"] = sDV.v

            if stop in (f"g{sub}zmm", f"g{sub}evac", f"g{sub}cmp", f"g{sub}oh"):
                raise _StopBuild()

            @blk.tensor
            def _(t):
                t.wait_ge(sDV.sem, M[f"g{sub}_zmm"])
                for i in range(n_tiles):
                    dst = PT[0:32, i * 128:(i + 1) * 128] if i < 8 else PRb[0:32, (i - 8) * 128:(i - 7) * 128]
                    last = t.transpose(dst, zmm_tm[:, i * 32:(i + 1) * 32], ident[:])
                sMM.inc(last)
                M[f"g{sub}_zmmT"] = sMM.v

            @blk.vector
            def _(v):
                v.wait_ge(sMM.sem, M[f"g{sub}_zmmT"])
                n1 = min(n_tiles, 8)
                last = v.tensor_copy(out=zmT_sb[0:32, 0:n1 * 128], in_=PT[0:32, 0:n1 * 128])
                if n_tiles > 8:
                    last = v.tensor_copy(out=zmT_sb[0:32, 8 * 128:n_tiles * 128],
                                         in_=PRb[0:32, 0:(n_tiles - 8) * 128])
                sDV.inc(last)
                M[f"g{sub}_zmT"] = sDV.v

        def ffn_main(sub, n_tiles, in_stream, out_stream, chunk_wait, ablock, store_rec,
                     wregion_free):
            for oc in range(8):
                @blk.gpsimd
                def _(g, oc=oc):
                    if oc >= 2:
                        g.wait_ge(sMM.sem, M[f"f{sub}_mmoc_{oc-2}"])
                    elif wregion_free is not None:
                        for semh, val in wregion_free:
                            g.wait_ge(semh, val)
                    sLD.inc(g.dma_start(out=wbuf[oc % 2][:].rearrange("p (hc o) -> p hc o", hc=HC),
                                        in_=wupT.rearrange("(hc p) o -> p hc o", p=128)[:, :,
                                                                                        oc * 512:(oc + 1) * 512]), 16)
                    M[f"f{sub}_w_{oc}"] = sLD.v

                dv_base = sDV.v
                for tb in range(n_tiles):
                    @blk.sync
                    def _(s, oc=oc, tb=tb):
                        if tb >= 2:
                            s.wait_ge(sDV.sem, fut(f"f{sub}_ev_{oc}_{tb-2}", dv_base + tb - 1))
                        elif oc >= 1:
                            s.wait_ge(sDV.sem, M[f"f{sub}_ev_{oc-1}_{tb}"])
                        w = chunk_wait(tb, oc)
                        if w is not None:
                            s.wait_ge(sST.sem, w)
                        sX.inc(s.dma_start(out=xold[tb % 2][:],
                                           in_=in_stream[tb * 128:(tb + 1) * 128,
                                                         oc * 512:(oc + 1) * 512]), 16)
                        M[f"f{sub}_xo_{oc}_{tb}"] = sX.v

                    @blk.tensor
                    def _(t, oc=oc, tb=tb):
                        if tb == 0:
                            t.wait_ge(sLD.sem, M[f"f{sub}_w_{oc}"])
                            if oc == 0:
                                t.wait_ge(sDV.sem, M[f"g{sub}_zmT"])
                        ps = PF[tb % 2]
                        if tb >= 2:
                            t.wait_ge(sDV.sem, fut(f"f{sub}_ev_{oc}_{tb-2}", dv_base + tb - 1))
                        elif oc >= 1:
                            t.wait_ge(sDV.sem, M[f"f{sub}_ev_{oc-1}_{n_tiles-2+tb}"])
                        else:
                            t.wait_ge(sSE.sem, M[f"g{sub}_ev"])
                        for hcx in range(HC):
                            t.matmul(ps[:], ablock(hcx, tb), wbuf[oc % 2][:, hcx * 512:(hcx + 1) * 512],
                                     start=(hcx == 0), stop=False)
                        last = t.matmul(ps[:], zmT_sb[0:32, tb * 128:(tb + 1) * 128],
                                        bflat_sb[0:32, oc * 512:(oc + 1) * 512], start=False, stop=True)
                        sMM.inc(last)
                        M[f"f{sub}_mm_{oc}_{tb}"] = sMM.v

                    @blk.vector
                    def _(v, oc=oc, tb=tb):
                        v.wait_ge(sMM.sem, M[f"f{sub}_mm_{oc}_{tb}"])
                        v.wait_ge(sX.sem, M[f"f{sub}_xo_{oc}_{tb}"])
                        if oc >= 1:
                            v.wait_ge(sST.sem, M[f"f{sub}_st_{oc-1}_{tb}"])
                        ps = PF[tb % 2]
                        sDV.inc(v.scalar_tensor_tensor(out=xnew[tb % 2][:], in0=ps[:], scalar=0.5,
                                                       in1=xold[tb % 2][:], op0=ALU.mult, op1=ALU.add))
                        M[f"f{sub}_ev_{oc}_{tb}"] = sDV.v

                    @blk.sync
                    def _(s, oc=oc, tb=tb):
                        s.wait_ge(sDV.sem, M[f"f{sub}_ev_{oc}_{tb}"])
                        sST.inc(s.dma_start(out=out_stream[tb * 128:(tb + 1) * 128,
                                                           oc * 512:(oc + 1) * 512],
                                            in_=xnew[tb % 2][:]), 16)
                        M[f"f{sub}_st_{oc}_{tb}"] = sST.v
                        if store_rec is not None:
                            store_rec[(tb, oc)] = sST.v
                M[f"f{sub}_mmoc_{oc}"] = sMM.v
            M[f"f{sub}_done_mm"] = sMM.v

        ln_stage(1, NT, x_in, lambda i: None, None)
        if stop == "ln1":
            @blk.sync
            def _(s):
                s.wait_ge(sST.sem, sST.v)
            raise _StopBuild()
        gating(1, NT, A_span)
        if stop == "g1":
            @blk.sync
            def _(s):
                s.wait_ge(sST.sem, sST.v)
            raise _StopBuild()
        ffn_main(1, NT, x_in, x1, lambda tb, oc: None, A_block, x1_store,
                 [(sSE.sem, M["ln1_ap_last"]), (sDV.sem, M["ln1_done"]),
                  (sMM.sem, M["ln1_tp_last"])])

        if stop == "f1":
            @blk.sync
            def _(s):
                s.wait_ge(sST.sem, sST.v)
            raise _StopBuild()
        ln_stage(2, NT, x1, lambda i: x1_store[(i, 7)], M["f1_done_mm"])

        if stop == "ln2":
            @blk.sync
            def _(s):
                s.wait_ge(sST.sem, sST.v)
            raise _StopBuild()
        for kh in range(NKV):
            @blk.gpsimd
            def _(g, kh=kh):
                if kh >= 2:
                    g.wait_ge(sMM.sem, M[f"k_mm_{kh-2}"])
                else:
                    g.wait_ge(sMM.sem, M["f1_done_mm"])
                    g.wait_ge(sSE.sem, M["ln2_ap_last"])
                    g.wait_ge(sDV.sem, M["ln2_done"])
                sLD.inc(g.dma_start(out=kwbuf[kh % 2][:].rearrange("p (hc o) -> p hc o", hc=HC),
                                    in_=wkT.rearrange("(hc p) o -> p hc o", p=128)[:, :,
                                                                                   kh * 128:(kh + 1) * 128]), 16)
                M[f"k_w_{kh}"] = sLD.v

            se_base = sSE.v
            mm_base = sMM.v
            for tg in range(3):
                @blk.tensor
                def _(t, kh=kh, tg=tg):
                    if tg == 0:
                        t.wait_ge(sLD.sem, M[f"k_w_{kh}"])
                        if kh == 0:
                            t.wait_ge(sDV.sem, M["ln2_done"])
                    ps = PF[tg % 2]
                    if tg >= 2:
                        t.wait_ge(sSE.sem, fut(f"k_ev_{kh}_{tg-2}", se_base + 2 * (tg - 2) + 1))
                    elif kh >= 1:
                        t.wait_ge(sSE.sem, M[f"k_ev_{kh-1}_{1+tg}"])
                    for hcx in range(HC):
                        last = t.matmul(ps[:], kwbuf[kh % 2][:, hcx * 128:(hcx + 1) * 128],
                                        A_span(hcx, tg), start=(hcx == 0), stop=(hcx == HC - 1))
                    sMM.inc(last)
                    M[f"k_mm_{kh}_{tg}"] = sMM.v

                @blk.scalar
                def _(sc, kh=kh, tg=tg):
                    sc.wait_ge(sMM.sem, M[f"k_mm_{kh}_{tg}"])
                    ps = PF[tg % 2]
                    sSE.inc(sc.activation(K_slice(kh, tg), ps[:], AF.Identity, scale=np.float32(QSCALE)))
                    M[f"k_ev_{kh}_{tg}"] = sSE.v
                    key = f"kn_mm_{kh}_{tg-2}" if tg >= 2 else (f"kn_mm_{kh-1}_{1+tg}" if kh >= 1 else None)
                    if key and key in M:
                        sc.wait_ge(sMM.sem, M[key])
                    sSE.inc(sc.activation(ksq_sb[tg % 2][:], K_slice(kh, tg), AF.Square))
                    M[f"k_sq_{kh}_{tg}"] = sSE.v

                @blk.tensor
                def _(t, kh=kh, tg=tg):
                    t.wait_ge(sSE.sem, M[f"k_sq_{kh}_{tg}"])
                    key = f"kn_red_{kh}_{tg-1}" if tg >= 1 else (f"kn_red_{kh-1}_2" if kh >= 1 else None)
                    if key and key in M:
                        t.wait_ge(sDV.sem, M[key])
                    last = t.matmul(PQ[0:1, 0:512], ones_col[:], ksq_sb[tg % 2][:], start=True, stop=True)
                    sMM.inc(last)
                    M[f"kn_mm_{kh}_{tg}"] = sMM.v

                @blk.vector
                def _(v, kh=kh, tg=tg):
                    if kh == 0 and tg == 0:
                        sDV.inc(v.memset(kmax2_sb[:], 0.0))
                        v.wait_ge(sDV.sem, sDV.v)
                    v.wait_ge(sMM.sem, M[f"kn_mm_{kh}_{tg}"])
                    sDV.inc(v.reduce_max(out=tmax_sb[0:1, 0:1], in_=PQ[0:1, 0:512], axis=AX.X))
                    v.wait_ge(sDV.sem, sDV.v)
                    sDV.inc(v.tensor_tensor(out=kmax2_sb[0:1, kh:kh + 1], in0=kmax2_sb[0:1, kh:kh + 1],
                                            in1=tmax_sb[0:1, 0:1], op=ALU.max))
                    M[f"kn_red_{kh}_{tg}"] = sDV.v
            M[f"k_mm_{kh}"] = M[f"k_mm_{kh}_2"]
        M["k_done"] = sMM.v
        M["kn_done"] = sDV.v

        if stop == "k":
            @blk.sync
            def _(s):
                s.wait_ge(sST.sem, sST.v)
            raise _StopBuild()
        for och in range(8):
            @blk.gpsimd
            def _(g, och=och):
                if och >= 2:
                    g.wait_ge(sMM.sem, M[f"v_mmoc_{och-2}"])
                else:
                    g.wait_ge(sMM.sem, M["k_done"])
                sLD.inc(g.dma_start(out=vwbuf[och % 2][:].rearrange("p (hc o) -> p hc o", hc=HC),
                                    in_=wvT.rearrange("(hc p) o -> p hc o", p=128)[:, :,
                                                                                   och * 128:(och + 1) * 128]), 16)
                M[f"v_w_{och}"] = sLD.v

            se_base = sSE.v
            for tb in range(NT):
                @blk.tensor
                def _(t, och=och, tb=tb):
                    if tb == 0:
                        t.wait_ge(sLD.sem, M[f"v_w_{och}"])
                    ps = PF[tb % 2]
                    if tb >= 2:
                        t.wait_ge(sSE.sem, fut(f"v_ev_{och}_{tb-2}", se_base + tb - 1))
                    elif och >= 1:
                        t.wait_ge(sSE.sem, M[f"v_ev_{och-1}_{NT-2+tb}"])
                    else:
                        t.wait_ge(sSE.sem, M[f"k_ev_{NKV-1}_{1+tb}"])
                    for hcx in range(HC):
                        last = t.matmul(ps[:, 0:128], A_block(hcx, tb),
                                        vwbuf[och % 2][:, hcx * 128:(hcx + 1) * 128],
                                        start=(hcx == 0), stop=(hcx == HC - 1))
                    sMM.inc(last)
                    M[f"v_mm_{och}_{tb}"] = sMM.v

                @blk.scalar
                def _(sc, och=och, tb=tb):
                    sc.wait_ge(sMM.sem, M[f"v_mm_{och}_{tb}"])
                    ps = PF[tb % 2]
                    sSE.inc(sc.activation(V_sb[:, tb * DVA + och * 128: tb * DVA + (och + 1) * 128],
                                          ps[:, 0:128], AF.Identity))
                    M[f"v_ev_{och}_{tb}"] = sSE.v
            M[f"v_mmoc_{och}"] = sMM.v
        M["v_done"] = sMM.v
        M["v_done_se"] = sSE.v

        if stop == "v":
            @blk.sync
            def _(s):
                s.wait_ge(sST.sem, sST.v)
            raise _StopBuild()
        for dblk in range(NQH):
            khq = dblk // 4

            @blk.gpsimd
            def _(g, dblk=dblk):
                if dblk >= 2:
                    g.wait_ge(sMM.sem, M[f"q_mm_{dblk-2}"])
                else:
                    g.wait_ge(sMM.sem, M["v_done"])
                sLD.inc(g.dma_start(out=qwbuf[dblk % 2][:].rearrange("p (hc o) -> p hc o", hc=HC),
                                    in_=wqT.rearrange("(hc p) o -> p hc o", p=128)[:, :,
                                                                                   dblk * 128:(dblk + 1) * 128]), 16)
                M[f"q_w_{dblk}"] = sLD.v

            mm_base = sMM.v
            for half in range(2):
                @blk.tensor
                def _(t, dblk=dblk, half=half):
                    if half == 0:
                        t.wait_ge(sLD.sem, M[f"q_w_{dblk}"])
                    if dblk >= 1:
                        t.wait_ge(sSE.sem, M[f"q_ev_{dblk-1}_{half}"])
                    elif half == 0:
                        t.wait_ge(sDV.sem, M["kn_done"])
                    for hcx in range(HC):
                        last = t.matmul(PQ[:, half * 512:(half + 1) * 512],
                                        qwbuf[dblk % 2][:, hcx * 128:(hcx + 1) * 128],
                                        A_own[:, hcx * TOWN + half * 512: hcx * TOWN + (half + 1) * 512],
                                        start=(hcx == 0), stop=(hcx == HC - 1))
                    sMM.inc(last)
                    M[f"q_mm_{dblk}_{half}"] = sMM.v

                @blk.scalar
                def _(sc, dblk=dblk, half=half, khq=khq):
                    if dblk == 0 and half == 0:
                        sc.wait_ge(sDV.sem, M["kn_done"])
                    sc.wait_ge(sMM.sem, M[f"q_mm_{dblk}_{half}"])
                    if dblk >= 2 and half == 0:
                        sc.wait_ge(sST.sem, M[f"q_st_{dblk-2}"])
                    sSE.inc(sc.activation(qstage[dblk % 2][:, half * 512:(half + 1) * 512],
                                          PQ[:, half * 512:(half + 1) * 512], AF.Identity,
                                          scale=np.float32(QSCALE)))
                    M[f"q_ev_{dblk}_{half}"] = sSE.v
                    key = f"qn_mm_{dblk}_{half-1}" if half >= 1 else (f"qn_mm_{dblk-1}_1" if dblk >= 1 else None)
                    if key:
                        val = fut(key, mm_base + 2) if half >= 1 else M[key]
                        sc.wait_ge(sMM.sem, val)
                    sSE.inc(sc.activation(qsq_sb[:], qstage[dblk % 2][:, half * 512:(half + 1) * 512],
                                          AF.Square))
                    M[f"q_sq_{dblk}_{half}"] = sSE.v

                @blk.tensor
                def _(t, dblk=dblk, half=half):
                    t.wait_ge(sSE.sem, M[f"q_sq_{dblk}_{half}"])
                    key = f"qn_ev_{dblk-1}_{half}" if dblk >= 1 else None
                    if key and key in M:
                        t.wait_ge(sSE.sem, M[key])
                    last = t.matmul(PSs[half][:], ones_col[:], qsq_sb[:], start=True, stop=True)
                    sMM.inc(last)
                    M[f"qn_mm_{dblk}_{half}"] = sMM.v

                @blk.scalar
                def _(sc, dblk=dblk, half=half, khq=khq):
                    sc.wait_ge(sMM.sem, M[f"qn_mm_{dblk}_{half}"])
                    if dblk >= 2 and half == 0:
                        sc.wait_ge(sST.sem, M[f"q_st_{dblk-2}"])
                    sSE.inc(sc.activation(qmrow[dblk % 2][0:1, half * 512:(half + 1) * 512],
                                          PSs[half][:], AF.Identity,
                                          bias=kmax2_sb[0:1, khq:khq + 1]))
                    M[f"qn_ev_{dblk}_{half}"] = sSE.v

            @blk.sync
            def _(s, dblk=dblk):
                s.wait_ge(sSE.sem, M[f"q_ev_{dblk}_1"])
                sST.inc(s.dma_start(out=qscr[dblk * 128:(dblk + 1) * 128, :], in_=qstage[dblk % 2][:]), 16)
                s.wait_ge(sSE.sem, M[f"qn_ev_{dblk}_1"])
                gq = dblk % 4
                khx = dblk // 4
                dst = qmd.rearrange("a (qb kh g t) -> a qb kh g t", qb=8, kh=8, g=4)[:, :, khx, gq, :]
                sST.inc(s.dma_start(out=dst, in_=qmrow[dblk % 2][:].rearrange("a (qb t) -> a qb t", qb=8)), 16)
                M[f"q_st_{dblk}"] = sST.v
            M[f"q_mm_{dblk}"] = M[f"qn_mm_{dblk}_1"]
        M["q_done_mm"] = sMM.v
        M["q_done_st"] = sST.v
        M["q_done_se"] = sSE.v

        if stop == "q":
            @blk.sync
            def _(s):
                s.wait_ge(sST.sem, sST.v)
            raise _StopBuild()
        @blk.gpsimd
        def _(g):
            g.wait_ge(sMM.sem, M["q_done_mm"])
            sLD.inc(g.dma_start(out=em_sb[:], in_=emrep), 16)
            M["em_ld"] = sLD.v

        EMI = {}
        _emi = 0
        for _qb in range(4):
            for _kb in range(4 - _qb):
                EMI[(_qb, _kb)] = _emi
                _emi += 1

        for qb in range(8):
            @blk.sync
            def _(s, qb=qb):
                if qb == 0:
                    s.wait_ge(sST.sem, M["q_done_st"])
                    s.wait_ge(sMM.sem, M["q_done_mm"])
                if qb >= 2:
                    s.wait_ge(sMM.sem, M[f"at_mmq_{qb-2}"])
                src = qscr.rearrange("(h d) t -> d h t", d=128)[:, :, qb * 128:(qb + 1) * 128]
                dst = qtile[qb % 2][:].rearrange("d (h t) -> d h t", h=NQH)
                for part in range(4):
                    sX.inc(s.dma_start(out=dst[:, part * 8:(part + 1) * 8, :],
                                       in_=src[:, part * 8:(part + 1) * 8, :]), 16)
                sX.inc(s.dma_start(out=qmrow2[qb % 2][:], in_=qmd[:, qb * 4096:(qb + 1) * 4096]), 16)
                M[f"at_q_{qb}"] = sX.v

            for kh in range(NKV):
                it = qb * NKV + kh
                par = it % 2
                se_base = sSE.v

                @blk.tensor
                def _(t, qb=qb, kh=kh, it=it, par=par, se_base=se_base):
                    if kh == 0:
                        t.wait_ge(sX.sem, M[f"at_q_{qb}"])
                    if it == 0:
                        t.wait_ge(sLD.sem, M["em_ld"])
                        t.wait_ge(sSE.sem, M["v_done_se"])
                    for kb in range(5):
                        sp = PF[kb % 2]
                        if kb >= 2:
                            t.wait_ge(sSE.sem, fut(f"at_exp_{it}_{kb-2}", se_base + kb - 1))
                        elif it >= 1:
                            prev_kb = 4 if kb == 0 else 3
                            t.wait_ge(sSE.sem, M[f"at_exp_{it-1}_{prev_kb}"])
                        kcol = kh * TTOT + qb * 128 + kb * 128
                        t.matmul(sp[:], K_sb[:, kcol:kcol + 128],
                                 qtile[qb % 2][:, kh * 512:(kh + 1) * 512], start=True, stop=False)
                        masks = []
                        if kb == 0:
                            masks.append(m0_sb[:])
                        if kb == 4:
                            masks.append(m4_sb[:])
                        if (qb, kb) in EMI:
                            ei = EMI[(qb, kb)]
                            masks.append(em_sb[:, ei * 512:(ei + 1) * 512])
                        last = t.matmul(sp[:], halfneg1[:], qmrow2[qb % 2][0:1, kh * 512:(kh + 1) * 512],
                                        start=False, stop=(len(masks) == 0))
                        for mi, mk in enumerate(masks):
                            last = t.matmul(sp[:], ident[:], mk, start=False, stop=(mi == len(masks) - 1))
                        sMM.inc(last)
                        M[f"at_s_{it}_{kb}"] = sMM.v
                    if kh == NKV - 1:
                        M[f"at_mmq_{qb}"] = sMM.v

                @blk.scalar
                def _(sc, qb=qb, kh=kh, it=it, par=par):
                    for kb in range(5):
                        sc.wait_ge(sMM.sem, M[f"at_s_{it}_{kb}"])
                        if kb == 0 and it >= 1:
                            sc.wait_ge(sMM.sem, M[f"at_pv_{it-1}"])
                        sp = PF[kb % 2]
                        sSE.inc(sc.activation(E_sb[:, kb * 512:(kb + 1) * 512], sp[:], AF.Exp))
                        M[f"at_exp_{it}_{kb}"] = sSE.v

                @blk.tensor
                def _(t, qb=qb, kh=kh, it=it, par=par):
                    for kb in range(5):
                        t.wait_ge(sSE.sem, M[f"at_exp_{it}_{kb}"])
                        if kb == 0:
                            if it >= 2:
                                t.wait_ge(sDV.sem, M[f"at_otev_{it-2}"])
                                t.wait_ge(sDV.sem, M[f"at_rs_{it-2}"])
                            else:
                                t.wait_ge(sSE.sem, M["q_done_se"])
                        vcol = (qb + kb) * DVA + kh * 128
                        t.matmul(POT[par], V_sb[:, vcol:vcol + 128],
                                 E_sb[:, kb * 512:(kb + 1) * 512], start=(kb == 0), stop=(kb == 4))
                        last = t.matmul(PSs[par][:], ones_col[:],
                                        E_sb[:, kb * 512:(kb + 1) * 512], start=(kb == 0), stop=(kb == 4))
                    sMM.inc(last)
                    M[f"at_pv_{it}"] = sMM.v

                @blk.vector
                def _(v, qb=qb, kh=kh, it=it, par=par):
                    v.wait_ge(sMM.sem, M[f"at_pv_{it}"])
                    if it >= 1:
                        v.wait_ge(sMM.sem, M[f"at_rsb_{it-1}"])
                    sDV.inc(v.reciprocal(rs_sb[:], PSs[par][:]))
                    M[f"at_rs_{it}"] = sDV.v

                @blk.tensor
                def _(t, qb=qb, kh=kh, it=it, par=par):
                    t.wait_ge(sDV.sem, M[f"at_rs_{it}"])
                    if it >= 1:
                        t.wait_ge(sSE.sem, M[f"at_rsbev_{it-1}"])
                    last = t.matmul(PR[:], onesf[:], rs_sb[:], start=True, stop=True)
                    sMM.inc(last)
                    M[f"at_rsb_{it}"] = sMM.v

                @blk.scalar
                def _(sc, qb=qb, kh=kh, it=it, par=par):
                    sc.wait_ge(sMM.sem, M[f"at_rsb_{it}"])
                    if it >= 1:
                        sc.wait_ge(sDV.sem, M[f"at_otev_{it-1}"])
                    sSE.inc(sc.activation(rsb_sb[:], PR[:], AF.Identity))
                    M[f"at_rsbev_{it}"] = sSE.v

                @blk.vector
                def _(v, qb=qb, kh=kh, it=it, par=par):
                    v.wait_ge(sSE.sem, M[f"at_rsbev_{it}"])
                    dst = OT_sb[:].rearrange("p (h t) -> p h t", h=NQH)[:, kh * 4:kh * 4 + 4,
                                                                       qb * 128:(qb + 1) * 128]
                    sDV.inc(v.tensor_tensor(out=dst, in0=POT[par].rearrange("p (g t) -> p g t", g=4),
                                            in1=rsb_sb[:].rearrange("p (g t) -> p g t", g=4), op=ALU.mult))
                    M[f"at_otev_{it}"] = sDV.v
        M["attn_done_dv"] = sDV.v
        M["attn_done_mm"] = sMM.v

        if stop == "at":
            @blk.sync
            def _(s):
                s.wait_ge(sST.sem, sST.v)
            raise _StopBuild()
        for oc in range(8):
            @blk.gpsimd
            def _(g, oc=oc):
                if oc >= 2:
                    g.wait_ge(sMM.sem, M[f"op_mmoc_{oc-2}"])
                else:
                    g.wait_ge(sMM.sem, M["attn_done_mm"])
                    g.wait_ge(sSE.sem, M[f"at_exp_{63}_4"])
                sLD.inc(g.dma_start(out=wbuf[oc % 2][:].rearrange("p (hc o) -> p hc o", hc=HC),
                                    in_=woT.rearrange("(hc p) o -> p hc o", p=128)[:, :,
                                                                                   oc * 512:(oc + 1) * 512]), 16)
                M[f"op_w_{oc}"] = sLD.v

            dv_base = sDV.v
            for tb in range(NTO):
                @blk.sync
                def _(s, oc=oc, tb=tb):
                    if tb >= 2:
                        s.wait_ge(sDV.sem, fut(f"op_ev_{oc}_{tb-2}", dv_base + tb - 1))
                    elif oc >= 1:
                        s.wait_ge(sDV.sem, M[f"op_ev_{oc-1}_{tb}"])
                    sX.inc(s.dma_start(out=xold[tb % 2][:],
                                       in_=x1[THALO + tb * 128:THALO + (tb + 1) * 128,
                                              oc * 512:(oc + 1) * 512]), 16)
                    M[f"op_xo_{oc}_{tb}"] = sX.v

                @blk.tensor
                def _(t, oc=oc, tb=tb):
                    if tb == 0:
                        t.wait_ge(sLD.sem, M[f"op_w_{oc}"])
                        if oc == 0:
                            t.wait_ge(sDV.sem, M["attn_done_dv"])
                    ps = PF[tb % 2]
                    if tb >= 2:
                        t.wait_ge(sDV.sem, fut(f"op_ev_{oc}_{tb-2}", dv_base + tb - 1))
                    elif oc >= 1:
                        t.wait_ge(sDV.sem, M[f"op_ev_{oc-1}_{NTO-2+tb}"])
                    for dc in range(HC):
                        last = t.matmul(ps[:], OT_sb[:, dc * TOWN + tb * 128: dc * TOWN + (tb + 1) * 128],
                                        wbuf[oc % 2][:, dc * 512:(dc + 1) * 512],
                                        start=(dc == 0), stop=(dc == HC - 1))
                    sMM.inc(last)
                    M[f"op_mm_{oc}_{tb}"] = sMM.v

                @blk.vector
                def _(v, oc=oc, tb=tb):
                    v.wait_ge(sMM.sem, M[f"op_mm_{oc}_{tb}"])
                    v.wait_ge(sX.sem, M[f"op_xo_{oc}_{tb}"])
                    if oc >= 1:
                        v.wait_ge(sST.sem, M[f"op_st_{oc-1}_{tb}"])
                    ps = PF[tb % 2]
                    sDV.inc(v.tensor_add(out=xnew[tb % 2][:], in0=ps[:], in1=xold[tb % 2][:]))
                    M[f"op_ev_{oc}_{tb}"] = sDV.v

                @blk.sync
                def _(s, oc=oc, tb=tb):
                    s.wait_ge(sDV.sem, M[f"op_ev_{oc}_{tb}"])
                    sST.inc(s.dma_start(out=x2[tb * 128:(tb + 1) * 128, oc * 512:(oc + 1) * 512],
                                        in_=xnew[tb % 2][:]), 16)
                    M[f"op_st_{oc}_{tb}"] = sST.v
                    x2_store[(tb, oc)] = sST.v
            M[f"op_mmoc_{oc}"] = sMM.v
        M["op_done_mm"] = sMM.v

        if stop == "op":
            @blk.sync
            def _(s):
                s.wait_ge(sST.sem, sST.v)
            raise _StopBuild()
        ln_stage(3, NTO, x2, lambda i: x2_store[(i, 7)], M["op_done_mm"])
        gating(3, NTO, A3_span)
        ffn_main(3, NTO, x2, out_t, lambda tb, oc: x2_store[(tb, oc)],
                 A3_block, None,
                 [(sMM.sem, M["ln3_tp_last"]), (sSE.sem, M["ln3_ap_last"]),
                  (sDV.sem, M["ln3_done"])])

        @blk.sync
        def _(s):
            s.wait_ge(sST.sem, sST.v)

    except _StopBuild:
        pass
    for key, val in futures:
        if key in M:
            assert M[key] == val, (key, M.get(key), val)

    ctx.close()
    return nc


def make_inputs_per_core(inp):
    x = np.asarray(inp["L0_attn_x"])[0]
    wq = np.asarray(inp["L0_attn_wq"])
    wk = np.asarray(inp["L0_attn_wk"])
    wv = np.asarray(inp["L0_attn_wv"])
    wo = np.asarray(inp["L0_attn_wo"])
    wup = np.asarray(inp["L0_ffn_W_up"])
    gate = np.asarray(inp["L0_ffn_gate_w"])
    eb = np.asarray(inp["L0_ffn_expert_bias"])
    lA = np.asarray(inp["L0_ffn_lora_A"])
    lB = np.asarray(inp["L0_ffn_lora_B"])

    f32 = np.float32
    bf16 = ml_dtypes.bfloat16
    wqT = np.ascontiguousarray(wq.T.astype(f32))
    wkT = np.ascontiguousarray(wk.T.astype(f32))
    wvT = np.ascontiguousarray(wv.T.astype(f32))
    woT = np.ascontiguousarray(wo.T.astype(f32))
    wupT = np.ascontiguousarray(wup.T.astype(f32))
    A_flat = lA.reshape(32, D)
    sw = np.concatenate([gate.T.astype(f32), A_flat.T.astype(f32)], axis=1)
    swr = np.ascontiguousarray(sw.reshape(HC, 128, 36).transpose(1, 0, 2).reshape(128, HC * 36))
    bflat_h = np.ascontiguousarray(lB.transpose(0, 2, 1).reshape(32, D).astype(f32))
    ebias_h = np.ascontiguousarray(np.tile(eb.astype(f32)[None, :], (128, 12)))

    ko = np.arange(128)[:, None]
    qi = np.arange(128)[None, :]
    m0 = np.where(ko > qi, 0.0, -1e30).astype(f32)
    m4 = np.where(ko <= qi, 0.0, -1e30).astype(f32)
    m0rep_h = np.ascontiguousarray(np.tile(m0, (1, 4))).astype(bf16)
    m4rep_h = np.ascontiguousarray(np.tile(m4, (1, 4))).astype(bf16)
    emfull = np.full((128, 10 * 512), np.float32(-1e30)).astype(bf16)
    emzero = np.zeros((128, 10 * 512), dtype=bf16)
    ident_h = np.eye(128, dtype=bf16)

    in_maps = []
    for c in range(8):
        if c == 0:
            x_c = np.concatenate([x[0:512], x[0:1024]], axis=0)
            em = emfull
        else:
            s0 = 1024 * c - 512
            x_c = x[s0:s0 + TTOT]
            em = emzero
        in_maps.append(dict(
            x_in=np.ascontiguousarray(x_c.astype(f32)),
            wqT=wqT, wkT=wkT, wvT=wvT, woT=woT, wupT=wupT,
            swr=swr, bflat=bflat_h, ebias=ebias_h,
            m0rep=m0rep_h, m4rep=m4rep_h, emrep=em,
            identin=ident_h,
        ))
    return in_maps


def kernel(**inputs):
    in_maps = make_inputs_per_core(inputs)
    nc = build()
    res = run_bass_kernel_spmd(nc, in_maps, core_ids=list(range(8)))
    outs = [np.asarray(r["out_t"]) for r in res.results]
    return np.concatenate(outs, axis=0).reshape(1, 8192, D)


# version 02
# speedup vs baseline: 1.0953x; 1.0953x over previous
import numpy as np
import ml_dtypes
import concourse.bass as bass
import concourse.mybir as mybir
from concourse.bass_utils import run_bass_kernel_spmd

F32 = mybir.dt.float32
BF16 = mybir.dt.bfloat16
AF = mybir.ActivationFunctionType
ALU = mybir.AluOpType
AX = mybir.AxisListType

D = 4096
HC = 32
TOWN = 1024
THALO = 512
TTOT = 1536
NT = 12
NTO = 8
NQH = 32
NKV = 8
HD = 128
DVA = NKV * HD
QSCALE = float(128.0 ** -0.25)
KB = 1024


class _StopBuild(Exception):
    pass


class Ctr:
    def __init__(self, sem):
        self.sem = sem
        self.v = 0

    def inc(self, ins, n=1):
        ins.then_inc(self.sem, n)
        self.v += n
        return self.v


def build(stop=None, dbg=False):
    nc = bass.Bass("TRN2", target_bir_lowering=False)

    x_in = nc.dram_tensor("x_in", [TTOT, D], F32, kind="ExternalInput").ap()
    wqT = nc.dram_tensor("wqT", [D, D], F32, kind="ExternalInput").ap()
    wkT = nc.dram_tensor("wkT", [D, NKV * HD], F32, kind="ExternalInput").ap()
    wvT = nc.dram_tensor("wvT", [D, NKV * HD], F32, kind="ExternalInput").ap()
    woT = nc.dram_tensor("woT", [D, D], F32, kind="ExternalInput").ap()
    wupT = nc.dram_tensor("wupT", [D, D], F32, kind="ExternalInput").ap()
    swr = nc.dram_tensor("swr", [128, HC * 36], F32, kind="ExternalInput").ap()
    bflat = nc.dram_tensor("bflat", [32, D], F32, kind="ExternalInput").ap()
    ebias = nc.dram_tensor("ebias", [128, 48], F32, kind="ExternalInput").ap()
    m0rep = nc.dram_tensor("m0rep", [128, 512], BF16, kind="ExternalInput").ap()
    m4rep = nc.dram_tensor("m4rep", [128, 512], BF16, kind="ExternalInput").ap()
    emrep = nc.dram_tensor("emrep", [128, 10 * 512], BF16, kind="ExternalInput").ap()
    identin = nc.dram_tensor("identin", [128, 128], BF16, kind="ExternalInput").ap()

    xkind = dict(kind="ExternalOutput") if dbg else {}
    x1 = nc.dram_tensor("x1", [TTOT, D], F32, **xkind).ap()
    x2 = nc.dram_tensor("x2", [TOWN, D], F32, **xkind).ap()
    qscr = nc.dram_tensor("qscr", [D, TOWN], BF16).ap()
    qmd = nc.dram_tensor("qmd", [1, NQH * TOWN], BF16).ap()
    rsd = nc.dram_tensor("rsd", [64, 512], F32).ap()
    out_t = nc.dram_tensor("out_t", [TOWN, D], F32, kind="ExternalOutput").ap()

    arena = nc.alloc_sbuf_tensor("arena", [128, 206 * KB + 512], mybir.dt.uint8)
    B0 = nc.lookup_mloc(arena).addr

    def sbt(name, shape, dtype, off):
        return nc.alloc_sbuf_tensor_at(name, shape, dtype, offset=B0 + off)

    ident = sbt("ident", [128, 128], BF16, 0)
    ones_col = sbt("ones_col", [128, 1], BF16, 256)
    halfneg1 = sbt("halfneg1", [1, 128], BF16, 288)
    onesf = sbt("onesf", [1, 128], F32, 544)
    sw_sb = sbt("sw_sb", [128, HC * 36], BF16, 1056)
    ebias_sb = sbt("ebias_sb", [128, 48], F32, 3392)
    m0_sb = sbt("m0_sb", [128, 512], BF16, 3584)
    m4_sb = sbt("m4_sb", [128, 512], BF16, 4608)
    lg_sb = sbt("lg_sb", [128, 48], F32, 5632)
    cmp_sb = sbt("cmp_sb", [128, 144], F32, 5824)
    oh_sb = sbt("oh_sb", [128, 48], F32, 6400)
    kmax2_sb = sbt("kmax2_sb", [1, 16], F32, 6592)
    tmax_sb = sbt("tmax_sb", [1, 16], F32, 6656)
    rs_sb = sbt("rs_sb", [1, 512], F32, 6720)
    st_s = [sbt(f"st_s{i}", [128, 1], F32, 8768 + i * 32) for i in range(2)]
    st_ss = [sbt(f"st_ss{i}", [128, 1], F32, 8832 + i * 32) for i in range(2)]
    st_mean = [sbt(f"st_mean{i}", [128, 1], F32, 8896 + i * 32) for i in range(2)]
    st_v2 = [sbt(f"st_v2{i}", [128, 1], F32, 8960 + i * 32) for i in range(2)]
    st_sd = [sbt(f"st_sd{i}", [128, 1], F32, 9024 + i * 32) for i in range(2)]
    st_r = [sbt(f"st_r{i}", [128, 1], F32, 9088 + i * 32) for i in range(2)]
    st_mr = [sbt(f"st_mr{i}", [128, 1], F32, 9152 + i * 32) for i in range(2)]
    xold = [sbt(f"xold{i}", [128, 512], F32, 9216 + i * 2048) for i in range(2)]
    xnew = [sbt(f"xnew{i}", [128, 512], F32, 13312 + i * 2048) for i in range(2)]
    GOFF = 18 * KB
    bflat_sb = sbt("bflat_sb", [32, D], BF16, GOFF)
    zT_sb = sbt("zT_sb", [32, NT * 128], BF16, GOFF + 8 * KB)
    zm_tm = sbt("zm_tm", [128, NT * 32], BF16, GOFF + 11 * KB)
    zmm_tm = sbt("zmm_tm", [128, NT * 32], BF16, GOFF + 12 * KB)
    zmT_sb = sbt("zmT_sb", [32, NT * 128], BF16, GOFF + 13 * KB)
    gfm_sb = sbt("gfm_sb", [4, TTOT], BF16, GOFF + 16 * KB)
    rs_pp = [sbt(f"rs_pp{i}", [1, 512], F32, GOFF + 8 * KB + i * 2 * KB) for i in range(2)]
    rsrow = [sbt(f"rsrow{i}", [1, 512], F32, GOFF + 12 * KB + i * 2 * KB) for i in range(2)]
    rsbc2 = sbt("rsbc2", [128, 512], F32, GOFF + 16 * KB)
    qsq4 = [sbt(f"qsq4_{i}", [128, 512], BF16, GOFF + 18 * KB + i * KB) for i in range(4)]
    WOFF = 40 * KB
    wbuf = [sbt(f"wbuf{i}", [128, HC * 512], BF16, WOFF + i * 32 * KB) for i in range(2)]
    xt = [sbt(f"xt{i}", [128, D], F32, WOFF + i * 16 * KB) for i in range(2)]
    At = [sbt(f"At{i}", [128, D], BF16, WOFF + 32 * KB + i * 8 * KB) for i in range(2)]
    kwbuf = [sbt(f"kwbuf{i}", [128, HC * 128], BF16, WOFF + i * 8 * KB) for i in range(2)]
    vwbuf = [sbt(f"vwbuf{i}", [128, HC * 128], BF16, WOFF + i * 8 * KB) for i in range(2)]
    qwbuf = [sbt(f"qwbuf{i}", [128, HC * 128], BF16, WOFF + i * 8 * KB) for i in range(2)]
    K_sb = sbt("K_sb", [128, NKV * TTOT], BF16, WOFF + 16 * KB)
    V_sb = sbt("V_sb", [128, NT * DVA], BF16, WOFF + 40 * KB)
    E_sb = sbt("E_sb", [128, 5 * 512], BF16, WOFF)
    em_sb = sbt("em_sb", [128, 10 * 512], BF16, WOFF + 5 * KB)
    R1 = 104 * KB
    A_own = sbt("A_own", [128, HC * TOWN], BF16, R1)
    OT_sb = sbt("OT_sb", [128, NQH * TOWN], BF16, R1)
    R2 = 168 * KB
    A_halo = sbt("A_halo", [128, HC * THALO], BF16, R2)
    qtile = [sbt(f"qtile{i}", [128, NQH * 128], BF16, R2 + i * 8 * KB) for i in range(2)]
    qmrow2 = [sbt(f"qmrow2_{i}", [1, NQH * 128], BF16, R2 + 16 * KB + i * 8 * KB) for i in range(2)]
    FOFF = 200 * KB
    qstage = [sbt(f"qstage{i}", [128, 1024], BF16, FOFF + i * 2 * KB) for i in range(2)]
    qsq_sb = sbt("qsq_sb", [128, 512], BF16, FOFF + 4 * KB)
    ksq_sb = [sbt(f"ksq{i}", [128, 512], BF16, FOFF + 4 * KB + i * KB) for i in range(2)]
    qmrow = [sbt(f"qmrow{i}", [1, 1024], BF16, R2 + i * 2 * KB) for i in range(2)]
    rsb_sb = sbt("rsb_sb", [128, 512], F32, FOFF + 4 * KB)

    P0 = nc.alloc_psum_tensor("P0", [128, 512], F32)
    P1 = nc.alloc_psum_tensor("P1", [128, 512], F32)
    PQ = nc.alloc_psum_tensor("PQ", [128, 1024], F32)
    PSs = [nc.alloc_psum_tensor(f"PSs{i}", [1, 512], F32) for i in range(2)]
    PR = nc.alloc_psum_tensor("PR", [128, 512], F32)
    PT = nc.alloc_psum_tensor("PT", [128, 1024], BF16)
    PF = [P0, P1]
    PRb = PR.ap().bitcast(BF16)
    POT = [PQ.ap()[:, 0:512], PQ.ap()[:, 512:1024]]

    from contextlib import ExitStack
    ctx = ExitStack()
    sLD = Ctr(ctx.enter_context(nc.semaphore("sLD")))
    sX = Ctr(ctx.enter_context(nc.semaphore("sX")))
    sMM = Ctr(ctx.enter_context(nc.semaphore("sMM")))
    sDV = Ctr(ctx.enter_context(nc.semaphore("sDV")))
    sSE = Ctr(ctx.enter_context(nc.semaphore("sSE")))
    sST = Ctr(ctx.enter_context(nc.semaphore("sST")))

    M = {}
    futures = []

    def fut(key, val):
        futures.append((key, val))
        return val

    def A_block(hc, tile):
        if tile < 4:
            return A_halo[:, hc * THALO + tile * 128: hc * THALO + (tile + 1) * 128]
        return A_own[:, hc * TOWN + (tile - 4) * 128: hc * TOWN + (tile - 3) * 128]

    def A_span(hc, tg):
        if tg == 0:
            return A_halo[:, hc * THALO: (hc + 1) * THALO]
        return A_own[:, hc * TOWN + (tg - 1) * 512: hc * TOWN + tg * 512]

    def A_evac_dst(sub, gidx, tile):
        if sub != 3 and tile < 4:
            return A_halo[:].rearrange("p (hc t) -> p hc t", hc=HC)[:, gidx * 4:gidx * 4 + 4,
                                                                   tile * 128:(tile + 1) * 128]
        t2 = tile - 4 if sub != 3 else tile
        return A_own[:].rearrange("p (hc t) -> p hc t", hc=HC)[:, gidx * 4:gidx * 4 + 4,
                                                               t2 * 128:(t2 + 1) * 128]

    def A3_block(hc, tile):
        return A_own[:, hc * TOWN + tile * 128: hc * TOWN + (tile + 1) * 128]

    def A3_span(hc, tg):
        return A_own[:, hc * TOWN + tg * 512: hc * TOWN + (tg + 1) * 512]

    def K_slice(kh, tg):
        return K_sb[:, kh * TTOT + tg * 512: kh * TTOT + (tg + 1) * 512]

    x1_store = {}
    x2_store = {}

    try:
      with nc.Block() as blk:
        @blk.gpsimd
        def _(g):
            sLD.inc(g.dma_start(out=sw_sb[:], in_=swr), 16)
            sLD.inc(g.dma_start(out=bflat_sb[:], in_=bflat), 16)
            sLD.inc(g.dma_start(out=ebias_sb[:], in_=ebias), 16)
            sLD.inc(g.dma_start(out=m0_sb[:], in_=m0rep), 16)
            sLD.inc(g.dma_start(out=m4_sb[:], in_=m4rep), 16)
            sLD.inc(g.dma_start(out=ident[:], in_=identin), 16)
        M["setup_ld"] = sLD.v

        @blk.vector
        def _(v):
            v.memset(ones_col[:], 1.0)
            v.memset(halfneg1[:], -0.5)
            sDV.inc(v.memset(onesf[:], 1.0))
        M["setup_dv"] = sDV.v

        def ln_stage(sub, n_tiles, in_stream, store_wait, region_wait_mm):
            for i in range(n_tiles):
                if stop == f"ln{sub}t{i}":
                    @blk.sync
                    def _(s):
                        s.wait_ge(sST.sem, sST.v)
                    raise _StopBuild()
                b = i % 2

                @blk.sync
                def _(s, i=i, b=b):
                    if i >= 2:
                        s.wait_ge(sSE.sem, M[f"ln{sub}_ap_{i-2}"])
                    elif region_wait_mm is not None:
                        s.wait_ge(sMM.sem, region_wait_mm)
                    w = store_wait(i)
                    if w is not None:
                        s.wait_ge(sST.sem, w)
                    sX.inc(s.dma_start(out=xt[b][:], in_=in_stream[i * 128:(i + 1) * 128, :]), 16)
                    M[f"ln{sub}_x_{i}"] = sX.v

                @blk.vector
                def _(v, i=i, b=b):
                    v.wait_ge(sX.sem, M[f"ln{sub}_x_{i}"])
                    if i >= 2:
                        v.wait_ge(sSE.sem, M[f"ln{sub}_ap_{i-2}"])
                        v.wait_ge(sMM.sem, M[f"ln{sub}_tp_{i-2}"])
                    elif region_wait_mm is not None:
                        v.wait_ge(sMM.sem, region_wait_mm)
                    sDV.inc(v.reduce_sum(out=st_s[b][:], in_=xt[b][:], axis=AX.X))
                    sDV.inc(v.scalar_tensor_tensor(out=At[b][:], in0=xt[b][:], scalar=1.0,
                                                   in1=xt[b][:], op0=ALU.mult, op1=ALU.mult,
                                                   accum_out=st_ss[b][:]))
                    v.wait_ge(sDV.sem, sDV.v)
                    sDV.inc(v.tensor_scalar_mul(out=st_mean[b][:], in0=st_s[b][:], scalar1=1.0 / D))
                    v.wait_ge(sDV.sem, sDV.v)
                    sDV.inc(v.scalar_tensor_tensor(out=st_v2[b][:], in0=st_s[b][:], scalar=st_mean[b][:],
                                                   in1=st_ss[b][:], op0=ALU.mult, op1=ALU.subtract))
                    v.wait_ge(sDV.sem, sDV.v)
                    sDV.inc(v.tensor_scalar(out=st_v2[b][:], in0=st_v2[b][:], scalar1=-1.0 / D,
                                            scalar2=1e-5, op0=ALU.mult, op1=ALU.add))
                    M[f"ln{sub}_v2_{i}"] = sDV.v

                if stop == f"ln{sub}s{i}":
                    raise _StopBuild()
                @blk.scalar
                def _(sc, i=i, b=b):
                    sc.wait_ge(sDV.sem, M[f"ln{sub}_v2_{i}"])
                    sSE.inc(sc.activation(st_sd[b][:], st_v2[b][:], AF.Sqrt))
                    M[f"ln{sub}_sd_{i}"] = sSE.v

                @blk.vector
                def _(v, i=i, b=b):
                    v.wait_ge(sSE.sem, M[f"ln{sub}_sd_{i}"])
                    sDV.inc(v.reciprocal(st_r[b][:], st_sd[b][:]))
                    v.wait_ge(sDV.sem, sDV.v)
                    sDV.inc(v.tensor_scalar(out=st_mr[b][:], in0=st_mean[b][:], scalar1=st_r[b][:],
                                            scalar2=-1.0, op0=ALU.mult, op1=ALU.mult))
                    M[f"ln{sub}_mr_{i}"] = sDV.v

                @blk.scalar
                def _(sc, i=i, b=b):
                    sc.wait_ge(sDV.sem, M[f"ln{sub}_mr_{i}"])
                    sSE.inc(sc.activation(At[b][:], xt[b][:], AF.Identity,
                                          bias=st_mr[b][:], scale=st_r[b][:]))
                    M[f"ln{sub}_ap_{i}"] = sSE.v

                if stop == f"ln{sub}a{i}":
                    raise _StopBuild()
                dv_base_tp = sDV.v

                @blk.tensor
                def _(t, i=i, b=b, dv_base_tp=dv_base_tp):
                    t.wait_ge(sSE.sem, M[f"ln{sub}_ap_{i}"])
                    if sub == 1 and i == 0:
                        t.wait_ge(sLD.sem, M["setup_ld"])
                    for gidx in range(8):
                        dst = PT[:] if gidx % 2 == 0 else PRb
                        if gidx >= 2:
                            t.wait_ge(sDV.sem, fut(f"ln{sub}_tpe_{i}_{gidx-2}",
                                                   dv_base_tp + gidx - 1))
                        elif i >= 1:
                            t.wait_ge(sDV.sem, M[f"ln{sub}_tpe_{i-1}_{6+gidx}"])
                        for bb in range(4):
                            hcx = gidx * 4 + bb
                            last = t.transpose(dst[:, bb * 128:(bb + 1) * 128],
                                               At[b][:, hcx * 128:(hcx + 1) * 128], ident[:])
                        sMM.inc(last)
                        M[f"ln{sub}_tpg_{i}_{gidx}"] = sMM.v
                    M[f"ln{sub}_tp_{i}"] = sMM.v

                @blk.vector
                def _(v, i=i, b=b):
                    for gidx in range(8):
                        v.wait_ge(sMM.sem, M[f"ln{sub}_tpg_{i}_{gidx}"])
                        src = PT[:, 0:512] if gidx % 2 == 0 else PRb[:, 0:512]
                        sDV.inc(v.tensor_copy(out=A_evac_dst(sub, gidx, i),
                                              in_=src.rearrange("p (c t) -> p c t", c=4)))
                        M[f"ln{sub}_tpe_{i}_{gidx}"] = sDV.v
            M[f"ln{sub}_done"] = sDV.v
            M[f"ln{sub}_ap_last"] = M[f"ln{sub}_ap_{n_tiles-1}"]
            M[f"ln{sub}_tp_last"] = M[f"ln{sub}_tp_{n_tiles-1}"]

        def gating(sub, n_tiles, span):
            ngr = (n_tiles * 128) // 512
            for tg in range(ngr):
                @blk.tensor
                def _(t, tg=tg):
                    if tg == 0:
                        t.wait_ge(sDV.sem, M[f"ln{sub}_done"])
                    else:
                        t.wait_ge(sSE.sem, M[f"g{sub}_gev_{tg-1}"])
                    for hcx in range(HC):
                        last = t.matmul(P0[0:4, :], sw_sb[:, hcx * 36:hcx * 36 + 4],
                                        span(hcx, tg), start=(hcx == 0), stop=(hcx == HC - 1))
                    sMM.inc(last)
                    M[f"g{sub}_g_{tg}"] = sMM.v
                    for hcx in range(HC):
                        last = t.matmul(P1[0:32, :], sw_sb[:, hcx * 36 + 4:hcx * 36 + 36],
                                        span(hcx, tg), start=(hcx == 0), stop=(hcx == HC - 1))
                    sMM.inc(last)
                    M[f"g{sub}_z_{tg}"] = sMM.v

                @blk.scalar
                def _(sc, tg=tg):
                    sc.wait_ge(sMM.sem, M[f"g{sub}_g_{tg}"])
                    sSE.inc(sc.activation(gfm_sb[0:4, tg * 512:(tg + 1) * 512], P0[0:4, :], AF.Identity))
                    sc.wait_ge(sMM.sem, M[f"g{sub}_z_{tg}"])
                    sSE.inc(sc.activation(zT_sb[0:32, tg * 512:(tg + 1) * 512], P1[0:32, :], AF.Identity))
                    M[f"g{sub}_gev_{tg}"] = sSE.v
            M[f"g{sub}_ev"] = sSE.v
            if stop == f"g{sub}mm":
                raise _StopBuild()

            @blk.tensor
            def _(t):
                t.wait_ge(sSE.sem, M[f"g{sub}_ev"])
                for i in range(n_tiles):
                    last = t.transpose(PT[:, 384 + i * 4:384 + (i + 1) * 4],
                                       gfm_sb[0:4, i * 128:(i + 1) * 128], ident[0:4, 0:4])
                sMM.inc(last)
                M[f"g{sub}_lgT"] = sMM.v
                for i in range(n_tiles):
                    last = t.transpose(PT[:, i * 32:(i + 1) * 32],
                                       zT_sb[0:32, i * 128:(i + 1) * 128], ident[0:32, 0:32])
                sMM.inc(last)
                M[f"g{sub}_zT"] = sMM.v

            if stop == f"g{sub}tp":
                raise _StopBuild()

            @blk.vector
            def _(v):
                v.wait_ge(sMM.sem, M[f"g{sub}_zT"])
                sDV.inc(v.tensor_copy(out=lg_sb[:, 0:n_tiles * 4], in_=PT[:, 384:384 + n_tiles * 4]))
                sDV.inc(v.tensor_copy(out=zm_tm[:, 0:n_tiles * 32], in_=PT[:, 0:n_tiles * 32]))
                M[f"g{sub}_ptev"] = sDV.v
                v.wait_ge(sDV.sem, sDV.v)
                sDV.inc(v.tensor_add(out=lg_sb[:, 0:n_tiles * 4], in0=lg_sb[:, 0:n_tiles * 4],
                                     in1=ebias_sb[:, 0:n_tiles * 4]))
                v.wait_ge(sDV.sem, sDV.v)
                if stop == f"g{sub}evac":
                    return

                def lcol(e):
                    return lg_sb[:].rearrange("p (t e) -> p t e", e=4)[:, 0:n_tiles, e:e + 1]

                def ccol(k):
                    return cmp_sb[:].rearrange("p (k t) -> p k t", t=12)[:, k:k + 1, 0:n_tiles]

                pairs = [(0, 1), (0, 2), (0, 3), (1, 2), (1, 3), (2, 3)]
                cidx = {}
                k = 0
                last = None
                for (a, bq) in pairs:
                    last = v.tensor_tensor(out=ccol(k), in0=lcol(a), in1=lcol(bq), op=ALU.is_ge)
                    cidx[(a, bq)] = k; k += 1
                    last = v.tensor_tensor(out=ccol(k), in0=lcol(bq), in1=lcol(a), op=ALU.is_gt)
                    cidx[(bq, a)] = k; k += 1
                sDV.inc(last)
                v.wait_ge(sDV.sem, sDV.v)
                if stop == f"g{sub}cmp":
                    return

                def ocol(e):
                    return oh_sb[:].rearrange("p (t e) -> p t e", e=4)[:, 0:n_tiles, e:e + 1]

                need = {0: [cidx[(0, 1)], cidx[(0, 2)], cidx[(0, 3)]],
                        1: [cidx[(1, 0)], cidx[(1, 2)], cidx[(1, 3)]],
                        2: [cidx[(2, 0)], cidx[(2, 1)], cidx[(2, 3)]],
                        3: [cidx[(3, 0)], cidx[(3, 1)], cidx[(3, 2)]]}
                for e in range(4):
                    a, bq, cq = need[e]
                    last = v.tensor_tensor(out=ocol(e), in0=ccol(a), in1=ccol(bq), op=ALU.mult)
                sDV.inc(last)
                v.wait_ge(sDV.sem, sDV.v)
                for e in range(4):
                    a, bq, cq = need[e]
                    last = v.tensor_tensor(out=ocol(e), in0=ocol(e), in1=ccol(cq), op=ALU.mult)
                sDV.inc(last)
                M[f"g{sub}_oh"] = sDV.v
                v.wait_ge(sDV.sem, sDV.v)
                if stop == f"g{sub}oh":
                    return
                for i in range(n_tiles):
                    for e in range(4):
                        last = v.tensor_scalar(out=zmm_tm[:, i * 32 + e * 8:i * 32 + e * 8 + 8],
                                               in0=zm_tm[:, i * 32 + e * 8:i * 32 + e * 8 + 8],
                                               scalar1=oh_sb[:, i * 4 + e:i * 4 + e + 1],
                                               scalar2=None, op0=ALU.mult)
                sDV.inc(last)
                M[f"g{sub}_zmm"] = sDV.v

            if stop in (f"g{sub}zmm", f"g{sub}evac", f"g{sub}cmp", f"g{sub}oh"):
                raise _StopBuild()

            @blk.tensor
            def _(t):
                t.wait_ge(sDV.sem, M[f"g{sub}_zmm"])
                for i in range(n_tiles):
                    dst = PT[0:32, i * 128:(i + 1) * 128] if i < 8 else PRb[0:32, (i - 8) * 128:(i - 7) * 128]
                    last = t.transpose(dst, zmm_tm[:, i * 32:(i + 1) * 32], ident[:])
                sMM.inc(last)
                M[f"g{sub}_zmmT"] = sMM.v

            @blk.vector
            def _(v):
                v.wait_ge(sMM.sem, M[f"g{sub}_zmmT"])
                n1 = min(n_tiles, 8)
                last = v.tensor_copy(out=zmT_sb[0:32, 0:n1 * 128], in_=PT[0:32, 0:n1 * 128])
                if n_tiles > 8:
                    last = v.tensor_copy(out=zmT_sb[0:32, 8 * 128:n_tiles * 128],
                                         in_=PRb[0:32, 0:(n_tiles - 8) * 128])
                sDV.inc(last)
                M[f"g{sub}_zmT"] = sDV.v

        def ffn_main(sub, n_tiles, in_stream, out_stream, chunk_wait, ablock, store_rec,
                     wregion_free):
            for oc in range(8):
                @blk.gpsimd
                def _(g, oc=oc):
                    if oc >= 2:
                        g.wait_ge(sMM.sem, M[f"f{sub}_mmoc_{oc-2}"])
                    elif wregion_free is not None:
                        for semh, val in wregion_free:
                            g.wait_ge(semh, val)
                    sLD.inc(g.dma_start(out=wbuf[oc % 2][:].rearrange("p (hc o) -> p hc o", hc=HC),
                                        in_=wupT.rearrange("(hc p) o -> p hc o", p=128)[:, :,
                                                                                        oc * 512:(oc + 1) * 512]), 16)
                    M[f"f{sub}_w_{oc}"] = sLD.v

                dv_base = sDV.v
                for tb in range(n_tiles):
                    @blk.sync
                    def _(s, oc=oc, tb=tb):
                        if tb >= 2:
                            s.wait_ge(sDV.sem, fut(f"f{sub}_ev_{oc}_{tb-2}", dv_base + tb - 1))
                        elif oc >= 1:
                            s.wait_ge(sDV.sem, M[f"f{sub}_ev_{oc-1}_{tb}"])
                        w = chunk_wait(tb, oc)
                        if w is not None:
                            s.wait_ge(sST.sem, w)
                        sX.inc(s.dma_start(out=xold[tb % 2][:],
                                           in_=in_stream[tb * 128:(tb + 1) * 128,
                                                         oc * 512:(oc + 1) * 512]), 16)
                        M[f"f{sub}_xo_{oc}_{tb}"] = sX.v

                    @blk.tensor
                    def _(t, oc=oc, tb=tb):
                        if tb == 0:
                            t.wait_ge(sLD.sem, M[f"f{sub}_w_{oc}"])
                            if oc == 0:
                                t.wait_ge(sDV.sem, M[f"g{sub}_zmT"])
                        ps = PF[tb % 2]
                        if tb >= 2:
                            t.wait_ge(sDV.sem, fut(f"f{sub}_ev_{oc}_{tb-2}", dv_base + tb - 1))
                        elif oc >= 1:
                            t.wait_ge(sDV.sem, M[f"f{sub}_ev_{oc-1}_{n_tiles-2+tb}"])
                        else:
                            t.wait_ge(sSE.sem, M[f"g{sub}_ev"])
                        for hcx in range(HC):
                            t.matmul(ps[:], ablock(hcx, tb), wbuf[oc % 2][:, hcx * 512:(hcx + 1) * 512],
                                     start=(hcx == 0), stop=False)
                        last = t.matmul(ps[:], zmT_sb[0:32, tb * 128:(tb + 1) * 128],
                                        bflat_sb[0:32, oc * 512:(oc + 1) * 512], start=False, stop=True)
                        sMM.inc(last)
                        M[f"f{sub}_mm_{oc}_{tb}"] = sMM.v

                    @blk.vector
                    def _(v, oc=oc, tb=tb):
                        v.wait_ge(sMM.sem, M[f"f{sub}_mm_{oc}_{tb}"])
                        v.wait_ge(sX.sem, M[f"f{sub}_xo_{oc}_{tb}"])
                        if oc >= 1:
                            v.wait_ge(sST.sem, M[f"f{sub}_st_{oc-1}_{tb}"])
                        ps = PF[tb % 2]
                        sDV.inc(v.scalar_tensor_tensor(out=xnew[tb % 2][:], in0=ps[:], scalar=0.5,
                                                       in1=xold[tb % 2][:], op0=ALU.mult, op1=ALU.add))
                        M[f"f{sub}_ev_{oc}_{tb}"] = sDV.v

                    @blk.sync
                    def _(s, oc=oc, tb=tb):
                        s.wait_ge(sDV.sem, M[f"f{sub}_ev_{oc}_{tb}"])
                        sST.inc(s.dma_start(out=out_stream[tb * 128:(tb + 1) * 128,
                                                           oc * 512:(oc + 1) * 512],
                                            in_=xnew[tb % 2][:]), 16)
                        M[f"f{sub}_st_{oc}_{tb}"] = sST.v
                        if store_rec is not None:
                            store_rec[(tb, oc)] = sST.v
                M[f"f{sub}_mmoc_{oc}"] = sMM.v
            M[f"f{sub}_done_mm"] = sMM.v

        ln_stage(1, NT, x_in, lambda i: None, None)
        if stop == "ln1":
            @blk.sync
            def _(s):
                s.wait_ge(sST.sem, sST.v)
            raise _StopBuild()
        gating(1, NT, A_span)
        if stop == "g1":
            @blk.sync
            def _(s):
                s.wait_ge(sST.sem, sST.v)
            raise _StopBuild()
        ffn_main(1, NT, x_in, x1, lambda tb, oc: None, A_block, x1_store,
                 [(sSE.sem, M["ln1_ap_last"]), (sDV.sem, M["ln1_done"]),
                  (sMM.sem, M["ln1_tp_last"])])

        if stop == "f1":
            @blk.sync
            def _(s):
                s.wait_ge(sST.sem, sST.v)
            raise _StopBuild()
        ln_stage(2, NT, x1, lambda i: x1_store[(i, 7)], M["f1_done_mm"])

        if stop == "ln2":
            @blk.sync
            def _(s):
                s.wait_ge(sST.sem, sST.v)
            raise _StopBuild()
        for kh in range(NKV):
            @blk.gpsimd
            def _(g, kh=kh):
                if kh >= 2:
                    g.wait_ge(sMM.sem, M[f"k_mm_{kh-2}"])
                else:
                    g.wait_ge(sMM.sem, M["f1_done_mm"])
                    g.wait_ge(sSE.sem, M["ln2_ap_last"])
                    g.wait_ge(sDV.sem, M["ln2_done"])
                sLD.inc(g.dma_start(out=kwbuf[kh % 2][:].rearrange("p (hc o) -> p hc o", hc=HC),
                                    in_=wkT.rearrange("(hc p) o -> p hc o", p=128)[:, :,
                                                                                   kh * 128:(kh + 1) * 128]), 16)
                M[f"k_w_{kh}"] = sLD.v

            se_base = sSE.v
            mm_base = sMM.v
            for tg in range(3):
                @blk.tensor
                def _(t, kh=kh, tg=tg):
                    if tg == 0:
                        t.wait_ge(sLD.sem, M[f"k_w_{kh}"])
                        if kh == 0:
                            t.wait_ge(sDV.sem, M["ln2_done"])
                    ps = PF[tg % 2]
                    if tg >= 2:
                        t.wait_ge(sSE.sem, fut(f"k_ev_{kh}_{tg-2}", se_base + 2 * (tg - 2) + 1))
                    elif kh >= 1:
                        t.wait_ge(sSE.sem, M[f"k_ev_{kh-1}_{1+tg}"])
                    for hcx in range(HC):
                        last = t.matmul(ps[:], kwbuf[kh % 2][:, hcx * 128:(hcx + 1) * 128],
                                        A_span(hcx, tg), start=(hcx == 0), stop=(hcx == HC - 1))
                    sMM.inc(last)
                    M[f"k_mm_{kh}_{tg}"] = sMM.v

                @blk.scalar
                def _(sc, kh=kh, tg=tg):
                    sc.wait_ge(sMM.sem, M[f"k_mm_{kh}_{tg}"])
                    ps = PF[tg % 2]
                    sSE.inc(sc.activation(K_slice(kh, tg), ps[:], AF.Identity, scale=np.float32(QSCALE)))
                    M[f"k_ev_{kh}_{tg}"] = sSE.v
                    key = f"kn_mm_{kh}_{tg-2}" if tg >= 2 else (f"kn_mm_{kh-1}_{1+tg}" if kh >= 1 else None)
                    if key and key in M:
                        sc.wait_ge(sMM.sem, M[key])
                    sSE.inc(sc.activation(ksq_sb[tg % 2][:], K_slice(kh, tg), AF.Square))
                    M[f"k_sq_{kh}_{tg}"] = sSE.v

                @blk.tensor
                def _(t, kh=kh, tg=tg):
                    t.wait_ge(sSE.sem, M[f"k_sq_{kh}_{tg}"])
                    key = f"kn_red_{kh}_{tg-1}" if tg >= 1 else (f"kn_red_{kh-1}_2" if kh >= 1 else None)
                    if key and key in M:
                        t.wait_ge(sDV.sem, M[key])
                    last = t.matmul(PQ[0:1, 0:512], ones_col[:], ksq_sb[tg % 2][:], start=True, stop=True)
                    sMM.inc(last)
                    M[f"kn_mm_{kh}_{tg}"] = sMM.v

                @blk.vector
                def _(v, kh=kh, tg=tg):
                    if kh == 0 and tg == 0:
                        sDV.inc(v.memset(kmax2_sb[:], 0.0))
                        v.wait_ge(sDV.sem, sDV.v)
                    v.wait_ge(sMM.sem, M[f"kn_mm_{kh}_{tg}"])
                    sDV.inc(v.reduce_max(out=tmax_sb[0:1, 0:1], in_=PQ[0:1, 0:512], axis=AX.X))
                    v.wait_ge(sDV.sem, sDV.v)
                    sDV.inc(v.tensor_tensor(out=kmax2_sb[0:1, kh:kh + 1], in0=kmax2_sb[0:1, kh:kh + 1],
                                            in1=tmax_sb[0:1, 0:1], op=ALU.max))
                    M[f"kn_red_{kh}_{tg}"] = sDV.v
            M[f"k_mm_{kh}"] = M[f"k_mm_{kh}_2"]
        M["k_done"] = sMM.v
        M["kn_done"] = sDV.v

        if stop == "k":
            @blk.sync
            def _(s):
                s.wait_ge(sST.sem, sST.v)
            raise _StopBuild()
        for och in range(8):
            @blk.gpsimd
            def _(g, och=och):
                if och >= 2:
                    g.wait_ge(sMM.sem, M[f"v_mmoc_{och-2}"])
                else:
                    g.wait_ge(sMM.sem, M["k_done"])
                sLD.inc(g.dma_start(out=vwbuf[och % 2][:].rearrange("p (hc o) -> p hc o", hc=HC),
                                    in_=wvT.rearrange("(hc p) o -> p hc o", p=128)[:, :,
                                                                                   och * 128:(och + 1) * 128]), 16)
                M[f"v_w_{och}"] = sLD.v

            se_base = sSE.v
            for tb in range(NT):
                @blk.tensor
                def _(t, och=och, tb=tb):
                    if tb == 0:
                        t.wait_ge(sLD.sem, M[f"v_w_{och}"])
                    ps = PF[tb % 2]
                    if tb >= 2:
                        t.wait_ge(sSE.sem, fut(f"v_ev_{och}_{tb-2}", se_base + tb - 1))
                    elif och >= 1:
                        t.wait_ge(sSE.sem, M[f"v_ev_{och-1}_{NT-2+tb}"])
                    else:
                        t.wait_ge(sSE.sem, M[f"k_ev_{NKV-1}_{1+tb}"])
                    for hcx in range(HC):
                        last = t.matmul(ps[:, 0:128], A_block(hcx, tb),
                                        vwbuf[och % 2][:, hcx * 128:(hcx + 1) * 128],
                                        start=(hcx == 0), stop=(hcx == HC - 1))
                    sMM.inc(last)
                    M[f"v_mm_{och}_{tb}"] = sMM.v

                @blk.scalar
                def _(sc, och=och, tb=tb):
                    sc.wait_ge(sMM.sem, M[f"v_mm_{och}_{tb}"])
                    ps = PF[tb % 2]
                    sSE.inc(sc.activation(V_sb[:, tb * DVA + och * 128: tb * DVA + (och + 1) * 128],
                                          ps[:, 0:128], AF.Identity))
                    M[f"v_ev_{och}_{tb}"] = sSE.v
            M[f"v_mmoc_{och}"] = sMM.v
        M["v_done"] = sMM.v
        M["v_done_se"] = sSE.v

        if stop == "v":
            @blk.sync
            def _(s):
                s.wait_ge(sST.sem, sST.v)
            raise _StopBuild()
        def q_norm_tail(dblk):
            khq = dblk // 4
            for half in range(2):
                @blk.tensor
                def _(t, dblk=dblk, half=half):
                    t.wait_ge(sSE.sem, M[f"q_sq_{dblk}_{half}"])
                    if dblk >= 1:
                        t.wait_ge(sSE.sem, M[f"qn_ev_{dblk-1}_{half}"])
                    last = t.matmul(PSs[half][:], ones_col[:],
                                    qsq4[(dblk % 2) * 2 + half][:], start=True, stop=True)
                    sMM.inc(last)
                    M[f"qn_mm_{dblk}_{half}"] = sMM.v

                @blk.scalar
                def _(sc, dblk=dblk, half=half, khq=khq):
                    sc.wait_ge(sMM.sem, M[f"qn_mm_{dblk}_{half}"])
                    if dblk >= 3 and half == 0:
                        sc.wait_ge(sST.sem, M[f"q_st_{dblk-2}"])
                    sSE.inc(sc.activation(qmrow[dblk % 2][0:1, half * 512:(half + 1) * 512],
                                          PSs[half][:], AF.Identity,
                                          bias=kmax2_sb[0:1, khq:khq + 1]))
                    M[f"qn_ev_{dblk}_{half}"] = sSE.v

            @blk.sync
            def _(s, dblk=dblk):
                s.wait_ge(sSE.sem, M[f"q_ev_{dblk}_1"])
                sST.inc(s.dma_start(out=qscr[dblk * 128:(dblk + 1) * 128, :], in_=qstage[dblk % 2][:]), 16)
                s.wait_ge(sSE.sem, M[f"qn_ev_{dblk}_1"])
                gq = dblk % 4
                khx = dblk // 4
                dst = qmd.rearrange("a (qb kh g t) -> a qb kh g t", qb=8, kh=8, g=4)[:, :, khx, gq, :]
                sST.inc(s.dma_start(out=dst, in_=qmrow[dblk % 2][:].rearrange("a (qb t) -> a qb t", qb=8)), 16)
                M[f"q_st_{dblk}"] = sST.v

        for dblk in range(NQH):
            khq = dblk // 4

            @blk.gpsimd
            def _(g, dblk=dblk):
                if dblk >= 2:
                    g.wait_ge(sMM.sem, M[f"q_mm_{dblk-2}_1"])
                else:
                    g.wait_ge(sMM.sem, M["v_done"])
                sLD.inc(g.dma_start(out=qwbuf[dblk % 2][:].rearrange("p (hc o) -> p hc o", hc=HC),
                                    in_=wqT.rearrange("(hc p) o -> p hc o", p=128)[:, :,
                                                                                   dblk * 128:(dblk + 1) * 128]), 16)
                M[f"q_w_{dblk}"] = sLD.v

            for half in range(2):
                @blk.tensor
                def _(t, dblk=dblk, half=half):
                    if half == 0:
                        t.wait_ge(sLD.sem, M[f"q_w_{dblk}"])
                    if dblk >= 1:
                        t.wait_ge(sSE.sem, M[f"q_ev_{dblk-1}_{half}"])
                    elif half == 0:
                        t.wait_ge(sDV.sem, M["kn_done"])
                    for hcx in range(HC):
                        last = t.matmul(PQ[:, half * 512:(half + 1) * 512],
                                        qwbuf[dblk % 2][:, hcx * 128:(hcx + 1) * 128],
                                        A_own[:, hcx * TOWN + half * 512: hcx * TOWN + (half + 1) * 512],
                                        start=(hcx == 0), stop=(hcx == HC - 1))
                    sMM.inc(last)
                    M[f"q_mm_{dblk}_{half}"] = sMM.v

                @blk.scalar
                def _(sc, dblk=dblk, half=half):
                    if dblk == 0 and half == 0:
                        sc.wait_ge(sDV.sem, M["kn_done"])
                    sc.wait_ge(sMM.sem, M[f"q_mm_{dblk}_{half}"])
                    if dblk >= 2 and half == 0:
                        sc.wait_ge(sST.sem, M[f"q_st_{dblk-2}"])
                    sSE.inc(sc.activation(qstage[dblk % 2][:, half * 512:(half + 1) * 512],
                                          PQ[:, half * 512:(half + 1) * 512], AF.Identity,
                                          scale=np.float32(QSCALE)))
                    M[f"q_ev_{dblk}_{half}"] = sSE.v
                    if dblk >= 2:
                        sc.wait_ge(sMM.sem, M[f"qn_mm_{dblk-2}_{half}"])
                    sSE.inc(sc.activation(qsq4[(dblk % 2) * 2 + half][:],
                                          qstage[dblk % 2][:, half * 512:(half + 1) * 512],
                                          AF.Square))
                    M[f"q_sq_{dblk}_{half}"] = sSE.v

            if dblk >= 1:
                q_norm_tail(dblk - 1)
        q_norm_tail(NQH - 1)
        M["q_done_mm"] = sMM.v
        M["q_done_st"] = sST.v
        M["q_done_se"] = sSE.v

        if stop == "q":
            @blk.sync
            def _(s):
                s.wait_ge(sST.sem, sST.v)
            raise _StopBuild()
        @blk.gpsimd
        def _(g):
            g.wait_ge(sMM.sem, M["q_done_mm"])
            sLD.inc(g.dma_start(out=em_sb[:], in_=emrep), 16)
            M["em_ld"] = sLD.v

        EMI = {}
        _emi = 0
        for _qb in range(4):
            for _kb in range(4 - _qb):
                EMI[(_qb, _kb)] = _emi
                _emi += 1

        for qb in range(8):
            @blk.sync
            def _(s, qb=qb):
                if qb == 0:
                    s.wait_ge(sST.sem, M["q_done_st"])
                    s.wait_ge(sMM.sem, M["q_done_mm"])
                if qb >= 2:
                    s.wait_ge(sMM.sem, M[f"at_mmq_{qb-2}"])
                src = qscr.rearrange("(h d) t -> d h t", d=128)[:, :, qb * 128:(qb + 1) * 128]
                dst = qtile[qb % 2][:].rearrange("d (h t) -> d h t", h=NQH)
                for part in range(4):
                    sX.inc(s.dma_start(out=dst[:, part * 8:(part + 1) * 8, :],
                                       in_=src[:, part * 8:(part + 1) * 8, :]), 16)
                sX.inc(s.dma_start(out=qmrow2[qb % 2][:], in_=qmd[:, qb * 4096:(qb + 1) * 4096]), 16)
                M[f"at_q_{qb}"] = sX.v

            for kh in range(NKV):
                it = qb * NKV + kh
                par = it % 2
                se_base = sSE.v

                @blk.tensor
                def _(t, qb=qb, kh=kh, it=it, par=par, se_base=se_base):
                    if kh == 0:
                        t.wait_ge(sX.sem, M[f"at_q_{qb}"])
                    if it == 0:
                        t.wait_ge(sLD.sem, M["em_ld"])
                        t.wait_ge(sSE.sem, M["v_done_se"])
                    for kb in range(5):
                        sp = PF[kb % 2]
                        if kb >= 2:
                            t.wait_ge(sSE.sem, fut(f"at_exp_{it}_{kb-2}", se_base + kb - 1))
                        elif it >= 1:
                            prev_kb = 4 if kb == 0 else 3
                            t.wait_ge(sSE.sem, M[f"at_exp_{it-1}_{prev_kb}"])
                        kcol = kh * TTOT + qb * 128 + kb * 128
                        t.matmul(sp[:], K_sb[:, kcol:kcol + 128],
                                 qtile[qb % 2][:, kh * 512:(kh + 1) * 512], start=True, stop=False)
                        masks = []
                        if kb == 0:
                            masks.append(m0_sb[:])
                        if kb == 4:
                            masks.append(m4_sb[:])
                        if (qb, kb) in EMI:
                            ei = EMI[(qb, kb)]
                            masks.append(em_sb[:, ei * 512:(ei + 1) * 512])
                        last = t.matmul(sp[:], halfneg1[:], qmrow2[qb % 2][0:1, kh * 512:(kh + 1) * 512],
                                        start=False, stop=(len(masks) == 0))
                        for mi, mk in enumerate(masks):
                            last = t.matmul(sp[:], ident[:], mk, start=False, stop=(mi == len(masks) - 1))
                        sMM.inc(last)
                        M[f"at_s_{it}_{kb}"] = sMM.v
                    if kh == NKV - 1:
                        M[f"at_mmq_{qb}"] = sMM.v

                @blk.scalar
                def _(sc, qb=qb, kh=kh, it=it, par=par):
                    for kb in range(5):
                        sc.wait_ge(sMM.sem, M[f"at_s_{it}_{kb}"])
                        if kb == 0 and it >= 1:
                            sc.wait_ge(sMM.sem, M[f"at_pv_{it-1}"])
                        sp = PF[kb % 2]
                        sSE.inc(sc.activation(E_sb[:, kb * 512:(kb + 1) * 512], sp[:], AF.Exp))
                        M[f"at_exp_{it}_{kb}"] = sSE.v

                @blk.tensor
                def _(t, qb=qb, kh=kh, it=it, par=par):
                    for kb in range(5):
                        t.wait_ge(sSE.sem, M[f"at_exp_{it}_{kb}"])
                        if kb == 0:
                            if it >= 2:
                                t.wait_ge(sDV.sem, M[f"at_otev_{it-2}"])
                                t.wait_ge(sDV.sem, M[f"at_rs_{it-2}"])
                            else:
                                t.wait_ge(sSE.sem, M["q_done_se"])
                        vcol = (qb + kb) * DVA + kh * 128
                        t.matmul(POT[par], V_sb[:, vcol:vcol + 128],
                                 E_sb[:, kb * 512:(kb + 1) * 512], start=(kb == 0), stop=(kb == 4))
                        last = t.matmul(PSs[par][:], ones_col[:],
                                        E_sb[:, kb * 512:(kb + 1) * 512], start=(kb == 0), stop=(kb == 4))
                    sMM.inc(last)
                    M[f"at_pv_{it}"] = sMM.v

                @blk.vector
                def _(v, qb=qb, kh=kh, it=it, par=par):
                    v.wait_ge(sMM.sem, M[f"at_pv_{it}"])
                    if it >= 2:
                        v.wait_ge(sST.sem, M[f"at_rsd_{it-2}"])
                    sDV.inc(v.reciprocal(rs_pp[it % 2][:], PSs[par][:]))
                    M[f"at_rs_{it}"] = sDV.v
                    dst = OT_sb[:].rearrange("p (h t) -> p h t", h=NQH)[:, kh * 4:kh * 4 + 4,
                                                                       qb * 128:(qb + 1) * 128]
                    sDV.inc(v.tensor_copy(out=dst, in_=POT[par].rearrange("p (g t) -> p g t", g=4)))
                    M[f"at_otev_{it}"] = sDV.v

                @blk.sync
                def _(s, it=it):
                    s.wait_ge(sDV.sem, M[f"at_rs_{it}"])
                    sST.inc(s.dma_start(out=rsd[it:it + 1, :], in_=rs_pp[it % 2][:]), 16)
                    M[f"at_rsd_{it}"] = sST.v
        M["attn_mainloop_mm"] = sMM.v

        for it in range(64):
            qb, kh = divmod(it, NKV)

            @blk.sync
            def _(s, it=it):
                if it == 0:
                    s.wait_ge(sST.sem, M[f"at_rsd_{63}"])
                if it >= 2:
                    s.wait_ge(sMM.sem, M[f"at_bnmm_{it-2}"])
                sX.inc(s.dma_start(out=rsrow[it % 2][:], in_=rsd[it:it + 1, :]), 16)
                M[f"at_bnx_{it}"] = sX.v

            @blk.tensor
            def _(t, it=it):
                t.wait_ge(sX.sem, M[f"at_bnx_{it}"])
                if it >= 2:
                    t.wait_ge(sDV.sem, M[f"at_bntt_{it-2}"])
                elif it < 2:
                    t.wait_ge(sSE.sem, M[f"at_exp_{63}_4"])
                    t.wait_ge(sDV.sem, M[f"at_otev_{63}"])
                last = t.matmul(PF[it % 2][:], onesf[:], rsrow[it % 2][:], start=True, stop=True)
                sMM.inc(last)
                M[f"at_bnmm_{it}"] = sMM.v

            @blk.scalar
            def _(sc, it=it):
                sc.wait_ge(sMM.sem, M[f"at_bnmm_{it}"])
                if it >= 2:
                    sc.wait_ge(sDV.sem, M[f"at_bntt_{it-2}"])
                dstc = rsb_sb[:] if it % 2 == 0 else rsbc2[:]
                sSE.inc(sc.activation(dstc, PF[it % 2][:], AF.Identity))
                M[f"at_bnse_{it}"] = sSE.v

            @blk.vector
            def _(v, it=it, qb=qb, kh=kh):
                v.wait_ge(sSE.sem, M[f"at_bnse_{it}"])
                srcc = rsb_sb[:] if it % 2 == 0 else rsbc2[:]
                dst = OT_sb[:].rearrange("p (h t) -> p h t", h=NQH)[:, kh * 4:kh * 4 + 4,
                                                                   qb * 128:(qb + 1) * 128]
                sDV.inc(v.tensor_tensor(out=dst, in0=dst,
                                        in1=srcc.rearrange("p (g t) -> p g t", g=4), op=ALU.mult))
                M[f"at_bntt_{it}"] = sDV.v
        M["attn_done_dv"] = sDV.v
        M["attn_done_mm"] = sMM.v

        if stop == "at":
            @blk.sync
            def _(s):
                s.wait_ge(sST.sem, sST.v)
            raise _StopBuild()
        for oc in range(8):
            @blk.gpsimd
            def _(g, oc=oc):
                if oc >= 2:
                    g.wait_ge(sMM.sem, M[f"op_mmoc_{oc-2}"])
                else:
                    g.wait_ge(sMM.sem, M["attn_done_mm"])
                    g.wait_ge(sSE.sem, M[f"at_exp_{63}_4"])
                sLD.inc(g.dma_start(out=wbuf[oc % 2][:].rearrange("p (hc o) -> p hc o", hc=HC),
                                    in_=woT.rearrange("(hc p) o -> p hc o", p=128)[:, :,
                                                                                   oc * 512:(oc + 1) * 512]), 16)
                M[f"op_w_{oc}"] = sLD.v

            dv_base = sDV.v
            for tb in range(NTO):
                @blk.sync
                def _(s, oc=oc, tb=tb):
                    if tb >= 2:
                        s.wait_ge(sDV.sem, fut(f"op_ev_{oc}_{tb-2}", dv_base + tb - 1))
                    elif oc >= 1:
                        s.wait_ge(sDV.sem, M[f"op_ev_{oc-1}_{tb}"])
                    sX.inc(s.dma_start(out=xold[tb % 2][:],
                                       in_=x1[THALO + tb * 128:THALO + (tb + 1) * 128,
                                              oc * 512:(oc + 1) * 512]), 16)
                    M[f"op_xo_{oc}_{tb}"] = sX.v

                @blk.tensor
                def _(t, oc=oc, tb=tb):
                    if tb == 0:
                        t.wait_ge(sLD.sem, M[f"op_w_{oc}"])
                        if oc == 0:
                            t.wait_ge(sDV.sem, M["attn_done_dv"])
                    ps = PF[tb % 2]
                    if tb >= 2:
                        t.wait_ge(sDV.sem, fut(f"op_ev_{oc}_{tb-2}", dv_base + tb - 1))
                    elif oc >= 1:
                        t.wait_ge(sDV.sem, M[f"op_ev_{oc-1}_{NTO-2+tb}"])
                    for dc in range(HC):
                        last = t.matmul(ps[:], OT_sb[:, dc * TOWN + tb * 128: dc * TOWN + (tb + 1) * 128],
                                        wbuf[oc % 2][:, dc * 512:(dc + 1) * 512],
                                        start=(dc == 0), stop=(dc == HC - 1))
                    sMM.inc(last)
                    M[f"op_mm_{oc}_{tb}"] = sMM.v

                @blk.vector
                def _(v, oc=oc, tb=tb):
                    v.wait_ge(sMM.sem, M[f"op_mm_{oc}_{tb}"])
                    v.wait_ge(sX.sem, M[f"op_xo_{oc}_{tb}"])
                    if oc >= 1:
                        v.wait_ge(sST.sem, M[f"op_st_{oc-1}_{tb}"])
                    ps = PF[tb % 2]
                    sDV.inc(v.tensor_add(out=xnew[tb % 2][:], in0=ps[:], in1=xold[tb % 2][:]))
                    M[f"op_ev_{oc}_{tb}"] = sDV.v

                @blk.sync
                def _(s, oc=oc, tb=tb):
                    s.wait_ge(sDV.sem, M[f"op_ev_{oc}_{tb}"])
                    sST.inc(s.dma_start(out=x2[tb * 128:(tb + 1) * 128, oc * 512:(oc + 1) * 512],
                                        in_=xnew[tb % 2][:]), 16)
                    M[f"op_st_{oc}_{tb}"] = sST.v
                    x2_store[(tb, oc)] = sST.v
            M[f"op_mmoc_{oc}"] = sMM.v
        M["op_done_mm"] = sMM.v

        if stop == "op":
            @blk.sync
            def _(s):
                s.wait_ge(sST.sem, sST.v)
            raise _StopBuild()
        ln_stage(3, NTO, x2, lambda i: x2_store[(i, 7)], M["op_done_mm"])
        gating(3, NTO, A3_span)
        ffn_main(3, NTO, x2, out_t, lambda tb, oc: x2_store[(tb, oc)],
                 A3_block, None,
                 [(sMM.sem, M["ln3_tp_last"]), (sSE.sem, M["ln3_ap_last"]),
                  (sDV.sem, M["ln3_done"])])

        @blk.sync
        def _(s):
            s.wait_ge(sST.sem, sST.v)

    except _StopBuild:
        pass
    for key, val in futures:
        if key in M:
            assert M[key] == val, (key, M.get(key), val)

    ctx.close()
    return nc


def make_inputs_per_core(inp):
    x = np.asarray(inp["L0_attn_x"])[0]
    wq = np.asarray(inp["L0_attn_wq"])
    wk = np.asarray(inp["L0_attn_wk"])
    wv = np.asarray(inp["L0_attn_wv"])
    wo = np.asarray(inp["L0_attn_wo"])
    wup = np.asarray(inp["L0_ffn_W_up"])
    gate = np.asarray(inp["L0_ffn_gate_w"])
    eb = np.asarray(inp["L0_ffn_expert_bias"])
    lA = np.asarray(inp["L0_ffn_lora_A"])
    lB = np.asarray(inp["L0_ffn_lora_B"])

    f32 = np.float32
    bf16 = ml_dtypes.bfloat16
    wqT = np.ascontiguousarray(wq.T.astype(f32))
    wkT = np.ascontiguousarray(wk.T.astype(f32))
    wvT = np.ascontiguousarray(wv.T.astype(f32))
    woT = np.ascontiguousarray(wo.T.astype(f32))
    wupT = np.ascontiguousarray(wup.T.astype(f32))
    A_flat = lA.reshape(32, D)
    sw = np.concatenate([gate.T.astype(f32), A_flat.T.astype(f32)], axis=1)
    swr = np.ascontiguousarray(sw.reshape(HC, 128, 36).transpose(1, 0, 2).reshape(128, HC * 36))
    bflat_h = np.ascontiguousarray(lB.transpose(0, 2, 1).reshape(32, D).astype(f32))
    ebias_h = np.ascontiguousarray(np.tile(eb.astype(f32)[None, :], (128, 12)))

    ko = np.arange(128)[:, None]
    qi = np.arange(128)[None, :]
    m0 = np.where(ko > qi, 0.0, -1e30).astype(f32)
    m4 = np.where(ko <= qi, 0.0, -1e30).astype(f32)
    m0rep_h = np.ascontiguousarray(np.tile(m0, (1, 4))).astype(bf16)
    m4rep_h = np.ascontiguousarray(np.tile(m4, (1, 4))).astype(bf16)
    emfull = np.full((128, 10 * 512), np.float32(-1e30)).astype(bf16)
    emzero = np.zeros((128, 10 * 512), dtype=bf16)
    ident_h = np.eye(128, dtype=bf16)

    in_maps = []
    for c in range(8):
        if c == 0:
            x_c = np.concatenate([x[0:512], x[0:1024]], axis=0)
            em = emfull
        else:
            s0 = 1024 * c - 512
            x_c = x[s0:s0 + TTOT]
            em = emzero
        in_maps.append(dict(
            x_in=np.ascontiguousarray(x_c.astype(f32)),
            wqT=wqT, wkT=wkT, wvT=wvT, woT=woT, wupT=wupT,
            swr=swr, bflat=bflat_h, ebias=ebias_h,
            m0rep=m0rep_h, m4rep=m4rep_h, emrep=em,
            identin=ident_h,
        ))
    return in_maps


def kernel(**inputs):
    in_maps = make_inputs_per_core(inputs)
    nc = build()
    res = run_bass_kernel_spmd(nc, in_maps, core_ids=list(range(8)))
    outs = [np.asarray(r["out_t"]) for r in res.results]
    return np.concatenate(outs, axis=0).reshape(1, 8192, D)


# version 03
# speedup vs baseline: 1.1745x; 1.0724x over previous
import numpy as np
import ml_dtypes
import concourse.bass as bass
import concourse.mybir as mybir
from concourse.bass_utils import run_bass_kernel_spmd

F32 = mybir.dt.float32
BF16 = mybir.dt.bfloat16
AF = mybir.ActivationFunctionType
ALU = mybir.AluOpType
AX = mybir.AxisListType

D = 4096
HC = 32
TOWN = 1024
THALO = 512
TTOT = 1536
NT = 12
NTO = 8
NQH = 32
NKV = 8
HD = 128
DVA = NKV * HD
QSCALE = float(128.0 ** -0.25)
KB = 1024


class _StopBuild(Exception):
    pass


class Ctr:
    def __init__(self, sem):
        self.sem = sem
        self.v = 0

    def inc(self, ins, n=1):
        ins.then_inc(self.sem, n)
        self.v += n
        return self.v


def build(stop=None, dbg=False):
    nc = bass.Bass("TRN2", target_bir_lowering=False)

    x_in = nc.dram_tensor("x_in", [TTOT, D], F32, kind="ExternalInput").ap()
    wqT = nc.dram_tensor("wqT", [D, D], F32, kind="ExternalInput").ap()
    wkT = nc.dram_tensor("wkT", [D, NKV * HD], F32, kind="ExternalInput").ap()
    wvT = nc.dram_tensor("wvT", [D, NKV * HD], F32, kind="ExternalInput").ap()
    woT = nc.dram_tensor("woT", [D, D], F32, kind="ExternalInput").ap()
    wupT = nc.dram_tensor("wupT", [D, D], F32, kind="ExternalInput").ap()
    swr = nc.dram_tensor("swr", [128, HC * 36], F32, kind="ExternalInput").ap()
    bflat = nc.dram_tensor("bflat", [32, D], F32, kind="ExternalInput").ap()
    ebias = nc.dram_tensor("ebias", [128, 48], F32, kind="ExternalInput").ap()
    m0rep = nc.dram_tensor("m0rep", [128, 512], BF16, kind="ExternalInput").ap()
    m4rep = nc.dram_tensor("m4rep", [128, 512], BF16, kind="ExternalInput").ap()
    emrep = nc.dram_tensor("emrep", [1, 512], BF16, kind="ExternalInput").ap()
    identin = nc.dram_tensor("identin", [128, 128], BF16, kind="ExternalInput").ap()

    xkind = dict(kind="ExternalOutput") if dbg else {}
    x1 = nc.dram_tensor("x1", [TTOT, D], F32, **xkind).ap()
    x2 = nc.dram_tensor("x2", [TOWN, D], F32, **xkind).ap()
    qscr = nc.dram_tensor("qscr", [D, TOWN], BF16).ap()
    qmd = nc.dram_tensor("qmd", [1, NQH * TOWN], BF16).ap()
    rsd = nc.dram_tensor("rsd", [64, 512], F32).ap()
    out_t = nc.dram_tensor("out_t", [TOWN, D], F32, kind="ExternalOutput").ap()

    arena = nc.alloc_sbuf_tensor("arena", [128, 206 * KB + 512], mybir.dt.uint8)
    B0 = nc.lookup_mloc(arena).addr

    def sbt(name, shape, dtype, off):
        return nc.alloc_sbuf_tensor_at(name, shape, dtype, offset=B0 + off)

    ident = sbt("ident", [128, 128], BF16, 0)
    ones_col = sbt("ones_col", [128, 1], BF16, 256)
    halfneg1 = sbt("halfneg1", [1, 128], BF16, 288)
    onesf = sbt("onesf", [1, 128], F32, 544)
    sw_sb = sbt("sw_sb", [128, HC * 36], BF16, 1056)
    ebias_sb = sbt("ebias_sb", [128, 48], F32, 3392)
    m0_sb = sbt("m0_sb", [128, 512], BF16, 3584)
    m4_sb = sbt("m4_sb", [128, 512], BF16, 4608)
    lg_sb = sbt("lg_sb", [128, 48], F32, 5632)
    cmp_sb = sbt("cmp_sb", [128, 144], F32, 5824)
    oh_sb = sbt("oh_sb", [128, 48], F32, 6400)
    kmax2_sb = sbt("kmax2_sb", [1, 16], F32, 6592)
    tmax_sb = sbt("tmax_sb", [1, 16], F32, 6656)
    ones_row = sbt("ones_row", [1, 128], BF16, 6720)
    st_s = [sbt(f"st_s{i}", [128, 1], F32, 8768 + i * 32) for i in range(2)]
    st_ss = [sbt(f"st_ss{i}", [128, 1], F32, 8832 + i * 32) for i in range(2)]
    st_mean = [sbt(f"st_mean{i}", [128, 1], F32, 8896 + i * 32) for i in range(2)]
    st_v2 = [sbt(f"st_v2{i}", [128, 1], F32, 8960 + i * 32) for i in range(2)]
    st_sd = [sbt(f"st_sd{i}", [128, 1], F32, 9024 + i * 32) for i in range(2)]
    st_r = [sbt(f"st_r{i}", [128, 1], F32, 9088 + i * 32) for i in range(2)]
    st_mr = [sbt(f"st_mr{i}", [128, 1], F32, 9152 + i * 32) for i in range(2)]
    xold = [sbt(f"xold{i}", [128, 512], F32, 9216 + i * 2048) for i in range(2)]
    xnew = [sbt(f"xnew{i}", [128, 512], F32, 13312 + i * 2048) for i in range(2)]
    GOFF = 18 * KB
    bflat_sb = sbt("bflat_sb", [32, D], BF16, GOFF)
    zT_sb = sbt("zT_sb", [32, NT * 128], BF16, GOFF + 8 * KB)
    zm_tm = sbt("zm_tm", [128, NT * 32], BF16, GOFF + 11 * KB)
    zmm_tm = sbt("zmm_tm", [128, NT * 32], BF16, GOFF + 12 * KB)
    zmT_sb = sbt("zmT_sb", [32, NT * 128], BF16, GOFF + 13 * KB)
    gfm_sb = sbt("gfm_sb", [4, TTOT], BF16, GOFF + 16 * KB)
    rs_pp = [sbt(f"rs_pp{i}", [1, 512], F32, GOFF + 8 * KB + i * 2 * KB) for i in range(2)]
    rsrow = [sbt(f"rsrow{i}", [1, 512], F32, GOFF + 12 * KB + i * 2 * KB) for i in range(2)]
    rsbc2 = sbt("rsbc2", [128, 512], F32, GOFF + 16 * KB)
    qsq4 = [sbt(f"qsq4_{i}", [128, 512], BF16, GOFF + 18 * KB + i * KB) for i in range(4)]
    WOFF = 40 * KB
    wbuf = [sbt(f"wbuf{i}", [128, HC * 512], BF16, WOFF + i * 32 * KB) for i in range(2)]
    xt = [sbt(f"xt{i}", [128, D], F32, WOFF + i * 16 * KB) for i in range(2)]
    At = [sbt(f"At{i}", [128, D], BF16, WOFF + 32 * KB + i * 8 * KB) for i in range(2)]
    kwbuf = [sbt(f"kwbuf{i}", [128, HC * 128], BF16, WOFF + i * 8 * KB) for i in range(2)]
    vwbuf = [sbt(f"vwbuf{i}", [128, HC * 128], BF16, WOFF + i * 8 * KB) for i in range(2)]
    qwbuf = [sbt(f"qwbuf{i}", [128, HC * 128], BF16, WOFF + i * 8 * KB) for i in range(2)]
    K_sb = sbt("K_sb", [128, NKV * TTOT], BF16, WOFF + 16 * KB)
    V_sb = sbt("V_sb", [128, NT * DVA], BF16, WOFF + 40 * KB)
    E_sb = sbt("E_sb", [128, 5 * 512], BF16, WOFF)
    emrow_sb = sbt("emrow_sb", [1, 512], BF16, WOFF + 5 * KB)
    zero_row = sbt("zero_row", [1, 512], BF16, WOFF + 6 * KB)
    R1 = 104 * KB
    A_own = sbt("A_own", [128, HC * TOWN], BF16, R1)
    OT_sb = sbt("OT_sb", [128, NQH * TOWN], BF16, R1)
    R2 = 168 * KB
    A_halo = sbt("A_halo", [128, HC * THALO], BF16, R2)
    qtile = [sbt(f"qtile{i}", [128, NQH * 128], BF16, R2 + i * 8 * KB) for i in range(2)]
    qmrow2 = [sbt(f"qmrow2_{i}", [1, NQH * 128], BF16, R2 + 16 * KB + i * 8 * KB) for i in range(2)]
    FOFF = 200 * KB
    qstage = [sbt(f"qstage{i}", [128, 1024], BF16, FOFF + i * 2 * KB) for i in range(2)]
    qsq_sb = sbt("qsq_sb", [128, 512], BF16, FOFF + 4 * KB)
    ksq_sb = [sbt(f"ksq{i}", [128, 512], BF16, FOFF + 4 * KB + i * KB) for i in range(2)]
    qmrow = [sbt(f"qmrow{i}", [1, 1024], BF16, R2 + i * 2 * KB) for i in range(2)]
    rsb_sb = sbt("rsb_sb", [128, 512], F32, FOFF + 4 * KB)

    P0 = nc.alloc_psum_tensor("P0", [128, 512], F32)
    P1 = nc.alloc_psum_tensor("P1", [128, 512], F32)
    PQ = nc.alloc_psum_tensor("PQ", [128, 1024], F32)
    PSs = [nc.alloc_psum_tensor(f"PSs{i}", [1, 512], F32) for i in range(2)]
    PR = nc.alloc_psum_tensor("PR", [128, 512], F32)
    PT = nc.alloc_psum_tensor("PT", [128, 1024], BF16)
    PF = [P0, P1]
    PRb = PR.ap().bitcast(BF16)
    POT = [PQ.ap()[:, 0:512], PQ.ap()[:, 512:1024]]

    from contextlib import ExitStack
    ctx = ExitStack()
    sLD = Ctr(ctx.enter_context(nc.semaphore("sLD")))
    sX = Ctr(ctx.enter_context(nc.semaphore("sX")))
    sMM = Ctr(ctx.enter_context(nc.semaphore("sMM")))
    sDV = Ctr(ctx.enter_context(nc.semaphore("sDV")))
    sSE = Ctr(ctx.enter_context(nc.semaphore("sSE")))
    sST = Ctr(ctx.enter_context(nc.semaphore("sST")))

    M = {}
    futures = []

    def fut(key, val):
        futures.append((key, val))
        return val

    def A_block(hc, tile):
        if tile < 4:
            return A_halo[:, hc * THALO + tile * 128: hc * THALO + (tile + 1) * 128]
        return A_own[:, hc * TOWN + (tile - 4) * 128: hc * TOWN + (tile - 3) * 128]

    def A_span(hc, tg):
        if tg == 0:
            return A_halo[:, hc * THALO: (hc + 1) * THALO]
        return A_own[:, hc * TOWN + (tg - 1) * 512: hc * TOWN + tg * 512]

    def A_evac_dst(sub, gidx, tile):
        if sub != 3 and tile < 4:
            return A_halo[:].rearrange("p (hc t) -> p hc t", hc=HC)[:, gidx * 4:gidx * 4 + 4,
                                                                   tile * 128:(tile + 1) * 128]
        t2 = tile - 4 if sub != 3 else tile
        return A_own[:].rearrange("p (hc t) -> p hc t", hc=HC)[:, gidx * 4:gidx * 4 + 4,
                                                               t2 * 128:(t2 + 1) * 128]

    def A3_block(hc, tile):
        return A_own[:, hc * TOWN + tile * 128: hc * TOWN + (tile + 1) * 128]

    def A3_span(hc, tg):
        return A_own[:, hc * TOWN + tg * 512: hc * TOWN + (tg + 1) * 512]

    def K_slice(kh, tg):
        return K_sb[:, kh * TTOT + tg * 512: kh * TTOT + (tg + 1) * 512]

    x1_store = {}
    x2_store = {}

    try:
      with nc.Block() as blk:
        @blk.gpsimd
        def _(g):
            sLD.inc(g.dma_start(out=sw_sb[:], in_=swr), 16)
            sLD.inc(g.dma_start(out=bflat_sb[:], in_=bflat), 16)
            sLD.inc(g.dma_start(out=ebias_sb[:], in_=ebias), 16)
            sLD.inc(g.dma_start(out=m0_sb[:], in_=m0rep), 16)
            sLD.inc(g.dma_start(out=m4_sb[:], in_=m4rep), 16)
            sLD.inc(g.dma_start(out=ident[:], in_=identin), 16)
        M["setup_ld"] = sLD.v

        @blk.vector
        def _(v):
            v.memset(ones_col[:], 1.0)
            v.memset(halfneg1[:], -0.5)
            v.memset(ones_row[:], 1.0)
            v.memset(zero_row[:], 0.0)
            sDV.inc(v.memset(onesf[:], 1.0))
        M["setup_dv"] = sDV.v

        def ln_stage(sub, n_tiles, in_stream, store_wait, region_wait_mm):
            for i in range(n_tiles):
                if stop == f"ln{sub}t{i}":
                    @blk.sync
                    def _(s):
                        s.wait_ge(sST.sem, sST.v)
                    raise _StopBuild()
                b = i % 2

                @blk.sync
                def _(s, i=i, b=b):
                    if i >= 2:
                        s.wait_ge(sDV.sem, M[f"ln{sub}_ap_{i-2}"])
                    elif region_wait_mm is not None:
                        s.wait_ge(sMM.sem, region_wait_mm)
                    w = store_wait(i)
                    if w is not None:
                        s.wait_ge(sST.sem, w)
                    sX.inc(s.dma_start(out=xt[b][:], in_=in_stream[i * 128:(i + 1) * 128, :]), 16)
                    M[f"ln{sub}_x_{i}"] = sX.v

                @blk.vector
                def _(v, i=i, b=b):
                    v.wait_ge(sX.sem, M[f"ln{sub}_x_{i}"])
                    if i >= 2:
                        v.wait_ge(sMM.sem, M[f"ln{sub}_tp_{i-2}"])
                    elif region_wait_mm is not None:
                        v.wait_ge(sMM.sem, region_wait_mm)
                    sDV.inc(v.reduce_sum(out=st_s[b][:], in_=xt[b][:], axis=AX.X))
                    sDV.inc(v.scalar_tensor_tensor(out=At[b][:], in0=xt[b][:], scalar=1.0,
                                                   in1=xt[b][:], op0=ALU.mult, op1=ALU.mult,
                                                   accum_out=st_ss[b][:]))
                    v.wait_ge(sDV.sem, sDV.v)
                    sDV.inc(v.tensor_scalar_mul(out=st_mean[b][:], in0=st_s[b][:], scalar1=1.0 / D))
                    v.wait_ge(sDV.sem, sDV.v)
                    sDV.inc(v.scalar_tensor_tensor(out=st_v2[b][:], in0=st_s[b][:], scalar=st_mean[b][:],
                                                   in1=st_ss[b][:], op0=ALU.mult, op1=ALU.subtract))
                    v.wait_ge(sDV.sem, sDV.v)
                    sDV.inc(v.tensor_scalar(out=st_v2[b][:], in0=st_v2[b][:], scalar1=-1.0 / D,
                                            scalar2=1e-5, op0=ALU.mult, op1=ALU.add))
                    M[f"ln{sub}_v2_{i}"] = sDV.v

                if stop == f"ln{sub}s{i}":
                    raise _StopBuild()
                @blk.scalar
                def _(sc, i=i, b=b):
                    sc.wait_ge(sDV.sem, M[f"ln{sub}_v2_{i}"])
                    sSE.inc(sc.activation(st_sd[b][:], st_v2[b][:], AF.Sqrt))
                    M[f"ln{sub}_sd_{i}"] = sSE.v

                @blk.vector
                def _(v, i=i, b=b):
                    v.wait_ge(sSE.sem, M[f"ln{sub}_sd_{i}"])
                    sDV.inc(v.reciprocal(st_r[b][:], st_sd[b][:]))
                    v.wait_ge(sDV.sem, sDV.v)
                    sDV.inc(v.tensor_scalar(out=st_mr[b][:], in0=st_mean[b][:], scalar1=st_r[b][:],
                                            scalar2=-1.0, op0=ALU.mult, op1=ALU.mult))
                    M[f"ln{sub}_mr_{i}"] = sDV.v

                @blk.vector
                def _(v, i=i, b=b):
                    v.wait_ge(sDV.sem, M[f"ln{sub}_mr_{i}"])
                    sDV.inc(v.tensor_scalar(out=At[b][:], in0=xt[b][:], scalar1=st_r[b][:],
                                            scalar2=st_mr[b][:], op0=ALU.mult, op1=ALU.add))
                    M[f"ln{sub}_ap_{i}"] = sDV.v

                if stop == f"ln{sub}a{i}":
                    raise _StopBuild()
                dv_base_tp = sDV.v

                @blk.tensor
                def _(t, i=i, b=b, dv_base_tp=dv_base_tp):
                    t.wait_ge(sDV.sem, M[f"ln{sub}_ap_{i}"])
                    if sub == 1 and i == 0:
                        t.wait_ge(sLD.sem, M["setup_ld"])
                    for gidx in range(8):
                        dst = PT[:] if gidx % 2 == 0 else PRb
                        if gidx >= 2:
                            t.wait_ge(sDV.sem, fut(f"ln{sub}_tpe_{i}_{gidx-2}",
                                                   dv_base_tp + gidx - 1))
                        elif i >= 1:
                            t.wait_ge(sDV.sem, M[f"ln{sub}_tpe_{i-1}_{6+gidx}"])
                        for bb in range(4):
                            hcx = gidx * 4 + bb
                            last = t.transpose(dst[:, bb * 128:(bb + 1) * 128],
                                               At[b][:, hcx * 128:(hcx + 1) * 128], ident[:])
                        sMM.inc(last)
                        M[f"ln{sub}_tpg_{i}_{gidx}"] = sMM.v
                    M[f"ln{sub}_tp_{i}"] = sMM.v

                @blk.vector
                def _(v, i=i, b=b):
                    for gidx in range(8):
                        v.wait_ge(sMM.sem, M[f"ln{sub}_tpg_{i}_{gidx}"])
                        src = PT[:, 0:512] if gidx % 2 == 0 else PRb[:, 0:512]
                        sDV.inc(v.tensor_copy(out=A_evac_dst(sub, gidx, i),
                                              in_=src.rearrange("p (c t) -> p c t", c=4)))
                        M[f"ln{sub}_tpe_{i}_{gidx}"] = sDV.v
            M[f"ln{sub}_done"] = sDV.v
            M[f"ln{sub}_ap_last"] = M[f"ln{sub}_ap_{n_tiles-1}"]
            M[f"ln{sub}_tp_last"] = M[f"ln{sub}_tp_{n_tiles-1}"]

        def gating(sub, n_tiles, span):
            ngr = (n_tiles * 128) // 512
            for tg in range(ngr):
                @blk.tensor
                def _(t, tg=tg):
                    if tg == 0:
                        t.wait_ge(sDV.sem, M[f"ln{sub}_done"])
                    else:
                        t.wait_ge(sDV.sem, M[f"g{sub}_gev_{tg-1}"])
                    for hcx in range(HC):
                        last = t.matmul(P0[0:4, :], sw_sb[:, hcx * 36:hcx * 36 + 4],
                                        span(hcx, tg), start=(hcx == 0), stop=(hcx == HC - 1))
                    sMM.inc(last)
                    M[f"g{sub}_g_{tg}"] = sMM.v
                    for hcx in range(HC):
                        last = t.matmul(P1[0:32, :], sw_sb[:, hcx * 36 + 4:hcx * 36 + 36],
                                        span(hcx, tg), start=(hcx == 0), stop=(hcx == HC - 1))
                    sMM.inc(last)
                    M[f"g{sub}_z_{tg}"] = sMM.v

                @blk.vector
                def _(v, tg=tg):
                    v.wait_ge(sMM.sem, M[f"g{sub}_g_{tg}"])
                    sDV.inc(v.tensor_copy(out=gfm_sb[0:4, tg * 512:(tg + 1) * 512], in_=P0[0:4, :]))
                    v.wait_ge(sMM.sem, M[f"g{sub}_z_{tg}"])
                    sDV.inc(v.tensor_copy(out=zT_sb[0:32, tg * 512:(tg + 1) * 512], in_=P1[0:32, :]))
                    M[f"g{sub}_gev_{tg}"] = sDV.v
            M[f"g{sub}_ev"] = sDV.v
            if stop == f"g{sub}mm":
                raise _StopBuild()

            @blk.tensor
            def _(t):
                t.wait_ge(sDV.sem, M[f"g{sub}_ev"])
                for i in range(n_tiles):
                    last = t.transpose(PT[:, 384 + i * 4:384 + (i + 1) * 4],
                                       gfm_sb[0:4, i * 128:(i + 1) * 128], ident[0:4, 0:4])
                sMM.inc(last)
                M[f"g{sub}_lgT"] = sMM.v
                for i in range(n_tiles):
                    last = t.transpose(PT[:, i * 32:(i + 1) * 32],
                                       zT_sb[0:32, i * 128:(i + 1) * 128], ident[0:32, 0:32])
                sMM.inc(last)
                M[f"g{sub}_zT"] = sMM.v

            if stop == f"g{sub}tp":
                raise _StopBuild()

            @blk.vector
            def _(v):
                v.wait_ge(sMM.sem, M[f"g{sub}_zT"])
                sDV.inc(v.tensor_copy(out=lg_sb[:, 0:n_tiles * 4], in_=PT[:, 384:384 + n_tiles * 4]))
                sDV.inc(v.tensor_copy(out=zm_tm[:, 0:n_tiles * 32], in_=PT[:, 0:n_tiles * 32]))
                M[f"g{sub}_ptev"] = sDV.v
                v.wait_ge(sDV.sem, sDV.v)
                sDV.inc(v.tensor_add(out=lg_sb[:, 0:n_tiles * 4], in0=lg_sb[:, 0:n_tiles * 4],
                                     in1=ebias_sb[:, 0:n_tiles * 4]))
                v.wait_ge(sDV.sem, sDV.v)
                if stop == f"g{sub}evac":
                    return

                def lcol(e):
                    return lg_sb[:].rearrange("p (t e) -> p t e", e=4)[:, 0:n_tiles, e:e + 1]

                def ccol(k):
                    return cmp_sb[:].rearrange("p (k t) -> p k t", t=12)[:, k:k + 1, 0:n_tiles]

                pairs = [(0, 1), (0, 2), (0, 3), (1, 2), (1, 3), (2, 3)]
                cidx = {}
                k = 0
                last = None
                for (a, bq) in pairs:
                    last = v.tensor_tensor(out=ccol(k), in0=lcol(a), in1=lcol(bq), op=ALU.is_ge)
                    cidx[(a, bq)] = k; k += 1
                    last = v.tensor_tensor(out=ccol(k), in0=lcol(bq), in1=lcol(a), op=ALU.is_gt)
                    cidx[(bq, a)] = k; k += 1
                sDV.inc(last)
                v.wait_ge(sDV.sem, sDV.v)
                if stop == f"g{sub}cmp":
                    return

                def ocol(e):
                    return oh_sb[:].rearrange("p (t e) -> p t e", e=4)[:, 0:n_tiles, e:e + 1]

                need = {0: [cidx[(0, 1)], cidx[(0, 2)], cidx[(0, 3)]],
                        1: [cidx[(1, 0)], cidx[(1, 2)], cidx[(1, 3)]],
                        2: [cidx[(2, 0)], cidx[(2, 1)], cidx[(2, 3)]],
                        3: [cidx[(3, 0)], cidx[(3, 1)], cidx[(3, 2)]]}
                for e in range(4):
                    a, bq, cq = need[e]
                    last = v.tensor_tensor(out=ocol(e), in0=ccol(a), in1=ccol(bq), op=ALU.mult)
                sDV.inc(last)
                v.wait_ge(sDV.sem, sDV.v)
                for e in range(4):
                    a, bq, cq = need[e]
                    last = v.tensor_tensor(out=ocol(e), in0=ocol(e), in1=ccol(cq), op=ALU.mult)
                sDV.inc(last)
                M[f"g{sub}_oh"] = sDV.v
                v.wait_ge(sDV.sem, sDV.v)
                if stop == f"g{sub}oh":
                    return
                for i in range(n_tiles):
                    for e in range(4):
                        last = v.tensor_scalar(out=zmm_tm[:, i * 32 + e * 8:i * 32 + e * 8 + 8],
                                               in0=zm_tm[:, i * 32 + e * 8:i * 32 + e * 8 + 8],
                                               scalar1=oh_sb[:, i * 4 + e:i * 4 + e + 1],
                                               scalar2=None, op0=ALU.mult)
                sDV.inc(last)
                M[f"g{sub}_zmm"] = sDV.v

            if stop in (f"g{sub}zmm", f"g{sub}evac", f"g{sub}cmp", f"g{sub}oh"):
                raise _StopBuild()

            @blk.tensor
            def _(t):
                t.wait_ge(sDV.sem, M[f"g{sub}_zmm"])
                for i in range(n_tiles):
                    dst = PT[0:32, i * 128:(i + 1) * 128] if i < 8 else PRb[0:32, (i - 8) * 128:(i - 7) * 128]
                    last = t.transpose(dst, zmm_tm[:, i * 32:(i + 1) * 32], ident[:])
                sMM.inc(last)
                M[f"g{sub}_zmmT"] = sMM.v

            @blk.vector
            def _(v):
                v.wait_ge(sMM.sem, M[f"g{sub}_zmmT"])
                n1 = min(n_tiles, 8)
                last = v.tensor_copy(out=zmT_sb[0:32, 0:n1 * 128], in_=PT[0:32, 0:n1 * 128])
                if n_tiles > 8:
                    last = v.tensor_copy(out=zmT_sb[0:32, 8 * 128:n_tiles * 128],
                                         in_=PRb[0:32, 0:(n_tiles - 8) * 128])
                sDV.inc(last)
                M[f"g{sub}_zmT"] = sDV.v

        def ffn_main(sub, n_tiles, in_stream, out_stream, chunk_wait, ablock, store_rec,
                     wregion_free):
            for oc in range(8):
                @blk.gpsimd
                def _(g, oc=oc):
                    if oc >= 2:
                        g.wait_ge(sMM.sem, M[f"f{sub}_mmoc_{oc-2}"])
                    elif wregion_free is not None:
                        for semh, val in wregion_free:
                            g.wait_ge(semh, val)
                    sLD.inc(g.dma_start(out=wbuf[oc % 2][:].rearrange("p (hc o) -> p hc o", hc=HC),
                                        in_=wupT.rearrange("(hc p) o -> p hc o", p=128)[:, :,
                                                                                        oc * 512:(oc + 1) * 512]), 16)
                    M[f"f{sub}_w_{oc}"] = sLD.v

                dv_base = sDV.v
                for tb in range(n_tiles):
                    @blk.sync
                    def _(s, oc=oc, tb=tb):
                        if tb >= 2:
                            s.wait_ge(sDV.sem, fut(f"f{sub}_ev_{oc}_{tb-2}", dv_base + tb - 1))
                        elif oc >= 1:
                            s.wait_ge(sDV.sem, M[f"f{sub}_ev_{oc-1}_{tb}"])
                        w = chunk_wait(tb, oc)
                        if w is not None:
                            s.wait_ge(sST.sem, w)
                        sX.inc(s.dma_start(out=xold[tb % 2][:],
                                           in_=in_stream[tb * 128:(tb + 1) * 128,
                                                         oc * 512:(oc + 1) * 512]), 16)
                        M[f"f{sub}_xo_{oc}_{tb}"] = sX.v

                    @blk.tensor
                    def _(t, oc=oc, tb=tb):
                        if tb == 0:
                            t.wait_ge(sLD.sem, M[f"f{sub}_w_{oc}"])
                            if oc == 0:
                                t.wait_ge(sDV.sem, M[f"g{sub}_zmT"])
                        ps = PF[tb % 2]
                        if tb >= 2:
                            t.wait_ge(sDV.sem, fut(f"f{sub}_ev_{oc}_{tb-2}", dv_base + tb - 1))
                        elif oc >= 1:
                            t.wait_ge(sDV.sem, M[f"f{sub}_ev_{oc-1}_{n_tiles-2+tb}"])
                        else:
                            t.wait_ge(sDV.sem, M[f"g{sub}_ev"])
                        for hcx in range(HC):
                            t.matmul(ps[:], ablock(hcx, tb), wbuf[oc % 2][:, hcx * 512:(hcx + 1) * 512],
                                     start=(hcx == 0), stop=False)
                        last = t.matmul(ps[:], zmT_sb[0:32, tb * 128:(tb + 1) * 128],
                                        bflat_sb[0:32, oc * 512:(oc + 1) * 512], start=False, stop=True)
                        sMM.inc(last)
                        M[f"f{sub}_mm_{oc}_{tb}"] = sMM.v

                    @blk.vector
                    def _(v, oc=oc, tb=tb):
                        v.wait_ge(sMM.sem, M[f"f{sub}_mm_{oc}_{tb}"])
                        v.wait_ge(sX.sem, M[f"f{sub}_xo_{oc}_{tb}"])
                        if oc >= 1:
                            v.wait_ge(sST.sem, M[f"f{sub}_st_{oc-1}_{tb}"])
                        ps = PF[tb % 2]
                        sDV.inc(v.scalar_tensor_tensor(out=xnew[tb % 2][:], in0=ps[:], scalar=0.5,
                                                       in1=xold[tb % 2][:], op0=ALU.mult, op1=ALU.add))
                        M[f"f{sub}_ev_{oc}_{tb}"] = sDV.v

                    @blk.sync
                    def _(s, oc=oc, tb=tb):
                        s.wait_ge(sDV.sem, M[f"f{sub}_ev_{oc}_{tb}"])
                        sST.inc(s.dma_start(out=out_stream[tb * 128:(tb + 1) * 128,
                                                           oc * 512:(oc + 1) * 512],
                                            in_=xnew[tb % 2][:]), 16)
                        M[f"f{sub}_st_{oc}_{tb}"] = sST.v
                        if store_rec is not None:
                            store_rec[(tb, oc)] = sST.v
                M[f"f{sub}_mmoc_{oc}"] = sMM.v
            M[f"f{sub}_done_mm"] = sMM.v

        ln_stage(1, NT, x_in, lambda i: None, None)
        if stop == "ln1":
            @blk.sync
            def _(s):
                s.wait_ge(sST.sem, sST.v)
            raise _StopBuild()
        gating(1, NT, A_span)
        if stop == "g1":
            @blk.sync
            def _(s):
                s.wait_ge(sST.sem, sST.v)
            raise _StopBuild()
        ffn_main(1, NT, x_in, x1, lambda tb, oc: None, A_block, x1_store,
                 [(sDV.sem, M["ln1_done"]), (sMM.sem, M["ln1_tp_last"])])

        if stop == "f1":
            @blk.sync
            def _(s):
                s.wait_ge(sST.sem, sST.v)
            raise _StopBuild()
        ln_stage(2, NT, x1, lambda i: x1_store[(i, 7)], M["f1_done_mm"])

        if stop == "ln2":
            @blk.sync
            def _(s):
                s.wait_ge(sST.sem, sST.v)
            raise _StopBuild()
        for kh in range(NKV):
            @blk.gpsimd
            def _(g, kh=kh):
                if kh >= 2:
                    g.wait_ge(sMM.sem, M[f"k_mm_{kh-2}"])
                else:
                    g.wait_ge(sMM.sem, M["f1_done_mm"])
                    g.wait_ge(sDV.sem, M["ln2_done"])
                sLD.inc(g.dma_start(out=kwbuf[kh % 2][:].rearrange("p (hc o) -> p hc o", hc=HC),
                                    in_=wkT.rearrange("(hc p) o -> p hc o", p=128)[:, :,
                                                                                   kh * 128:(kh + 1) * 128]), 16)
                M[f"k_w_{kh}"] = sLD.v

            dv_base_k = sDV.v
            for tg in range(3):
                @blk.tensor
                def _(t, kh=kh, tg=tg):
                    if tg == 0:
                        t.wait_ge(sLD.sem, M[f"k_w_{kh}"])
                        if kh == 0:
                            t.wait_ge(sDV.sem, M["ln2_done"])
                    ps = PF[tg % 2]
                    if tg >= 2:
                        t.wait_ge(sDV.sem, fut(f"k_ev_{kh}_{tg-2}", dv_base_k + 2 * (tg - 2) + 1))
                    elif kh >= 1:
                        t.wait_ge(sDV.sem, M[f"k_ev_{kh-1}_{1+tg}"])
                    for hcx in range(HC):
                        last = t.matmul(ps[:], kwbuf[kh % 2][:, hcx * 128:(hcx + 1) * 128],
                                        A_span(hcx, tg), start=(hcx == 0), stop=(hcx == HC - 1))
                    sMM.inc(last)
                    M[f"k_mm_{kh}_{tg}"] = sMM.v

                @blk.vector
                def _(v, kh=kh, tg=tg):
                    v.wait_ge(sMM.sem, M[f"k_mm_{kh}_{tg}"])
                    ps = PF[tg % 2]
                    sDV.inc(v.tensor_scalar_mul(out=K_slice(kh, tg), in0=ps[:], scalar1=QSCALE))
                    M[f"k_ev_{kh}_{tg}"] = sDV.v
                    key = f"kn_mm_{kh}_{tg-2}" if tg >= 2 else (f"kn_mm_{kh-1}_{1+tg}" if kh >= 1 else None)
                    if key and key in M:
                        v.wait_ge(sMM.sem, M[key])
                    v.wait_ge(sDV.sem, sDV.v)
                    sDV.inc(v.scalar_tensor_tensor(out=ksq_sb[tg % 2][:], in0=K_slice(kh, tg), scalar=1.0,
                                                   in1=K_slice(kh, tg), op0=ALU.mult, op1=ALU.mult))
                    M[f"k_sq_{kh}_{tg}"] = sDV.v

                @blk.tensor
                def _(t, kh=kh, tg=tg):
                    t.wait_ge(sDV.sem, M[f"k_sq_{kh}_{tg}"])
                    key = f"kn_red_{kh}_{tg-1}" if tg >= 1 else (f"kn_red_{kh-1}_2" if kh >= 1 else None)
                    if key and key in M:
                        t.wait_ge(sDV.sem, M[key])
                    last = t.matmul(PQ[0:1, 0:512], ones_col[:], ksq_sb[tg % 2][:], start=True, stop=True)
                    sMM.inc(last)
                    M[f"kn_mm_{kh}_{tg}"] = sMM.v

                @blk.vector
                def _(v, kh=kh, tg=tg):
                    if kh == 0 and tg == 0:
                        sDV.inc(v.memset(kmax2_sb[:], 0.0))
                        v.wait_ge(sDV.sem, sDV.v)
                    v.wait_ge(sMM.sem, M[f"kn_mm_{kh}_{tg}"])
                    sDV.inc(v.reduce_max(out=tmax_sb[0:1, 0:1], in_=PQ[0:1, 0:512], axis=AX.X))
                    v.wait_ge(sDV.sem, sDV.v)
                    sDV.inc(v.tensor_tensor(out=kmax2_sb[0:1, kh:kh + 1], in0=kmax2_sb[0:1, kh:kh + 1],
                                            in1=tmax_sb[0:1, 0:1], op=ALU.max))
                    M[f"kn_red_{kh}_{tg}"] = sDV.v
            M[f"k_mm_{kh}"] = M[f"k_mm_{kh}_2"]
        M["k_done"] = sMM.v
        M["kn_done"] = sDV.v

        if stop == "k":
            @blk.sync
            def _(s):
                s.wait_ge(sST.sem, sST.v)
            raise _StopBuild()
        for och in range(8):
            @blk.gpsimd
            def _(g, och=och):
                if och >= 2:
                    g.wait_ge(sMM.sem, M[f"v_mmoc_{och-2}"])
                else:
                    g.wait_ge(sMM.sem, M["k_done"])
                sLD.inc(g.dma_start(out=vwbuf[och % 2][:].rearrange("p (hc o) -> p hc o", hc=HC),
                                    in_=wvT.rearrange("(hc p) o -> p hc o", p=128)[:, :,
                                                                                   och * 128:(och + 1) * 128]), 16)
                M[f"v_w_{och}"] = sLD.v

            dv_base_v = sDV.v
            for tb in range(NT):
                @blk.tensor
                def _(t, och=och, tb=tb):
                    if tb == 0:
                        t.wait_ge(sLD.sem, M[f"v_w_{och}"])
                    ps = PF[tb % 2]
                    if tb >= 2:
                        t.wait_ge(sDV.sem, fut(f"v_ev_{och}_{tb-2}", dv_base_v + tb - 1))
                    elif och >= 1:
                        t.wait_ge(sDV.sem, M[f"v_ev_{och-1}_{NT-2+tb}"])
                    else:
                        t.wait_ge(sDV.sem, M[f"k_ev_{NKV-1}_{1+tb}"])
                    for hcx in range(HC):
                        last = t.matmul(ps[:, 0:128], A_block(hcx, tb),
                                        vwbuf[och % 2][:, hcx * 128:(hcx + 1) * 128],
                                        start=(hcx == 0), stop=(hcx == HC - 1))
                    sMM.inc(last)
                    M[f"v_mm_{och}_{tb}"] = sMM.v

                @blk.vector
                def _(v, och=och, tb=tb):
                    v.wait_ge(sMM.sem, M[f"v_mm_{och}_{tb}"])
                    ps = PF[tb % 2]
                    sDV.inc(v.tensor_copy(out=V_sb[:, tb * DVA + och * 128: tb * DVA + (och + 1) * 128],
                                          in_=ps[:, 0:128]))
                    M[f"v_ev_{och}_{tb}"] = sDV.v
            M[f"v_mmoc_{och}"] = sMM.v
        M["v_done"] = sMM.v
        M["v_done_dv"] = sDV.v

        if stop == "v":
            @blk.sync
            def _(s):
                s.wait_ge(sST.sem, sST.v)
            raise _StopBuild()
        def q_norm_tail(dblk):
            khq = dblk // 4
            for half in range(2):
                @blk.tensor
                def _(t, dblk=dblk, half=half):
                    t.wait_ge(sDV.sem, M[f"q_sq_{dblk}_{half}"])
                    if dblk >= 1:
                        t.wait_ge(sDV.sem, M[f"qn_ev_{dblk-1}_{half}"])
                    last = t.matmul(PSs[half][:], ones_col[:],
                                    qsq4[(dblk % 2) * 2 + half][:], start=True, stop=True)
                    sMM.inc(last)
                    M[f"qn_mm_{dblk}_{half}"] = sMM.v

                @blk.vector
                def _(v, dblk=dblk, half=half, khq=khq):
                    v.wait_ge(sMM.sem, M[f"qn_mm_{dblk}_{half}"])
                    if dblk >= 3 and half == 0:
                        v.wait_ge(sST.sem, M[f"q_st_{dblk-2}"])
                    sDV.inc(v.tensor_scalar(out=qmrow[dblk % 2][0:1, half * 512:(half + 1) * 512],
                                            in0=PSs[half][:], scalar1=kmax2_sb[0:1, khq:khq + 1],
                                            scalar2=None, op0=ALU.add))
                    M[f"qn_ev_{dblk}_{half}"] = sDV.v

            @blk.sync
            def _(s, dblk=dblk):
                s.wait_ge(sDV.sem, M[f"q_sq_{dblk}_1"])
                sST.inc(s.dma_start(out=qscr[dblk * 128:(dblk + 1) * 128, :], in_=qstage[dblk % 2][:]), 16)
                s.wait_ge(sDV.sem, M[f"qn_ev_{dblk}_1"])
                gq = dblk % 4
                khx = dblk // 4
                dst = qmd.rearrange("a (qb kh g t) -> a qb kh g t", qb=8, kh=8, g=4)[:, :, khx, gq, :]
                sST.inc(s.dma_start(out=dst, in_=qmrow[dblk % 2][:].rearrange("a (qb t) -> a qb t", qb=8)), 16)
                M[f"q_st_{dblk}"] = sST.v

        for dblk in range(NQH):
            khq = dblk // 4

            @blk.gpsimd
            def _(g, dblk=dblk):
                if dblk >= 2:
                    g.wait_ge(sMM.sem, M[f"q_mm_{dblk-2}_1"])
                else:
                    g.wait_ge(sMM.sem, M["v_done"])
                sLD.inc(g.dma_start(out=qwbuf[dblk % 2][:].rearrange("p (hc o) -> p hc o", hc=HC),
                                    in_=wqT.rearrange("(hc p) o -> p hc o", p=128)[:, :,
                                                                                   dblk * 128:(dblk + 1) * 128]), 16)
                M[f"q_w_{dblk}"] = sLD.v

            for half in range(2):
                @blk.tensor
                def _(t, dblk=dblk, half=half):
                    if half == 0:
                        t.wait_ge(sLD.sem, M[f"q_w_{dblk}"])
                    if dblk >= 1:
                        t.wait_ge(sDV.sem, M[f"q_ev_{dblk-1}_{half}"])
                    elif half == 0:
                        t.wait_ge(sDV.sem, M["kn_done"])
                    for hcx in range(HC):
                        last = t.matmul(PQ[:, half * 512:(half + 1) * 512],
                                        qwbuf[dblk % 2][:, hcx * 128:(hcx + 1) * 128],
                                        A_own[:, hcx * TOWN + half * 512: hcx * TOWN + (half + 1) * 512],
                                        start=(hcx == 0), stop=(hcx == HC - 1))
                    sMM.inc(last)
                    M[f"q_mm_{dblk}_{half}"] = sMM.v

                @blk.vector
                def _(v, dblk=dblk, half=half):
                    v.wait_ge(sMM.sem, M[f"q_mm_{dblk}_{half}"])
                    if dblk >= 2 and half == 0:
                        v.wait_ge(sST.sem, M[f"q_st_{dblk-2}"])
                    sDV.inc(v.tensor_scalar_mul(out=qstage[dblk % 2][:, half * 512:(half + 1) * 512],
                                                in0=PQ[:, half * 512:(half + 1) * 512], scalar1=QSCALE))
                    M[f"q_ev_{dblk}_{half}"] = sDV.v
                    if dblk >= 2:
                        v.wait_ge(sMM.sem, M[f"qn_mm_{dblk-2}_{half}"])
                    v.wait_ge(sDV.sem, sDV.v)
                    qs = qstage[dblk % 2][:, half * 512:(half + 1) * 512]
                    sDV.inc(v.scalar_tensor_tensor(out=qsq4[(dblk % 2) * 2 + half][:], in0=qs, scalar=1.0,
                                                   in1=qs, op0=ALU.mult, op1=ALU.mult))
                    M[f"q_sq_{dblk}_{half}"] = sDV.v

            if dblk >= 1:
                q_norm_tail(dblk - 1)
        q_norm_tail(NQH - 1)
        M["q_done_mm"] = sMM.v
        M["q_done_st"] = sST.v
        M["q_done_dv"] = sDV.v

        if stop == "q":
            @blk.sync
            def _(s):
                s.wait_ge(sST.sem, sST.v)
            raise _StopBuild()
        @blk.gpsimd
        def _(g):
            g.wait_ge(sMM.sem, M["q_done_mm"])
            sLD.inc(g.dma_start(out=emrow_sb[:], in_=emrep), 16)
            M["em_ld"] = sLD.v

        EMI = {}
        _emi = 0
        for _qb in range(4):
            for _kb in range(4 - _qb):
                EMI[(_qb, _kb)] = _emi
                _emi += 1

        for qb in range(8):
            @blk.sync
            def _(s, qb=qb):
                if qb == 0:
                    s.wait_ge(sST.sem, M["q_done_st"])
                    s.wait_ge(sMM.sem, M["q_done_mm"])
                if qb >= 2:
                    s.wait_ge(sMM.sem, M[f"at_mmq_{qb-2}"])
                src = qscr.rearrange("(h d) t -> d h t", d=128)[:, :, qb * 128:(qb + 1) * 128]
                dst = qtile[qb % 2][:].rearrange("d (h t) -> d h t", h=NQH)
                for part in range(4):
                    sX.inc(s.dma_start(out=dst[:, part * 8:(part + 1) * 8, :],
                                       in_=src[:, part * 8:(part + 1) * 8, :]), 16)
                sX.inc(s.dma_start(out=qmrow2[qb % 2][:], in_=qmd[:, qb * 4096:(qb + 1) * 4096]), 16)
                M[f"at_q_{qb}"] = sX.v

            se_base_qb = sSE.v
            dv_base_qb = sDV.v
            for kh in range(NKV):
                it = qb * NKV + kh
                par = it % 2

                @blk.tensor
                def _(t, qb=qb, kh=kh, it=it, par=par):
                    if kh == 0:
                        t.wait_ge(sX.sem, M[f"at_q_{qb}"])
                    if it == 0:
                        t.wait_ge(sLD.sem, M["em_ld"])
                        t.wait_ge(sDV.sem, M["v_done_dv"])
                    for kb in range(5):
                        sp = PF[kb % 2]
                        if kb >= 2:
                            t.wait_ge(sSE.sem, fut(f"at_exp_{it}_{kb-2}", se_base_qb + kh * 5 + kb - 1))
                        elif it >= 1:
                            prev_kb = 4 if kb == 0 else 3
                            if kh >= 1:
                                t.wait_ge(sSE.sem, fut(f"at_exp_{it-1}_{prev_kb}",
                                                       se_base_qb + (kh - 1) * 5 + prev_kb + 1))
                            else:
                                t.wait_ge(sSE.sem, M[f"at_exp_{it-1}_{prev_kb}"])
                        kcol = kh * TTOT + qb * 128 + kb * 128
                        t.matmul(sp[:], K_sb[:, kcol:kcol + 128],
                                 qtile[qb % 2][:, kh * 512:(kh + 1) * 512], start=True, stop=False)
                        masks = []
                        if kb == 0:
                            masks.append(m0_sb[:])
                        if kb == 4:
                            masks.append(m4_sb[:])
                        emx = (qb, kb) in EMI
                        adders = [(halfneg1[:], qmrow2[qb % 2][0:1, kh * 512:(kh + 1) * 512])]
                        if emx:
                            adders.append((ones_row[:], emrow_sb[0:1, :]))
                        adders += [(ident[:], mk) for mk in masks]
                        for ai, (lh, rh) in enumerate(adders):
                            last = t.matmul(sp[:], lh, rh, start=False, stop=(ai == len(adders) - 1))
                        sMM.inc(last)
                        M[f"at_s_{it}_{kb}"] = sMM.v
                    if kh == NKV - 1:
                        M[f"at_mmq_{qb}"] = sMM.v

                @blk.tensor
                def _(t, qb=qb, kh=kh, it=it, par=par):
                    for kb in range(5):
                        t.wait_ge(sSE.sem, fut(f"at_exp_{it}_{kb}", se_base_qb + kh * 5 + kb + 1))
                        if kb == 0:
                            if it >= 2:
                                if kh >= 2:
                                    t.wait_ge(sDV.sem, fut(f"at_otev_{it-2}", dv_base_qb + (kh - 2) * 2 + 2))
                                    t.wait_ge(sDV.sem, fut(f"at_rs_{it-2}", dv_base_qb + (kh - 2) * 2 + 1))
                                else:
                                    t.wait_ge(sDV.sem, M[f"at_otev_{it-2}"])
                                    t.wait_ge(sDV.sem, M[f"at_rs_{it-2}"])
                            else:
                                t.wait_ge(sDV.sem, M["q_done_dv"])
                        vcol = (qb + kb) * DVA + kh * 128
                        t.matmul(POT[par], V_sb[:, vcol:vcol + 128],
                                 E_sb[:, kb * 512:(kb + 1) * 512], start=(kb == 0), stop=(kb == 4))
                        last = t.matmul(PSs[par][:], ones_col[:],
                                        E_sb[:, kb * 512:(kb + 1) * 512], start=(kb == 0), stop=(kb == 4))
                    sMM.inc(last)
                    M[f"at_pv_{it}"] = sMM.v

            @blk.scalar
            def _(sc, qb=qb):
                for kh in range(NKV):
                    it = qb * NKV + kh
                    for kb in range(5):
                        sc.wait_ge(sMM.sem, M[f"at_s_{it}_{kb}"])
                        if kb == 0 and it >= 1:
                            sc.wait_ge(sMM.sem, M[f"at_pv_{it-1}"])
                        sp = PF[kb % 2]
                        sSE.inc(sc.activation(E_sb[:, kb * 512:(kb + 1) * 512], sp[:], AF.Exp))
                        M[f"at_exp_{it}_{kb}"] = sSE.v

            for kh in range(NKV):
                it = qb * NKV + kh
                par = it % 2

                @blk.vector
                def _(v, qb=qb, kh=kh, it=it, par=par):
                    v.wait_ge(sMM.sem, M[f"at_pv_{it}"])
                    if it >= 2:
                        v.wait_ge(sST.sem, M[f"at_rsd_{it-2}"])
                    sDV.inc(v.reciprocal(rs_pp[it % 2][:], PSs[par][:]))
                    M[f"at_rs_{it}"] = sDV.v
                    dst = OT_sb[:].rearrange("p (h t) -> p h t", h=NQH)[:, kh * 4:kh * 4 + 4,
                                                                       qb * 128:(qb + 1) * 128]
                    sDV.inc(v.tensor_copy(out=dst, in_=POT[par].rearrange("p (g t) -> p g t", g=4)))
                    M[f"at_otev_{it}"] = sDV.v

                @blk.sync
                def _(s, it=it):
                    s.wait_ge(sDV.sem, M[f"at_rs_{it}"])
                    sST.inc(s.dma_start(out=rsd[it:it + 1, :], in_=rs_pp[it % 2][:]), 16)
                    M[f"at_rsd_{it}"] = sST.v
        M["attn_mainloop_mm"] = sMM.v

        mm_base_bn = sMM.v
        dv_base_bn = sDV.v
        se_base_bn = sSE.v
        for it in range(64):
            @blk.sync
            def _(s, it=it):
                if it == 0:
                    s.wait_ge(sST.sem, M[f"at_rsd_{63}"])
                if it >= 2:
                    s.wait_ge(sMM.sem, M[f"at_bnmm_{it-2}"])
                sX.inc(s.dma_start(out=rsrow[it % 2][:], in_=rsd[it:it + 1, :]), 16)
                M[f"at_bnx_{it}"] = sX.v

            @blk.tensor
            def _(t, it=it):
                t.wait_ge(sX.sem, M[f"at_bnx_{it}"])
                if it >= 2:
                    t.wait_ge(sDV.sem, fut(f"at_bntt_{it-2}", dv_base_bn + it - 1))
                else:
                    t.wait_ge(sSE.sem, M[f"at_exp_{63}_4"])
                    t.wait_ge(sDV.sem, M[f"at_otev_{63}"])
                last = t.matmul(PF[it % 2][:], onesf[:], rsrow[it % 2][:], start=True, stop=True)
                sMM.inc(last)
                M[f"at_bnmm_{it}"] = sMM.v

        @blk.scalar
        def _(sc):
            for it in range(64):
                sc.wait_ge(sMM.sem, M[f"at_bnmm_{it}"])
                if it >= 2:
                    sc.wait_ge(sDV.sem, fut(f"at_bntt_{it-2}", dv_base_bn + it - 1))
                dstc = rsb_sb[:] if it % 2 == 0 else rsbc2[:]
                sSE.inc(sc.activation(dstc, PF[it % 2][:], AF.Identity))
                M[f"at_bnse_{it}"] = sSE.v

        for it in range(64):
            qb, kh = divmod(it, NKV)

            @blk.vector
            def _(v, it=it, qb=qb, kh=kh):
                v.wait_ge(sSE.sem, M[f"at_bnse_{it}"])
                srcc = rsb_sb[:] if it % 2 == 0 else rsbc2[:]
                dst = OT_sb[:].rearrange("p (h t) -> p h t", h=NQH)[:, kh * 4:kh * 4 + 4,
                                                                   qb * 128:(qb + 1) * 128]
                sDV.inc(v.tensor_tensor(out=dst, in0=dst,
                                        in1=srcc.rearrange("p (g t) -> p g t", g=4), op=ALU.mult))
                M[f"at_bntt_{it}"] = sDV.v
        M["attn_done_dv"] = sDV.v
        M["attn_done_mm"] = sMM.v

        if stop == "at":
            @blk.sync
            def _(s):
                s.wait_ge(sST.sem, sST.v)
            raise _StopBuild()
        for oc in range(8):
            @blk.gpsimd
            def _(g, oc=oc):
                if oc >= 2:
                    g.wait_ge(sMM.sem, M[f"op_mmoc_{oc-2}"])
                else:
                    g.wait_ge(sMM.sem, M["attn_done_mm"])
                    g.wait_ge(sSE.sem, M[f"at_exp_{63}_4"])
                sLD.inc(g.dma_start(out=wbuf[oc % 2][:].rearrange("p (hc o) -> p hc o", hc=HC),
                                    in_=woT.rearrange("(hc p) o -> p hc o", p=128)[:, :,
                                                                                   oc * 512:(oc + 1) * 512]), 16)
                M[f"op_w_{oc}"] = sLD.v

            dv_base = sDV.v
            for tb in range(NTO):
                @blk.sync
                def _(s, oc=oc, tb=tb):
                    if tb >= 2:
                        s.wait_ge(sDV.sem, fut(f"op_ev_{oc}_{tb-2}", dv_base + tb - 1))
                    elif oc >= 1:
                        s.wait_ge(sDV.sem, M[f"op_ev_{oc-1}_{tb}"])
                    sX.inc(s.dma_start(out=xold[tb % 2][:],
                                       in_=x1[THALO + tb * 128:THALO + (tb + 1) * 128,
                                              oc * 512:(oc + 1) * 512]), 16)
                    M[f"op_xo_{oc}_{tb}"] = sX.v

                @blk.tensor
                def _(t, oc=oc, tb=tb):
                    if tb == 0:
                        t.wait_ge(sLD.sem, M[f"op_w_{oc}"])
                        if oc == 0:
                            t.wait_ge(sDV.sem, M["attn_done_dv"])
                    ps = PF[tb % 2]
                    if tb >= 2:
                        t.wait_ge(sDV.sem, fut(f"op_ev_{oc}_{tb-2}", dv_base + tb - 1))
                    elif oc >= 1:
                        t.wait_ge(sDV.sem, M[f"op_ev_{oc-1}_{NTO-2+tb}"])
                    for dc in range(HC):
                        last = t.matmul(ps[:], OT_sb[:, dc * TOWN + tb * 128: dc * TOWN + (tb + 1) * 128],
                                        wbuf[oc % 2][:, dc * 512:(dc + 1) * 512],
                                        start=(dc == 0), stop=(dc == HC - 1))
                    sMM.inc(last)
                    M[f"op_mm_{oc}_{tb}"] = sMM.v

                @blk.vector
                def _(v, oc=oc, tb=tb):
                    v.wait_ge(sMM.sem, M[f"op_mm_{oc}_{tb}"])
                    v.wait_ge(sX.sem, M[f"op_xo_{oc}_{tb}"])
                    if oc >= 1:
                        v.wait_ge(sST.sem, M[f"op_st_{oc-1}_{tb}"])
                    ps = PF[tb % 2]
                    sDV.inc(v.tensor_add(out=xnew[tb % 2][:], in0=ps[:], in1=xold[tb % 2][:]))
                    M[f"op_ev_{oc}_{tb}"] = sDV.v

                @blk.sync
                def _(s, oc=oc, tb=tb):
                    s.wait_ge(sDV.sem, M[f"op_ev_{oc}_{tb}"])
                    sST.inc(s.dma_start(out=x2[tb * 128:(tb + 1) * 128, oc * 512:(oc + 1) * 512],
                                        in_=xnew[tb % 2][:]), 16)
                    M[f"op_st_{oc}_{tb}"] = sST.v
                    x2_store[(tb, oc)] = sST.v
            M[f"op_mmoc_{oc}"] = sMM.v
        M["op_done_mm"] = sMM.v

        if stop == "op":
            @blk.sync
            def _(s):
                s.wait_ge(sST.sem, sST.v)
            raise _StopBuild()
        ln_stage(3, NTO, x2, lambda i: x2_store[(i, 7)], M["op_done_mm"])
        gating(3, NTO, A3_span)
        ffn_main(3, NTO, x2, out_t, lambda tb, oc: x2_store[(tb, oc)],
                 A3_block, None,
                 [(sMM.sem, M["ln3_tp_last"]),
                  (sDV.sem, M["ln3_done"])])

        @blk.sync
        def _(s):
            s.wait_ge(sST.sem, sST.v)

    except _StopBuild:
        pass
    for key, val in futures:
        if key in M:
            assert M[key] == val, (key, M.get(key), val)

    ctx.close()
    return nc


def make_inputs_per_core(inp):
    x = np.asarray(inp["L0_attn_x"])[0]
    wq = np.asarray(inp["L0_attn_wq"])
    wk = np.asarray(inp["L0_attn_wk"])
    wv = np.asarray(inp["L0_attn_wv"])
    wo = np.asarray(inp["L0_attn_wo"])
    wup = np.asarray(inp["L0_ffn_W_up"])
    gate = np.asarray(inp["L0_ffn_gate_w"])
    eb = np.asarray(inp["L0_ffn_expert_bias"])
    lA = np.asarray(inp["L0_ffn_lora_A"])
    lB = np.asarray(inp["L0_ffn_lora_B"])

    f32 = np.float32
    bf16 = ml_dtypes.bfloat16
    wqT = np.ascontiguousarray(wq.T.astype(f32))
    wkT = np.ascontiguousarray(wk.T.astype(f32))
    wvT = np.ascontiguousarray(wv.T.astype(f32))
    woT = np.ascontiguousarray(wo.T.astype(f32))
    wupT = np.ascontiguousarray(wup.T.astype(f32))
    A_flat = lA.reshape(32, D)
    sw = np.concatenate([gate.T.astype(f32), A_flat.T.astype(f32)], axis=1)
    swr = np.ascontiguousarray(sw.reshape(HC, 128, 36).transpose(1, 0, 2).reshape(128, HC * 36))
    bflat_h = np.ascontiguousarray(lB.transpose(0, 2, 1).reshape(32, D).astype(f32))
    ebias_h = np.ascontiguousarray(np.tile(eb.astype(f32)[None, :], (128, 12)))

    ko = np.arange(128)[:, None]
    qi = np.arange(128)[None, :]
    m0 = np.where(ko > qi, 0.0, -1e30).astype(f32)
    m4 = np.where(ko <= qi, 0.0, -1e30).astype(f32)
    m0rep_h = np.ascontiguousarray(np.tile(m0, (1, 4))).astype(bf16)
    m4rep_h = np.ascontiguousarray(np.tile(m4, (1, 4))).astype(bf16)
    emfull = np.full((1, 512), np.float32(-1e30)).astype(bf16)
    emzero = np.zeros((1, 512), dtype=bf16)
    ident_h = np.eye(128, dtype=bf16)

    in_maps = []
    for c in range(8):
        if c == 0:
            x_c = np.concatenate([x[0:512], x[0:1024]], axis=0)
            em = emfull
        else:
            s0 = 1024 * c - 512
            x_c = x[s0:s0 + TTOT]
            em = emzero
        in_maps.append(dict(
            x_in=np.ascontiguousarray(x_c.astype(f32)),
            wqT=wqT, wkT=wkT, wvT=wvT, woT=woT, wupT=wupT,
            swr=swr, bflat=bflat_h, ebias=ebias_h,
            m0rep=m0rep_h, m4rep=m4rep_h, emrep=em,
            identin=ident_h,
        ))
    return in_maps


def kernel(**inputs):
    in_maps = make_inputs_per_core(inputs)
    nc = build()
    res = run_bass_kernel_spmd(nc, in_maps, core_ids=list(range(8)))
    outs = [np.asarray(r["out_t"]) for r in res.results]
    return np.concatenate(outs, axis=0).reshape(1, 8192, D)


# version 04 (final)
# speedup vs baseline: 1.2831x; 1.0924x over previous
import numpy as np
import ml_dtypes
import concourse.bass as bass
import concourse.mybir as mybir
from concourse.bass_utils import run_bass_kernel_spmd

F32 = mybir.dt.float32
BF16 = mybir.dt.bfloat16
AF = mybir.ActivationFunctionType
ALU = mybir.AluOpType
AX = mybir.AxisListType

D = 4096
HC = 32
TOWN = 1024
THALO = 512
TTOT = 1536
NT = 12
NTO = 8
NQH = 32
NKV = 8
HD = 128
DVA = NKV * HD
QSCALE = float(128.0 ** -0.25)
KB = 1024


class _StopBuild(Exception):
    pass


class Ctr:
    def __init__(self, sem):
        self.sem = sem
        self.v = 0

    def inc(self, ins, n=1):
        ins.then_inc(self.sem, n)
        self.v += n
        return self.v


def build(stop=None, dbg=False):
    nc = bass.Bass("TRN2", target_bir_lowering=False)

    x_in = nc.dram_tensor("x_in", [TTOT, D], F32, kind="ExternalInput").ap()
    wqT = nc.dram_tensor("wqT", [D, D], F32, kind="ExternalInput").ap()
    wkT = nc.dram_tensor("wkT", [D, NKV * HD], F32, kind="ExternalInput").ap()
    wvT = nc.dram_tensor("wvT", [D, NKV * HD], F32, kind="ExternalInput").ap()
    woT = nc.dram_tensor("woT", [D, D], F32, kind="ExternalInput").ap()
    wupT = nc.dram_tensor("wupT", [D, D], F32, kind="ExternalInput").ap()
    swr = nc.dram_tensor("swr", [128, HC * 36], F32, kind="ExternalInput").ap()
    bflat = nc.dram_tensor("bflat", [32, D], F32, kind="ExternalInput").ap()
    ebias = nc.dram_tensor("ebias", [128, 48], F32, kind="ExternalInput").ap()
    m0rep = nc.dram_tensor("m0rep", [128, 512], BF16, kind="ExternalInput").ap()
    m4rep = nc.dram_tensor("m4rep", [128, 512], BF16, kind="ExternalInput").ap()
    emrep = nc.dram_tensor("emrep", [128, 512], BF16, kind="ExternalInput").ap()
    identin = nc.dram_tensor("identin", [128, 128], BF16, kind="ExternalInput").ap()

    xkind = dict(kind="ExternalOutput") if dbg else {}
    x1 = nc.dram_tensor("x1", [TTOT, D], F32, **xkind).ap()
    x2 = nc.dram_tensor("x2", [TOWN, D], F32, **xkind).ap()
    qscr = nc.dram_tensor("qscr", [D, TOWN], BF16).ap()
    qmd = nc.dram_tensor("qmd", [1, NQH * TOWN], BF16).ap()
    rsd = nc.dram_tensor("rsd", [64, 512], F32).ap()
    out_t = nc.dram_tensor("out_t", [TOWN, D], F32, kind="ExternalOutput").ap()

    arena = nc.alloc_sbuf_tensor("arena", [128, 206 * KB + 512], mybir.dt.uint8)
    B0 = nc.lookup_mloc(arena).addr

    def sbt(name, shape, dtype, off):
        return nc.alloc_sbuf_tensor_at(name, shape, dtype, offset=B0 + off)

    ident = sbt("ident", [128, 128], BF16, 0)
    ones_col = sbt("ones_col", [128, 1], BF16, 256)
    halfneg1 = sbt("halfneg1", [1, 128], BF16, 288)
    onesf = sbt("onesf", [1, 128], F32, 544)
    sw_sb = sbt("sw_sb", [128, HC * 36], BF16, 1056)
    ebias_sb = sbt("ebias_sb", [128, 48], F32, 3392)
    m0_sb = sbt("m0_sb", [128, 512], BF16, 3584)
    m4_sb = sbt("m4_sb", [128, 512], BF16, 4608)
    lg_sb = sbt("lg_sb", [128, 48], F32, 5632)
    cmp_sb = sbt("cmp_sb", [128, 144], F32, 5824)
    oh_sb = sbt("oh_sb", [128, 48], F32, 6400)
    kmax2_sb = sbt("kmax2_sb", [1, 16], F32, 6592)
    tmax_sb = sbt("tmax_sb", [1, 16], F32, 6656)
    ones_row = sbt("ones_row", [1, 128], BF16, 6720)
    ones128 = sbt("ones128", [128, 128], BF16, 7040)
    st_s = [sbt(f"st_s{i}", [128, 1], F32, 8768 + i * 32) for i in range(2)]
    st_ss = [sbt(f"st_ss{i}", [128, 1], F32, 8832 + i * 32) for i in range(2)]
    st_mean = [sbt(f"st_mean{i}", [128, 1], F32, 8896 + i * 32) for i in range(2)]
    st_v2 = [sbt(f"st_v2{i}", [128, 1], F32, 8960 + i * 32) for i in range(2)]
    st_sd = [sbt(f"st_sd{i}", [128, 1], F32, 9024 + i * 32) for i in range(2)]
    st_r = [sbt(f"st_r{i}", [128, 1], F32, 9088 + i * 32) for i in range(2)]
    st_mr = [sbt(f"st_mr{i}", [128, 1], F32, 9152 + i * 32) for i in range(2)]
    xold = [sbt(f"xold{i}", [128, 512], F32, 9216 + i * 2048) for i in range(2)]
    xnew = [sbt(f"xnew{i}", [128, 512], F32, 13312 + i * 2048) for i in range(2)]
    GOFF = 18 * KB
    bflat_sb = sbt("bflat_sb", [32, D], BF16, GOFF)
    zT_sb = sbt("zT_sb", [32, NT * 128], BF16, GOFF + 8 * KB)
    zm_tm = sbt("zm_tm", [128, NT * 32], BF16, GOFF + 11 * KB)
    zmm_tm = sbt("zmm_tm", [128, NT * 32], BF16, GOFF + 12 * KB)
    zmT_sb = sbt("zmT_sb", [32, NT * 128], BF16, GOFF + 13 * KB)
    gfm_sb = sbt("gfm_sb", [4, TTOT], BF16, GOFF + 16 * KB)
    rs_pp2 = [sbt(f"rs_pp2_{i}", [128, 512], F32, GOFF + 8 * KB + i * 2 * KB) for i in range(2)]
    rssc = sbt("rssc", [128, 512], F32, GOFF + 12 * KB)
    qsq4 = [sbt(f"qsq4_{i}", [128, 512], BF16, GOFF + 18 * KB + i * KB) for i in range(4)]
    WOFF = 40 * KB
    wbuf = [sbt(f"wbuf{i}", [128, HC * 512], BF16, WOFF + i * 32 * KB) for i in range(2)]
    xt = [sbt(f"xt{i}", [128, D], F32, WOFF + i * 16 * KB) for i in range(2)]
    At = [sbt(f"At{i}", [128, D], BF16, WOFF + 32 * KB + i * 8 * KB) for i in range(2)]
    kwbuf = [sbt(f"kwbuf{i}", [128, HC * 128], BF16, WOFF + i * 8 * KB) for i in range(2)]
    vwbuf = [sbt(f"vwbuf{i}", [128, HC * 128], BF16, WOFF + i * 8 * KB) for i in range(2)]
    qwbuf = [sbt(f"qwbuf{i}", [128, HC * 128], BF16, WOFF + i * 8 * KB) for i in range(2)]
    K_sb = sbt("K_sb", [128, NKV * TTOT], BF16, WOFF + 16 * KB)
    V_sb = sbt("V_sb", [128, NT * DVA], BF16, WOFF + 40 * KB)
    E_sb = sbt("E_sb", [128, 5 * 512], BF16, WOFF)
    emrow_sb = sbt("emrow_sb", [128, 512], BF16, WOFF + 5 * KB)
    mb_sb = [sbt(f"mb_sb{i}", [128, 512], BF16, WOFF + 6 * KB + i * KB) for i in range(2)]
    R1 = 104 * KB
    A_own = sbt("A_own", [128, HC * TOWN], BF16, R1)
    OT_sb = sbt("OT_sb", [128, NQH * TOWN], BF16, R1)
    R2 = 168 * KB
    A_halo = sbt("A_halo", [128, HC * THALO], BF16, R2)
    qtile = [sbt(f"qtile{i}", [128, NQH * 128], BF16, R2 + i * 8 * KB) for i in range(2)]
    qmrow2 = [sbt(f"qmrow2_{i}", [1, NQH * 128], BF16, R2 + 16 * KB + i * 8 * KB) for i in range(2)]
    FOFF = 200 * KB
    qstage = [sbt(f"qstage{i}", [128, 1024], BF16, FOFF + i * 2 * KB) for i in range(2)]
    qsq_sb = sbt("qsq_sb", [128, 512], BF16, FOFF + 4 * KB)
    ksq_sb = [sbt(f"ksq{i}", [128, 512], BF16, FOFF + 4 * KB + i * KB) for i in range(2)]
    qmrow = [sbt(f"qmrow{i}", [1, 1024], BF16, R2 + i * 2 * KB) for i in range(2)]
    rsb_sb = sbt("rsb_sb", [128, 512], F32, FOFF + 4 * KB)

    P0 = nc.alloc_psum_tensor("P0", [128, 512], F32)
    P1 = nc.alloc_psum_tensor("P1", [128, 512], F32)
    PQ = nc.alloc_psum_tensor("PQ", [128, 1024], F32)
    PSs = [nc.alloc_psum_tensor(f"PSs{i}", [128, 512], F32) for i in range(2)]
    PR = nc.alloc_psum_tensor("PR", [128, 512], F32)
    PT = nc.alloc_psum_tensor("PT", [128, 1024], BF16)
    PF = [P0, P1]
    PRb = PR.ap().bitcast(BF16)
    PTf = PT.ap().bitcast(F32)[:, 0:512]
    POT = [PQ.ap()[:, 0:512], PQ.ap()[:, 512:1024]]

    from contextlib import ExitStack
    ctx = ExitStack()
    sLD = Ctr(ctx.enter_context(nc.semaphore("sLD")))
    sX = Ctr(ctx.enter_context(nc.semaphore("sX")))
    sMM = Ctr(ctx.enter_context(nc.semaphore("sMM")))
    sDV = Ctr(ctx.enter_context(nc.semaphore("sDV")))
    sSE = Ctr(ctx.enter_context(nc.semaphore("sSE")))
    sST = Ctr(ctx.enter_context(nc.semaphore("sST")))

    M = {}
    futures = []

    def fut(key, val):
        futures.append((key, val))
        return val

    def A_block(hc, tile):
        if tile < 4:
            return A_halo[:, hc * THALO + tile * 128: hc * THALO + (tile + 1) * 128]
        return A_own[:, hc * TOWN + (tile - 4) * 128: hc * TOWN + (tile - 3) * 128]

    def A_span(hc, tg):
        if tg == 0:
            return A_halo[:, hc * THALO: (hc + 1) * THALO]
        return A_own[:, hc * TOWN + (tg - 1) * 512: hc * TOWN + tg * 512]

    def A_evac_dst(sub, gidx, tile):
        if sub != 3 and tile < 4:
            return A_halo[:].rearrange("p (hc t) -> p hc t", hc=HC)[:, gidx * 4:gidx * 4 + 4,
                                                                   tile * 128:(tile + 1) * 128]
        t2 = tile - 4 if sub != 3 else tile
        return A_own[:].rearrange("p (hc t) -> p hc t", hc=HC)[:, gidx * 4:gidx * 4 + 4,
                                                               t2 * 128:(t2 + 1) * 128]

    def A3_block(hc, tile):
        return A_own[:, hc * TOWN + tile * 128: hc * TOWN + (tile + 1) * 128]

    def A3_span(hc, tg):
        return A_own[:, hc * TOWN + tg * 512: hc * TOWN + (tg + 1) * 512]

    def K_slice(kh, tg):
        return K_sb[:, kh * TTOT + tg * 512: kh * TTOT + (tg + 1) * 512]

    x1_store = {}
    x2_store = {}

    try:
      with nc.Block() as blk:
        @blk.gpsimd
        def _(g):
            sLD.inc(g.dma_start(out=sw_sb[:], in_=swr), 16)
            sLD.inc(g.dma_start(out=bflat_sb[:], in_=bflat), 16)
            sLD.inc(g.dma_start(out=ebias_sb[:], in_=ebias), 16)
            sLD.inc(g.dma_start(out=m0_sb[:], in_=m0rep), 16)
            sLD.inc(g.dma_start(out=m4_sb[:], in_=m4rep), 16)
            sLD.inc(g.dma_start(out=ident[:], in_=identin), 16)
        M["setup_ld"] = sLD.v

        @blk.vector
        def _(v):
            v.memset(ones_col[:], 1.0)
            v.memset(halfneg1[:], -0.5)
            v.memset(ones_row[:], 1.0)
            v.memset(ones128[:], 1.0)
            sDV.inc(v.memset(onesf[:], 1.0))
        M["setup_dv"] = sDV.v

        def ln_stage(sub, n_tiles, in_stream, store_wait, region_wait_mm):
            for i in range(n_tiles):
                if stop == f"ln{sub}t{i}":
                    @blk.sync
                    def _(s):
                        s.wait_ge(sST.sem, sST.v)
                    raise _StopBuild()
                b = i % 2

                @blk.sync
                def _(s, i=i, b=b):
                    if i >= 2:
                        s.wait_ge(sDV.sem, M[f"ln{sub}_ap_{i-2}"])
                    elif region_wait_mm is not None:
                        s.wait_ge(sMM.sem, region_wait_mm)
                    w = store_wait(i)
                    if w is not None:
                        s.wait_ge(sST.sem, w)
                    sX.inc(s.dma_start(out=xt[b][:], in_=in_stream[i * 128:(i + 1) * 128, :]), 16)
                    M[f"ln{sub}_x_{i}"] = sX.v

                @blk.vector
                def _(v, i=i, b=b):
                    v.wait_ge(sX.sem, M[f"ln{sub}_x_{i}"])
                    if i >= 2:
                        v.wait_ge(sMM.sem, M[f"ln{sub}_tp_{i-2}"])
                    elif region_wait_mm is not None:
                        v.wait_ge(sMM.sem, region_wait_mm)
                    sDV.inc(v.reduce_sum(out=st_s[b][:], in_=xt[b][:], axis=AX.X))
                    sDV.inc(v.scalar_tensor_tensor(out=At[b][:], in0=xt[b][:], scalar=1.0,
                                                   in1=xt[b][:], op0=ALU.mult, op1=ALU.mult,
                                                   accum_out=st_ss[b][:]))
                    v.wait_ge(sDV.sem, sDV.v)
                    sDV.inc(v.tensor_scalar_mul(out=st_mean[b][:], in0=st_s[b][:], scalar1=1.0 / D))
                    v.wait_ge(sDV.sem, sDV.v)
                    sDV.inc(v.scalar_tensor_tensor(out=st_v2[b][:], in0=st_s[b][:], scalar=st_mean[b][:],
                                                   in1=st_ss[b][:], op0=ALU.mult, op1=ALU.subtract))
                    v.wait_ge(sDV.sem, sDV.v)
                    sDV.inc(v.tensor_scalar(out=st_v2[b][:], in0=st_v2[b][:], scalar1=-1.0 / D,
                                            scalar2=1e-5, op0=ALU.mult, op1=ALU.add))
                    M[f"ln{sub}_v2_{i}"] = sDV.v

                if stop == f"ln{sub}s{i}":
                    raise _StopBuild()
                @blk.scalar
                def _(sc, i=i, b=b):
                    sc.wait_ge(sDV.sem, M[f"ln{sub}_v2_{i}"])
                    sSE.inc(sc.activation(st_sd[b][:], st_v2[b][:], AF.Sqrt))
                    M[f"ln{sub}_sd_{i}"] = sSE.v

                @blk.vector
                def _(v, i=i, b=b):
                    v.wait_ge(sSE.sem, M[f"ln{sub}_sd_{i}"])
                    sDV.inc(v.reciprocal(st_r[b][:], st_sd[b][:]))
                    v.wait_ge(sDV.sem, sDV.v)
                    sDV.inc(v.tensor_scalar(out=st_mr[b][:], in0=st_mean[b][:], scalar1=st_r[b][:],
                                            scalar2=-1.0, op0=ALU.mult, op1=ALU.mult))
                    M[f"ln{sub}_mr_{i}"] = sDV.v

                @blk.vector
                def _(v, i=i, b=b):
                    v.wait_ge(sDV.sem, M[f"ln{sub}_mr_{i}"])
                    sDV.inc(v.tensor_scalar(out=At[b][:], in0=xt[b][:], scalar1=st_r[b][:],
                                            scalar2=st_mr[b][:], op0=ALU.mult, op1=ALU.add))
                    M[f"ln{sub}_ap_{i}"] = sDV.v

                if stop == f"ln{sub}a{i}":
                    raise _StopBuild()
                dv_base_tp = sDV.v

                @blk.tensor
                def _(t, i=i, b=b, dv_base_tp=dv_base_tp):
                    t.wait_ge(sDV.sem, M[f"ln{sub}_ap_{i}"])
                    if sub == 1 and i == 0:
                        t.wait_ge(sLD.sem, M["setup_ld"])
                    for gidx in range(8):
                        dst = PT[:] if gidx % 2 == 0 else PRb
                        if gidx >= 2:
                            t.wait_ge(sDV.sem, fut(f"ln{sub}_tpe_{i}_{gidx-2}",
                                                   dv_base_tp + gidx - 1))
                        elif i >= 1:
                            t.wait_ge(sDV.sem, M[f"ln{sub}_tpe_{i-1}_{6+gidx}"])
                        for bb in range(4):
                            hcx = gidx * 4 + bb
                            last = t.transpose(dst[:, bb * 128:(bb + 1) * 128],
                                               At[b][:, hcx * 128:(hcx + 1) * 128], ident[:])
                        sMM.inc(last)
                        M[f"ln{sub}_tpg_{i}_{gidx}"] = sMM.v
                    M[f"ln{sub}_tp_{i}"] = sMM.v

                @blk.vector
                def _(v, i=i, b=b):
                    for gidx in range(8):
                        v.wait_ge(sMM.sem, M[f"ln{sub}_tpg_{i}_{gidx}"])
                        src = PT[:, 0:512] if gidx % 2 == 0 else PRb[:, 0:512]
                        sDV.inc(v.tensor_copy(out=A_evac_dst(sub, gidx, i),
                                              in_=src.rearrange("p (c t) -> p c t", c=4)))
                        M[f"ln{sub}_tpe_{i}_{gidx}"] = sDV.v
            M[f"ln{sub}_done"] = sDV.v
            M[f"ln{sub}_ap_last"] = M[f"ln{sub}_ap_{n_tiles-1}"]
            M[f"ln{sub}_tp_last"] = M[f"ln{sub}_tp_{n_tiles-1}"]

        def gating(sub, n_tiles, span):
            ngr = (n_tiles * 128) // 512
            for tg in range(ngr):
                @blk.tensor
                def _(t, tg=tg):
                    if tg == 0:
                        t.wait_ge(sDV.sem, M[f"ln{sub}_done"])
                    else:
                        t.wait_ge(sDV.sem, M[f"g{sub}_gev_{tg-1}"])
                    for hcx in range(HC):
                        last = t.matmul(P0[0:4, :], sw_sb[:, hcx * 36:hcx * 36 + 4],
                                        span(hcx, tg), start=(hcx == 0), stop=(hcx == HC - 1))
                    sMM.inc(last)
                    M[f"g{sub}_g_{tg}"] = sMM.v
                    for hcx in range(HC):
                        last = t.matmul(P1[0:32, :], sw_sb[:, hcx * 36 + 4:hcx * 36 + 36],
                                        span(hcx, tg), start=(hcx == 0), stop=(hcx == HC - 1))
                    sMM.inc(last)
                    M[f"g{sub}_z_{tg}"] = sMM.v

                @blk.vector
                def _(v, tg=tg):
                    v.wait_ge(sMM.sem, M[f"g{sub}_g_{tg}"])
                    sDV.inc(v.tensor_copy(out=gfm_sb[0:4, tg * 512:(tg + 1) * 512], in_=P0[0:4, :]))
                    v.wait_ge(sMM.sem, M[f"g{sub}_z_{tg}"])
                    sDV.inc(v.tensor_copy(out=zT_sb[0:32, tg * 512:(tg + 1) * 512], in_=P1[0:32, :]))
                    M[f"g{sub}_gev_{tg}"] = sDV.v
            M[f"g{sub}_ev"] = sDV.v
            if stop == f"g{sub}mm":
                raise _StopBuild()

            @blk.tensor
            def _(t):
                t.wait_ge(sDV.sem, M[f"g{sub}_ev"])
                for i in range(n_tiles):
                    last = t.transpose(PT[:, 384 + i * 4:384 + (i + 1) * 4],
                                       gfm_sb[0:4, i * 128:(i + 1) * 128], ident[0:4, 0:4])
                sMM.inc(last)
                M[f"g{sub}_lgT"] = sMM.v
                for i in range(n_tiles):
                    last = t.transpose(PT[:, i * 32:(i + 1) * 32],
                                       zT_sb[0:32, i * 128:(i + 1) * 128], ident[0:32, 0:32])
                sMM.inc(last)
                M[f"g{sub}_zT"] = sMM.v

            if stop == f"g{sub}tp":
                raise _StopBuild()

            @blk.vector
            def _(v):
                v.wait_ge(sMM.sem, M[f"g{sub}_zT"])
                sDV.inc(v.tensor_copy(out=lg_sb[:, 0:n_tiles * 4], in_=PT[:, 384:384 + n_tiles * 4]))
                sDV.inc(v.tensor_copy(out=zm_tm[:, 0:n_tiles * 32], in_=PT[:, 0:n_tiles * 32]))
                M[f"g{sub}_ptev"] = sDV.v
                v.wait_ge(sDV.sem, sDV.v)
                sDV.inc(v.tensor_add(out=lg_sb[:, 0:n_tiles * 4], in0=lg_sb[:, 0:n_tiles * 4],
                                     in1=ebias_sb[:, 0:n_tiles * 4]))
                v.wait_ge(sDV.sem, sDV.v)
                if stop == f"g{sub}evac":
                    return

                def lcol(e):
                    return lg_sb[:].rearrange("p (t e) -> p t e", e=4)[:, 0:n_tiles, e:e + 1]

                def ccol(k):
                    return cmp_sb[:].rearrange("p (k t) -> p k t", t=12)[:, k:k + 1, 0:n_tiles]

                pairs = [(0, 1), (0, 2), (0, 3), (1, 2), (1, 3), (2, 3)]
                cidx = {}
                k = 0
                last = None
                for (a, bq) in pairs:
                    last = v.tensor_tensor(out=ccol(k), in0=lcol(a), in1=lcol(bq), op=ALU.is_ge)
                    cidx[(a, bq)] = k; k += 1
                    last = v.tensor_tensor(out=ccol(k), in0=lcol(bq), in1=lcol(a), op=ALU.is_gt)
                    cidx[(bq, a)] = k; k += 1
                sDV.inc(last)
                v.wait_ge(sDV.sem, sDV.v)
                if stop == f"g{sub}cmp":
                    return

                def ocol(e):
                    return oh_sb[:].rearrange("p (t e) -> p t e", e=4)[:, 0:n_tiles, e:e + 1]

                need = {0: [cidx[(0, 1)], cidx[(0, 2)], cidx[(0, 3)]],
                        1: [cidx[(1, 0)], cidx[(1, 2)], cidx[(1, 3)]],
                        2: [cidx[(2, 0)], cidx[(2, 1)], cidx[(2, 3)]],
                        3: [cidx[(3, 0)], cidx[(3, 1)], cidx[(3, 2)]]}
                for e in range(4):
                    a, bq, cq = need[e]
                    last = v.tensor_tensor(out=ocol(e), in0=ccol(a), in1=ccol(bq), op=ALU.mult)
                sDV.inc(last)
                v.wait_ge(sDV.sem, sDV.v)
                for e in range(4):
                    a, bq, cq = need[e]
                    last = v.tensor_tensor(out=ocol(e), in0=ocol(e), in1=ccol(cq), op=ALU.mult)
                sDV.inc(last)
                M[f"g{sub}_oh"] = sDV.v
                v.wait_ge(sDV.sem, sDV.v)
                if stop == f"g{sub}oh":
                    return
                for i in range(n_tiles):
                    for e in range(4):
                        last = v.tensor_scalar(out=zmm_tm[:, i * 32 + e * 8:i * 32 + e * 8 + 8],
                                               in0=zm_tm[:, i * 32 + e * 8:i * 32 + e * 8 + 8],
                                               scalar1=oh_sb[:, i * 4 + e:i * 4 + e + 1],
                                               scalar2=None, op0=ALU.mult)
                sDV.inc(last)
                M[f"g{sub}_zmm"] = sDV.v

            if stop in (f"g{sub}zmm", f"g{sub}evac", f"g{sub}cmp", f"g{sub}oh"):
                raise _StopBuild()

            @blk.tensor
            def _(t):
                t.wait_ge(sDV.sem, M[f"g{sub}_zmm"])
                for i in range(n_tiles):
                    dst = PT[0:32, i * 128:(i + 1) * 128] if i < 8 else PRb[0:32, (i - 8) * 128:(i - 7) * 128]
                    last = t.transpose(dst, zmm_tm[:, i * 32:(i + 1) * 32], ident[:])
                sMM.inc(last)
                M[f"g{sub}_zmmT"] = sMM.v

            @blk.vector
            def _(v):
                v.wait_ge(sMM.sem, M[f"g{sub}_zmmT"])
                n1 = min(n_tiles, 8)
                last = v.tensor_copy(out=zmT_sb[0:32, 0:n1 * 128], in_=PT[0:32, 0:n1 * 128])
                if n_tiles > 8:
                    last = v.tensor_copy(out=zmT_sb[0:32, 8 * 128:n_tiles * 128],
                                         in_=PRb[0:32, 0:(n_tiles - 8) * 128])
                sDV.inc(last)
                M[f"g{sub}_zmT"] = sDV.v

        def ffn_main(sub, n_tiles, in_stream, out_stream, chunk_wait, ablock, store_rec,
                     wregion_free):
            for oc in range(8):
                @blk.gpsimd
                def _(g, oc=oc):
                    if oc >= 2:
                        g.wait_ge(sMM.sem, M[f"f{sub}_mmoc_{oc-2}"])
                    elif wregion_free is not None:
                        for semh, val in wregion_free:
                            g.wait_ge(semh, val)
                    sLD.inc(g.dma_start(out=wbuf[oc % 2][:].rearrange("p (hc o) -> p hc o", hc=HC),
                                        in_=wupT.rearrange("(hc p) o -> p hc o", p=128)[:, :,
                                                                                        oc * 512:(oc + 1) * 512]), 16)
                    M[f"f{sub}_w_{oc}"] = sLD.v

                dv_base = sDV.v
                for tb in range(n_tiles):
                    @blk.sync
                    def _(s, oc=oc, tb=tb):
                        if tb >= 2:
                            s.wait_ge(sDV.sem, fut(f"f{sub}_ev_{oc}_{tb-2}", dv_base + tb - 1))
                        elif oc >= 1:
                            s.wait_ge(sDV.sem, M[f"f{sub}_ev_{oc-1}_{tb}"])
                        w = chunk_wait(tb, oc)
                        if w is not None:
                            s.wait_ge(sST.sem, w)
                        sX.inc(s.dma_start(out=xold[tb % 2][:],
                                           in_=in_stream[tb * 128:(tb + 1) * 128,
                                                         oc * 512:(oc + 1) * 512]), 16)
                        M[f"f{sub}_xo_{oc}_{tb}"] = sX.v

                    @blk.tensor
                    def _(t, oc=oc, tb=tb):
                        if tb == 0:
                            t.wait_ge(sLD.sem, M[f"f{sub}_w_{oc}"])
                            if oc == 0:
                                t.wait_ge(sDV.sem, M[f"g{sub}_zmT"])
                        ps = PF[tb % 2]
                        if tb >= 2:
                            t.wait_ge(sDV.sem, fut(f"f{sub}_ev_{oc}_{tb-2}", dv_base + tb - 1))
                        elif oc >= 1:
                            t.wait_ge(sDV.sem, M[f"f{sub}_ev_{oc-1}_{n_tiles-2+tb}"])
                        else:
                            t.wait_ge(sDV.sem, M[f"g{sub}_ev"])
                        for hcx in range(HC):
                            t.matmul(ps[:], ablock(hcx, tb), wbuf[oc % 2][:, hcx * 512:(hcx + 1) * 512],
                                     start=(hcx == 0), stop=False)
                        last = t.matmul(ps[:], zmT_sb[0:32, tb * 128:(tb + 1) * 128],
                                        bflat_sb[0:32, oc * 512:(oc + 1) * 512], start=False, stop=True)
                        sMM.inc(last)
                        M[f"f{sub}_mm_{oc}_{tb}"] = sMM.v

                    @blk.vector
                    def _(v, oc=oc, tb=tb):
                        v.wait_ge(sMM.sem, M[f"f{sub}_mm_{oc}_{tb}"])
                        v.wait_ge(sX.sem, M[f"f{sub}_xo_{oc}_{tb}"])
                        if oc >= 1:
                            v.wait_ge(sST.sem, M[f"f{sub}_st_{oc-1}_{tb}"])
                        ps = PF[tb % 2]
                        sDV.inc(v.scalar_tensor_tensor(out=xnew[tb % 2][:], in0=ps[:], scalar=0.5,
                                                       in1=xold[tb % 2][:], op0=ALU.mult, op1=ALU.add))
                        M[f"f{sub}_ev_{oc}_{tb}"] = sDV.v

                    @blk.sync
                    def _(s, oc=oc, tb=tb):
                        s.wait_ge(sDV.sem, M[f"f{sub}_ev_{oc}_{tb}"])
                        sST.inc(s.dma_start(out=out_stream[tb * 128:(tb + 1) * 128,
                                                           oc * 512:(oc + 1) * 512],
                                            in_=xnew[tb % 2][:]), 16)
                        M[f"f{sub}_st_{oc}_{tb}"] = sST.v
                        if store_rec is not None:
                            store_rec[(tb, oc)] = sST.v
                M[f"f{sub}_mmoc_{oc}"] = sMM.v
            M[f"f{sub}_done_mm"] = sMM.v

        ln_stage(1, NT, x_in, lambda i: None, None)
        if stop == "ln1":
            @blk.sync
            def _(s):
                s.wait_ge(sST.sem, sST.v)
            raise _StopBuild()
        gating(1, NT, A_span)
        if stop == "g1":
            @blk.sync
            def _(s):
                s.wait_ge(sST.sem, sST.v)
            raise _StopBuild()
        ffn_main(1, NT, x_in, x1, lambda tb, oc: None, A_block, x1_store,
                 [(sDV.sem, M["ln1_done"]), (sMM.sem, M["ln1_tp_last"])])

        if stop == "f1":
            @blk.sync
            def _(s):
                s.wait_ge(sST.sem, sST.v)
            raise _StopBuild()
        ln_stage(2, NT, x1, lambda i: x1_store[(i, 7)], M["f1_done_mm"])

        if stop == "ln2":
            @blk.sync
            def _(s):
                s.wait_ge(sST.sem, sST.v)
            raise _StopBuild()
        for kh in range(NKV):
            @blk.gpsimd
            def _(g, kh=kh):
                if kh >= 2:
                    g.wait_ge(sMM.sem, M[f"k_mm_{kh-2}"])
                else:
                    g.wait_ge(sMM.sem, M["f1_done_mm"])
                    g.wait_ge(sDV.sem, M["ln2_done"])
                sLD.inc(g.dma_start(out=kwbuf[kh % 2][:].rearrange("p (hc o) -> p hc o", hc=HC),
                                    in_=wkT.rearrange("(hc p) o -> p hc o", p=128)[:, :,
                                                                                   kh * 128:(kh + 1) * 128]), 16)
                M[f"k_w_{kh}"] = sLD.v

            dv_base_k = sDV.v
            for tg in range(3):
                @blk.tensor
                def _(t, kh=kh, tg=tg):
                    if tg == 0:
                        t.wait_ge(sLD.sem, M[f"k_w_{kh}"])
                        if kh == 0:
                            t.wait_ge(sDV.sem, M["ln2_done"])
                    ps = PF[tg % 2]
                    if tg >= 2:
                        t.wait_ge(sDV.sem, fut(f"k_ev_{kh}_{tg-2}", dv_base_k + 2 * (tg - 2) + 1))
                    elif kh >= 1:
                        t.wait_ge(sDV.sem, M[f"k_ev_{kh-1}_{1+tg}"])
                    for hcx in range(HC):
                        last = t.matmul(ps[:], kwbuf[kh % 2][:, hcx * 128:(hcx + 1) * 128],
                                        A_span(hcx, tg), start=(hcx == 0), stop=(hcx == HC - 1))
                    sMM.inc(last)
                    M[f"k_mm_{kh}_{tg}"] = sMM.v

                @blk.vector
                def _(v, kh=kh, tg=tg):
                    v.wait_ge(sMM.sem, M[f"k_mm_{kh}_{tg}"])
                    ps = PF[tg % 2]
                    sDV.inc(v.tensor_scalar_mul(out=K_slice(kh, tg), in0=ps[:], scalar1=QSCALE))
                    M[f"k_ev_{kh}_{tg}"] = sDV.v
                    key = f"kn_mm_{kh}_{tg-2}" if tg >= 2 else (f"kn_mm_{kh-1}_{1+tg}" if kh >= 1 else None)
                    if key and key in M:
                        v.wait_ge(sMM.sem, M[key])
                    v.wait_ge(sDV.sem, sDV.v)
                    sDV.inc(v.scalar_tensor_tensor(out=ksq_sb[tg % 2][:], in0=K_slice(kh, tg), scalar=1.0,
                                                   in1=K_slice(kh, tg), op0=ALU.mult, op1=ALU.mult))
                    M[f"k_sq_{kh}_{tg}"] = sDV.v

                @blk.tensor
                def _(t, kh=kh, tg=tg):
                    t.wait_ge(sDV.sem, M[f"k_sq_{kh}_{tg}"])
                    key = f"kn_red_{kh}_{tg-1}" if tg >= 1 else (f"kn_red_{kh-1}_2" if kh >= 1 else None)
                    if key and key in M:
                        t.wait_ge(sDV.sem, M[key])
                    last = t.matmul(PQ[0:1, 0:512], ones_col[:], ksq_sb[tg % 2][:], start=True, stop=True)
                    sMM.inc(last)
                    M[f"kn_mm_{kh}_{tg}"] = sMM.v

                @blk.vector
                def _(v, kh=kh, tg=tg):
                    if kh == 0 and tg == 0:
                        sDV.inc(v.memset(kmax2_sb[:], 0.0))
                        v.wait_ge(sDV.sem, sDV.v)
                    v.wait_ge(sMM.sem, M[f"kn_mm_{kh}_{tg}"])
                    sDV.inc(v.reduce_max(out=tmax_sb[0:1, 0:1], in_=PQ[0:1, 0:512], axis=AX.X))
                    v.wait_ge(sDV.sem, sDV.v)
                    sDV.inc(v.tensor_tensor(out=kmax2_sb[0:1, kh:kh + 1], in0=kmax2_sb[0:1, kh:kh + 1],
                                            in1=tmax_sb[0:1, 0:1], op=ALU.max))
                    M[f"kn_red_{kh}_{tg}"] = sDV.v
            M[f"k_mm_{kh}"] = M[f"k_mm_{kh}_2"]
        M["k_done"] = sMM.v
        M["kn_done"] = sDV.v

        if stop == "k":
            @blk.sync
            def _(s):
                s.wait_ge(sST.sem, sST.v)
            raise _StopBuild()
        for och in range(8):
            @blk.gpsimd
            def _(g, och=och):
                if och >= 2:
                    g.wait_ge(sMM.sem, M[f"v_mmoc_{och-2}"])
                else:
                    g.wait_ge(sMM.sem, M["k_done"])
                sLD.inc(g.dma_start(out=vwbuf[och % 2][:].rearrange("p (hc o) -> p hc o", hc=HC),
                                    in_=wvT.rearrange("(hc p) o -> p hc o", p=128)[:, :,
                                                                                   och * 128:(och + 1) * 128]), 16)
                M[f"v_w_{och}"] = sLD.v

            dv_base_v = sDV.v
            for tb in range(NT):
                @blk.tensor
                def _(t, och=och, tb=tb):
                    if tb == 0:
                        t.wait_ge(sLD.sem, M[f"v_w_{och}"])
                    ps = PF[tb % 2]
                    if tb >= 2:
                        t.wait_ge(sDV.sem, fut(f"v_ev_{och}_{tb-2}", dv_base_v + tb - 1))
                    elif och >= 1:
                        t.wait_ge(sDV.sem, M[f"v_ev_{och-1}_{NT-2+tb}"])
                    else:
                        t.wait_ge(sDV.sem, M[f"k_ev_{NKV-1}_{1+tb}"])
                    for hcx in range(HC):
                        last = t.matmul(ps[:, 0:128], A_block(hcx, tb),
                                        vwbuf[och % 2][:, hcx * 128:(hcx + 1) * 128],
                                        start=(hcx == 0), stop=(hcx == HC - 1))
                    sMM.inc(last)
                    M[f"v_mm_{och}_{tb}"] = sMM.v

                @blk.vector
                def _(v, och=och, tb=tb):
                    v.wait_ge(sMM.sem, M[f"v_mm_{och}_{tb}"])
                    ps = PF[tb % 2]
                    sDV.inc(v.tensor_copy(out=V_sb[:, tb * DVA + och * 128: tb * DVA + (och + 1) * 128],
                                          in_=ps[:, 0:128]))
                    M[f"v_ev_{och}_{tb}"] = sDV.v
            M[f"v_mmoc_{och}"] = sMM.v
        M["v_done"] = sMM.v
        M["v_done_dv"] = sDV.v

        if stop == "v":
            @blk.sync
            def _(s):
                s.wait_ge(sST.sem, sST.v)
            raise _StopBuild()
        def q_norm_tail(dblk):
            khq = dblk // 4
            for half in range(2):
                @blk.tensor
                def _(t, dblk=dblk, half=half):
                    t.wait_ge(sDV.sem, M[f"q_sq_{dblk}_{half}"])
                    if dblk >= 1:
                        t.wait_ge(sDV.sem, M[f"qn_ev_{dblk-1}_{half}"])
                    last = t.matmul(PSs[half].ap()[0:1, :], ones_col[:],
                                    qsq4[(dblk % 2) * 2 + half][:], start=True, stop=True)
                    sMM.inc(last)
                    M[f"qn_mm_{dblk}_{half}"] = sMM.v

                @blk.vector
                def _(v, dblk=dblk, half=half, khq=khq):
                    v.wait_ge(sMM.sem, M[f"qn_mm_{dblk}_{half}"])
                    if dblk >= 3 and half == 0:
                        v.wait_ge(sST.sem, M[f"q_st_{dblk-2}"])
                    sDV.inc(v.tensor_scalar(out=qmrow[dblk % 2][0:1, half * 512:(half + 1) * 512],
                                            in0=PSs[half].ap()[0:1, :], scalar1=kmax2_sb[0:1, khq:khq + 1],
                                            scalar2=None, op0=ALU.add))
                    M[f"qn_ev_{dblk}_{half}"] = sDV.v

            @blk.sync
            def _(s, dblk=dblk):
                s.wait_ge(sDV.sem, M[f"q_sq_{dblk}_1"])
                sST.inc(s.dma_start(out=qscr[dblk * 128:(dblk + 1) * 128, :], in_=qstage[dblk % 2][:]), 16)
                s.wait_ge(sDV.sem, M[f"qn_ev_{dblk}_1"])
                gq = dblk % 4
                khx = dblk // 4
                dst = qmd.rearrange("a (qb kh g t) -> a qb kh g t", qb=8, kh=8, g=4)[:, :, khx, gq, :]
                sST.inc(s.dma_start(out=dst, in_=qmrow[dblk % 2][:].rearrange("a (qb t) -> a qb t", qb=8)), 16)
                M[f"q_st_{dblk}"] = sST.v

        for dblk in range(NQH):
            khq = dblk // 4

            @blk.gpsimd
            def _(g, dblk=dblk):
                if dblk >= 2:
                    g.wait_ge(sMM.sem, M[f"q_mm_{dblk-2}_1"])
                else:
                    g.wait_ge(sMM.sem, M["v_done"])
                sLD.inc(g.dma_start(out=qwbuf[dblk % 2][:].rearrange("p (hc o) -> p hc o", hc=HC),
                                    in_=wqT.rearrange("(hc p) o -> p hc o", p=128)[:, :,
                                                                                   dblk * 128:(dblk + 1) * 128]), 16)
                M[f"q_w_{dblk}"] = sLD.v

            for half in range(2):
                @blk.tensor
                def _(t, dblk=dblk, half=half):
                    if half == 0:
                        t.wait_ge(sLD.sem, M[f"q_w_{dblk}"])
                    if dblk >= 1:
                        t.wait_ge(sDV.sem, M[f"q_ev_{dblk-1}_{half}"])
                    elif half == 0:
                        t.wait_ge(sDV.sem, M["kn_done"])
                    for hcx in range(HC):
                        last = t.matmul(PQ[:, half * 512:(half + 1) * 512],
                                        qwbuf[dblk % 2][:, hcx * 128:(hcx + 1) * 128],
                                        A_own[:, hcx * TOWN + half * 512: hcx * TOWN + (half + 1) * 512],
                                        start=(hcx == 0), stop=(hcx == HC - 1))
                    sMM.inc(last)
                    M[f"q_mm_{dblk}_{half}"] = sMM.v

                @blk.vector
                def _(v, dblk=dblk, half=half):
                    v.wait_ge(sMM.sem, M[f"q_mm_{dblk}_{half}"])
                    if dblk >= 2 and half == 0:
                        v.wait_ge(sST.sem, M[f"q_st_{dblk-2}"])
                    sDV.inc(v.tensor_scalar_mul(out=qstage[dblk % 2][:, half * 512:(half + 1) * 512],
                                                in0=PQ[:, half * 512:(half + 1) * 512], scalar1=QSCALE))
                    M[f"q_ev_{dblk}_{half}"] = sDV.v
                    if dblk >= 2:
                        v.wait_ge(sMM.sem, M[f"qn_mm_{dblk-2}_{half}"])
                    v.wait_ge(sDV.sem, sDV.v)
                    qs = qstage[dblk % 2][:, half * 512:(half + 1) * 512]
                    sDV.inc(v.scalar_tensor_tensor(out=qsq4[(dblk % 2) * 2 + half][:], in0=qs, scalar=1.0,
                                                   in1=qs, op0=ALU.mult, op1=ALU.mult))
                    M[f"q_sq_{dblk}_{half}"] = sDV.v

            if dblk >= 1:
                q_norm_tail(dblk - 1)
        q_norm_tail(NQH - 1)
        M["q_done_mm"] = sMM.v
        M["q_done_st"] = sST.v
        M["q_done_dv"] = sDV.v

        if stop == "q":
            @blk.sync
            def _(s):
                s.wait_ge(sST.sem, sST.v)
            raise _StopBuild()
        @blk.gpsimd
        def _(g):
            g.wait_ge(sMM.sem, M["q_done_mm"])
            sLD.inc(g.dma_start(out=emrow_sb[:], in_=emrep), 16)
            M["em_ld"] = sLD.v

        EMI = {}
        _emi = 0
        for _qb in range(4):
            for _kb in range(4 - _qb):
                EMI[(_qb, _kb)] = _emi
                _emi += 1

        for qb in range(8):
            if stop == f"atq{qb}":
                @blk.sync
                def _(s):
                    s.wait_ge(sST.sem, sST.v)
                raise _StopBuild()

            @blk.sync
            def _(s, qb=qb):
                if qb == 0:
                    s.wait_ge(sST.sem, M["q_done_st"])
                    s.wait_ge(sMM.sem, M["q_done_mm"])
                if qb >= 2:
                    s.wait_ge(sMM.sem, M[f"at_mmq_{qb-2}"])
                src = qscr.rearrange("(h d) t -> d h t", d=128)[:, :, qb * 128:(qb + 1) * 128]
                dst = qtile[qb % 2][:].rearrange("d (h t) -> d h t", h=NQH)
                for part in range(4):
                    sX.inc(s.dma_start(out=dst[:, part * 8:(part + 1) * 8, :],
                                       in_=src[:, part * 8:(part + 1) * 8, :]), 16)
                sX.inc(s.dma_start(out=qmrow2[qb % 2][:], in_=qmd[:, qb * 4096:(qb + 1) * 4096]), 16)
                M[f"at_q_{qb}"] = sX.v

            se_base_qb = sSE.v
            dv_base_qb = sDV.v
            for kh in range(NKV):
                it = qb * NKV + kh
                par = it % 2

                @blk.tensor
                def _(t, qb=qb, kh=kh, it=it, par=par):
                    if kh == 0:
                        t.wait_ge(sX.sem, M[f"at_q_{qb}"])
                    if it == 0:
                        t.wait_ge(sLD.sem, M["em_ld"])
                        t.wait_ge(sDV.sem, M["v_done_dv"])
                    for kb in range(5):
                        sp = PF[kb % 2]
                        if kb >= 2:
                            t.wait_ge(sSE.sem, fut(f"at_exp_{it}_{kb-2}", se_base_qb + kh * 5 + kb - 1))
                        elif it >= 1:
                            prev_kb = 4 if kb == 0 else 3
                            if kh >= 1:
                                t.wait_ge(sSE.sem, fut(f"at_exp_{it-1}_{prev_kb}",
                                                       se_base_qb + (kh - 1) * 5 + prev_kb + 1))
                            else:
                                t.wait_ge(sSE.sem, M[f"at_exp_{it-1}_{prev_kb}"])
                        kcol = kh * TTOT + qb * 128 + kb * 128
                        t.matmul(sp[:], K_sb[:, kcol:kcol + 128],
                                 qtile[qb % 2][:, kh * 512:(kh + 1) * 512], start=True, stop=False)
                        adders = [(halfneg1[:], qmrow2[qb % 2][0:1, kh * 512:(kh + 1) * 512])]
                        if kb == 0:
                            adders.append((ident[:], m0_sb[:]))
                        if kb == 4:
                            adders.append((ident[:], m4_sb[:]))
                        if (qb, kb) in EMI:
                            adders.append((ident[:], emrow_sb[:]))
                        for ai, (lh, rh) in enumerate(adders):
                            last = t.matmul(sp[:], lh, rh, start=False, stop=(ai == len(adders) - 1))
                        sMM.inc(last)
                        M[f"at_s_{it}_{kb}"] = sMM.v
                    if kh == NKV - 1:
                        M[f"at_mmq_{qb}"] = sMM.v

                @blk.tensor
                def _(t, qb=qb, kh=kh, it=it, par=par):
                    for kb in range(5):
                        t.wait_ge(sSE.sem, fut(f"at_exp_{it}_{kb}", se_base_qb + kh * 5 + kb + 1))
                        if kb == 0:
                            if it >= 2:
                                if kh >= 2:
                                    t.wait_ge(sDV.sem, fut(f"at_otev_{it-2}", dv_base_qb + (kh - 2) * 2 + 2))
                                else:
                                    t.wait_ge(sDV.sem, M[f"at_otev_{it-2}"])
                            else:
                                t.wait_ge(sDV.sem, M["q_done_dv"])
                        vcol = (qb + kb) * DVA + kh * 128
                        t.matmul(POT[par], V_sb[:, vcol:vcol + 128],
                                 E_sb[:, kb * 512:(kb + 1) * 512], start=(kb == 0), stop=(kb == 4))
                        last = t.matmul(PSs[par].ap(), ones128[:],
                                        E_sb[:, kb * 512:(kb + 1) * 512], start=(kb == 0), stop=(kb == 4))
                    sMM.inc(last)
                    M[f"at_pv_{it}"] = sMM.v

            @blk.scalar
            def _(sc, qb=qb):
                for kh in range(NKV):
                    it = qb * NKV + kh
                    for kb in range(5):
                        sc.wait_ge(sMM.sem, M[f"at_s_{it}_{kb}"])
                        if kb == 0 and it >= 1:
                            sc.wait_ge(sMM.sem, M[f"at_pv_{it-1}"])
                        sp = PF[kb % 2]
                        sSE.inc(sc.activation(E_sb[:, kb * 512:(kb + 1) * 512], sp[:], AF.Exp))
                        M[f"at_exp_{it}_{kb}"] = sSE.v

            for kh in range(NKV):
                it = qb * NKV + kh
                par = it % 2

                @blk.vector
                def _(v, qb=qb, kh=kh, it=it, par=par):
                    v.wait_ge(sMM.sem, M[f"at_pv_{it}"])
                    sDV.inc(v.reciprocal(rs_pp2[par][:], PSs[par].ap()))
                    v.wait_ge(sDV.sem, sDV.v)
                    dst = OT_sb[:].rearrange("p (h t) -> p h t", h=NQH)[:, kh * 4:kh * 4 + 4,
                                                                       qb * 128:(qb + 1) * 128]
                    sDV.inc(v.tensor_tensor(out=dst, in0=POT[par].rearrange("p (g t) -> p g t", g=4),
                                            in1=rs_pp2[par][:].rearrange("p (g t) -> p g t", g=4),
                                            op=ALU.mult))
                    M[f"at_otev_{it}"] = sDV.v
        M["attn_done_dv"] = sDV.v
        M["attn_done_mm"] = sMM.v

        if stop == "at":
            @blk.sync
            def _(s):
                s.wait_ge(sST.sem, sST.v)
            raise _StopBuild()
        for oc in range(8):
            @blk.gpsimd
            def _(g, oc=oc):
                if oc >= 2:
                    g.wait_ge(sMM.sem, M[f"op_mmoc_{oc-2}"])
                else:
                    g.wait_ge(sMM.sem, M["attn_done_mm"])
                    g.wait_ge(sSE.sem, M[f"at_exp_{63}_4"])
                sLD.inc(g.dma_start(out=wbuf[oc % 2][:].rearrange("p (hc o) -> p hc o", hc=HC),
                                    in_=woT.rearrange("(hc p) o -> p hc o", p=128)[:, :,
                                                                                   oc * 512:(oc + 1) * 512]), 16)
                M[f"op_w_{oc}"] = sLD.v

            dv_base = sDV.v
            for tb in range(NTO):
                @blk.sync
                def _(s, oc=oc, tb=tb):
                    if tb >= 2:
                        s.wait_ge(sDV.sem, fut(f"op_ev_{oc}_{tb-2}", dv_base + tb - 1))
                    elif oc >= 1:
                        s.wait_ge(sDV.sem, M[f"op_ev_{oc-1}_{tb}"])
                    sX.inc(s.dma_start(out=xold[tb % 2][:],
                                       in_=x1[THALO + tb * 128:THALO + (tb + 1) * 128,
                                              oc * 512:(oc + 1) * 512]), 16)
                    M[f"op_xo_{oc}_{tb}"] = sX.v

                @blk.tensor
                def _(t, oc=oc, tb=tb):
                    if tb == 0:
                        t.wait_ge(sLD.sem, M[f"op_w_{oc}"])
                        if oc == 0:
                            t.wait_ge(sDV.sem, M["attn_done_dv"])
                    ps = PF[tb % 2]
                    if tb >= 2:
                        t.wait_ge(sDV.sem, fut(f"op_ev_{oc}_{tb-2}", dv_base + tb - 1))
                    elif oc >= 1:
                        t.wait_ge(sDV.sem, M[f"op_ev_{oc-1}_{NTO-2+tb}"])
                    for dc in range(HC):
                        last = t.matmul(ps[:], OT_sb[:, dc * TOWN + tb * 128: dc * TOWN + (tb + 1) * 128],
                                        wbuf[oc % 2][:, dc * 512:(dc + 1) * 512],
                                        start=(dc == 0), stop=(dc == HC - 1))
                    sMM.inc(last)
                    M[f"op_mm_{oc}_{tb}"] = sMM.v

                @blk.vector
                def _(v, oc=oc, tb=tb):
                    v.wait_ge(sMM.sem, M[f"op_mm_{oc}_{tb}"])
                    v.wait_ge(sX.sem, M[f"op_xo_{oc}_{tb}"])
                    if oc >= 1:
                        v.wait_ge(sST.sem, M[f"op_st_{oc-1}_{tb}"])
                    ps = PF[tb % 2]
                    sDV.inc(v.tensor_add(out=xnew[tb % 2][:], in0=ps[:], in1=xold[tb % 2][:]))
                    M[f"op_ev_{oc}_{tb}"] = sDV.v

                @blk.sync
                def _(s, oc=oc, tb=tb):
                    s.wait_ge(sDV.sem, M[f"op_ev_{oc}_{tb}"])
                    sST.inc(s.dma_start(out=x2[tb * 128:(tb + 1) * 128, oc * 512:(oc + 1) * 512],
                                        in_=xnew[tb % 2][:]), 16)
                    M[f"op_st_{oc}_{tb}"] = sST.v
                    x2_store[(tb, oc)] = sST.v
            M[f"op_mmoc_{oc}"] = sMM.v
        M["op_done_mm"] = sMM.v

        if stop == "op":
            @blk.sync
            def _(s):
                s.wait_ge(sST.sem, sST.v)
            raise _StopBuild()
        ln_stage(3, NTO, x2, lambda i: x2_store[(i, 7)], M["op_done_mm"])
        gating(3, NTO, A3_span)
        ffn_main(3, NTO, x2, out_t, lambda tb, oc: x2_store[(tb, oc)],
                 A3_block, None,
                 [(sMM.sem, M["ln3_tp_last"]),
                  (sDV.sem, M["ln3_done"])])

        @blk.sync
        def _(s):
            s.wait_ge(sST.sem, sST.v)

    except _StopBuild:
        pass
    for key, val in futures:
        if key in M:
            assert M[key] == val, (key, M.get(key), val)

    ctx.close()
    return nc


def make_inputs_per_core(inp):
    x = np.asarray(inp["L0_attn_x"])[0]
    wq = np.asarray(inp["L0_attn_wq"])
    wk = np.asarray(inp["L0_attn_wk"])
    wv = np.asarray(inp["L0_attn_wv"])
    wo = np.asarray(inp["L0_attn_wo"])
    wup = np.asarray(inp["L0_ffn_W_up"])
    gate = np.asarray(inp["L0_ffn_gate_w"])
    eb = np.asarray(inp["L0_ffn_expert_bias"])
    lA = np.asarray(inp["L0_ffn_lora_A"])
    lB = np.asarray(inp["L0_ffn_lora_B"])

    f32 = np.float32
    bf16 = ml_dtypes.bfloat16
    wqT = np.ascontiguousarray(wq.T.astype(f32))
    wkT = np.ascontiguousarray(wk.T.astype(f32))
    wvT = np.ascontiguousarray(wv.T.astype(f32))
    woT = np.ascontiguousarray(wo.T.astype(f32))
    wupT = np.ascontiguousarray(wup.T.astype(f32))
    A_flat = lA.reshape(32, D)
    sw = np.concatenate([gate.T.astype(f32), A_flat.T.astype(f32)], axis=1)
    swr = np.ascontiguousarray(sw.reshape(HC, 128, 36).transpose(1, 0, 2).reshape(128, HC * 36))
    bflat_h = np.ascontiguousarray(lB.transpose(0, 2, 1).reshape(32, D).astype(f32))
    ebias_h = np.ascontiguousarray(np.tile(eb.astype(f32)[None, :], (128, 12)))

    ko = np.arange(128)[:, None]
    qi = np.arange(128)[None, :]
    m0 = np.where(ko > qi, 0.0, -1e30).astype(f32)
    m4 = np.where(ko <= qi, 0.0, -1e30).astype(f32)
    m0rep_h = np.ascontiguousarray(np.tile(m0, (1, 4))).astype(bf16)
    m4rep_h = np.ascontiguousarray(np.tile(m4, (1, 4))).astype(bf16)
    emfull = np.full((128, 512), np.float32(-1e30)).astype(bf16)
    emzero = np.zeros((128, 512), dtype=bf16)
    ident_h = np.eye(128, dtype=bf16)

    in_maps = []
    for c in range(8):
        if c == 0:
            x_c = np.concatenate([x[0:512], x[0:1024]], axis=0)
            em = emfull
        else:
            s0 = 1024 * c - 512
            x_c = x[s0:s0 + TTOT]
            em = emzero
        in_maps.append(dict(
            x_in=np.ascontiguousarray(x_c.astype(f32)),
            wqT=wqT, wkT=wkT, wvT=wvT, woT=woT, wupT=wupT,
            swr=swr, bflat=bflat_h, ebias=ebias_h,
            m0rep=m0rep_h, m4rep=m4rep_h, emrep=em,
            identin=ident_h,
        ))
    return in_maps


def kernel(**inputs):
    in_maps = make_inputs_per_core(inputs)
    nc = build()
    res = run_bass_kernel_spmd(nc, in_maps, core_ids=list(range(8)))
    outs = [np.asarray(r["out_t"]) for r in res.results]
    return np.concatenate(outs, axis=0).reshape(1, 8192, D)
